# version 04
# baseline (speedup 1.0000x reference)
import numpy as np
import ml_dtypes
from contextlib import ExitStack

import concourse.bass as bass
import concourse.mybir as mybir
from concourse.bass_utils import run_bass_kernel_spmd

F32 = mybir.dt.float32
BF16 = mybir.dt.bfloat16
AF = mybir.ActivationFunctionType
ALU = mybir.AluOpType
AX = mybir.AxisListType.X
NPBF = ml_dtypes.bfloat16

N, D, H, HD = 8192, 2048, 16, 128
NCORE, NSLOT = 8, 8
BS, NB, SEL, WIN = 64, 128, 8, 64
INTER = 5632
NIT = INTER // 128
NDT = D // 128
CH = 512
TOK = NSLOT * 128

ENGS = ("tensor", "vector", "scalar", "sync", "gpsimd")


def nchunks(k):
    return 2 * (k + 1)


def nswa(k):
    return min(3, nchunks(k))


class _Sem:
    def __init__(self, h):
        self.h = h
        self.n = 0


class _Res:
    __slots__ = ("w", "readers")

    def __init__(self):
        self.w = None
        self.readers = []


class _Op:
    __slots__ = ("fn", "waits", "incs")

    def __init__(self, fn, waits, incs):
        self.fn, self.waits, self.incs = fn, waits, incs


class Prog:

    def __init__(self, nc, ctx):
        self.nc = nc
        self.ops = {e: [] for e in ENGS}
        self.sem = {e: _Sem(ctx.enter_context(nc.semaphore(f"s_{e}"))) for e in ENGS}

    def emit(self, eng, fn, reads=(), writes=(), extra_waits=(), step=1):
        s = self.sem[eng]
        want = {}

        def add(c):
            if c is None:
                return
            sm, v = c
            k = id(sm)
            if k not in want or want[k][1] < v:
                want[k] = (sm, v)

        for r in reads:
            add(r.w)
        for w in writes:
            add(w.w)
            for c in w.readers:
                add(c)
        for ew in extra_waits:
            add(ew)
        wl = []
        for sm, v in want.values():
            if sm is s and eng == "tensor":
                continue
            wl.append((sm, v))
        if eng in ("vector", "scalar") and s.n > 0:
            wl.append((s, s.n))
        best = {}
        for sm, v in wl:
            if id(sm) not in best or best[id(sm)][1] < v:
                best[id(sm)] = (sm, v)
        s.n += step
        self.ops[eng].append(_Op(fn, list(best.values()), [(s, step)]))
        comp = (s, s.n)
        for w in writes:
            w.w = comp
            w.readers = []
        for r in reads:
            r.readers.append(comp)
        return comp

    def wait_all(self, eng):
        s = self.sem[eng]
        self.ops[eng].append(_Op(None, [(s, s.n)], []))

    def run(self, block):
        def mk(name):
            def body(engine):
                for op in self.ops[name]:
                    for sm, v in op.waits:
                        engine.wait_ge(sm.h, v)
                    if op.fn is not None:
                        ins = op.fn(engine)
                        for sm, k in op.incs:
                            ins.then_inc(sm.h, k)
            return body

        block.tensor(mk("tensor"))
        block.vector(mk("vector"))
        block.scalar(mk("scalar"))
        block.sync(mk("sync"))
        block.gpsimd(mk("gpsimd"))


def _rope_tables():
    half = HD // 2
    inv = 1.0 / 10000.0 ** (np.arange(half, dtype=np.float64) / half)
    th = np.arange(N, dtype=np.float64)[:, None] * inv[None, :]
    return np.cos(th).T.astype(np.float32), np.sin(th).T.astype(np.float32)


def _perm_blocks(a):
    v = a.reshape(a.shape[:-1] + (8, 8, 2))
    v = np.swapaxes(v, -3, -2)
    return np.ascontiguousarray(v.reshape(a.shape))


def _core_inputs(c, full, cos, sin):
    x = np.asarray(full["L0_attn_x"], np.float32)[0]
    tiles = [8 * k + c for k in range(NSLOT)]
    x_loc = np.stack([x[t * 128:(t + 1) * 128] for t in tiles])
    xT_loc = np.ascontiguousarray(
        np.concatenate([x[t * 128:(t + 1) * 128] for t in tiles], 0).T)
    scale = np.float32(HD ** -0.5)
    cols = np.concatenate([np.arange(t * 128, (t + 1) * 128) for t in tiles])
    rope = np.concatenate(
        [cos[:, cols] * scale, sin[:, cols] * scale, cos[:, cols], sin[:, cols]],
        axis=1).astype(NPBF)
    rope = np.vstack([rope, rope])

    bmask = np.zeros((NSLOT, 128, 1024), np.float32)
    swam = np.zeros((NSLOT, 128, 1536), np.float32)
    vis = np.zeros((NSLOT, 128, NB), np.float32)
    cur2 = np.zeros((NSLOT, 128, NB), np.float32)
    notany = np.zeros((128, NSLOT), np.float32)
    any01 = np.zeros((128, NSLOT), np.float32)
    r = np.arange(128)
    mm = np.arange(NB)
    for k in range(NSLOT):
        t = 8 * k + c
        T0 = t * 128
        rows = T0 + r
        scan = 1024 * (k + 1)
        j = np.arange(scan - 1024, scan)
        bmask[k] = (j[None, :] <= rows[:, None]) & (j[None, :] < T0 + 128)
        w = 512 * nswa(k)
        j2 = np.arange(scan - w, scan)
        m = (j2[None, :] <= rows[:, None]) & (rows[:, None] - j2[None, :] <= WIN)
        swam[k, :, 1536 - w:] = m
        vis[k] = rows[:, None] >= ((mm + 1) * 64 - 1)[None, :]
        cur2[k, r, rows // 64] = 2.0
        na = ~(vis[k].any(1))
        notany[:, k] = na
        any01[:, k] = ~na
    gs = np.asarray(full["L0_ffn_gate_s"], np.float32)
    us = np.asarray(full["L0_ffn_up_s"], np.float32)
    ds = np.asarray(full["L0_ffn_down_s"], np.float32)
    return {
        "x_loc": x_loc,
        "xT_loc": xT_loc,
        "rope": rope,
        "bmask": bmask.astype(NPBF),
        "swam": swam.astype(NPBF),
        "vis": _perm_blocks(vis).astype(NPBF),
        "cur2": _perm_blocks(cur2).astype(NPBF),
        "notany": notany,
        "any01": any01,
        "Wq": np.asarray(full["L0_attn_Wq"], np.float32),
        "Wk": np.asarray(full["L0_attn_Wk"], np.float32),
        "Wv": np.asarray(full["L0_attn_Wv"], np.float32),
        "Wg": np.asarray(full["L0_attn_Wg"], np.float32),
        "Wo": np.asarray(full["L0_attn_Wo"], np.float32),
        "gwT": np.ascontiguousarray(np.asarray(full["L0_ffn_gate_w"], np.float32).T),
        "uwT": np.ascontiguousarray(np.asarray(full["L0_ffn_up_w"], np.float32).T),
        "dwT": np.ascontiguousarray(np.asarray(full["L0_ffn_down_w"], np.float32).T),
        "gs_pt": np.ascontiguousarray(gs.reshape(NIT, 128).T),
        "us_pt": np.ascontiguousarray(us.reshape(NIT, 128).T),
        "ds_pt": np.ascontiguousarray(ds.reshape(NDT, 128).T),
    }


def build(debug_taps=()):
    nc = bass.Bass("TRN2", target_bir_lowering=False)
    RG = [list(range(NCORE))]

    def din(name, shape, dt=F32):
        return nc.dram_tensor(name, shape, dt, kind="ExternalInput").ap()

    x_loc = din("x_loc", [NSLOT, 128, D])
    xT_loc = din("xT_loc", [D, TOK])
    rope_t = din("rope", [128, 4096], BF16)
    bmask_i = din("bmask", [NSLOT, 128, 1024], BF16)
    swam_i = din("swam", [NSLOT, 128, 1536], BF16)
    vis_i = din("vis", [NSLOT, 128, NB], BF16)
    cur2_i = din("cur2", [NSLOT, 128, NB], BF16)
    notany_i = din("notany", [128, NSLOT])
    any01_i = din("any01", [128, NSLOT])
    Wq = din("Wq", [D, D]); Wk = din("Wk", [D, D]); Wv = din("Wv", [D, D])
    Wg = din("Wg", [D, 3 * H]); Wo = din("Wo", [D, D])
    gwT = din("gwT", [D, INTER]); uwT = din("uwT", [D, INTER])
    dwT = din("dwT", [INTER, D])
    gs_pt = din("gs_pt", [128, NIT]); us_pt = din("us_pt", [128, NIT])
    ds_pt = din("ds_pt", [128, NDT])
    outT = nc.dram_tensor("outT", [D, TOK], F32, kind="ExternalOutput").ap()

    KT_contrib = nc.dram_tensor("KT_contrib", [H * 128, TOK], BF16, kind="Internal").ap()
    V_contrib = nc.dram_tensor("V_contrib", [H * NSLOT * 128, 128], BF16, kind="Internal").ap()
    KC_contrib = nc.dram_tensor("KC_contrib", [16, D], BF16, kind="Internal").ap()
    VC_contrib = nc.dram_tensor("VC_contrib", [16, D], BF16, kind="Internal").ap()
    KT_gath = nc.dram_tensor("KT_gath", [NCORE * H * 128, TOK], BF16, kind="Internal").ap()
    V_gath = nc.dram_tensor("V_gath", [NCORE * H * NSLOT * 128, 128], BF16, kind="Internal").ap()
    KC_gath = nc.dram_tensor("KC_gath", [NCORE * 16, D], BF16, kind="Internal").ap()
    VC_gath = nc.dram_tensor("VC_gath", [NCORE * 16, D], BF16, kind="Internal").ap()
    xnt_spill = nc.dram_tensor("xnt_spill", [128, NDT * TOK], BF16, kind="Internal").ap()
    part_scr = nc.dram_tensor("part_scr", [D, TOK], F32, kind="Internal").ap()

    ident_h = nc.inline_tensor(np.eye(128, dtype=NPBF), "identb").ap()
    taps = {}

    with ExitStack() as ctx:
        en = ctx.enter_context

        def sb(name, shape, dt):
            return en(nc.sbuf_tensor(name, shape, dt))

        A_xnt = sb("A_xnt", [128, NDT * TOK], BF16)
        A_qt = sb("A_qt", [128, H * TOK], BF16)
        A_kv = sb("A_kv", [128, H * TOK], BF16)
        A_w = sb("A_w", [128, 16384], BF16)
        ident = sb("identS", [128, 128], BF16)
        vcmp = sb("vcmpS", [128, H * 128], BF16)
        kcmpT = sb("kcmpTS", [128, H * 128], BF16)
        kc16 = sb("kc16", [128, H * 16], BF16)
        kcstage = sb("kcstage", [16, D], BF16)
        vcstage = sb("vcstage", [2, D], BF16)
        ropes = sb("ropes", [128, 4096], BF16)
        rtmp = sb("rtmp", [128, CH], BF16)
        xnnat = sb("xnnat", [128, D], BF16)
        ecmp = sb("ecmpS", [128, 128], BF16)
        evis = sb("evisS", [128, 128], BF16)
        ecT = sb("ecTS", [128, 128], BF16)
        impS = sb("impS", [128, 128], BF16)
        mselS = sb("mselS", [128, 128], BF16)
        ogS = sb("ogS", [128, 128], BF16)
        visS = sb("visS", [128, NSLOT * NB], BF16)
        cur2S = sb("cur2S", [128, NSLOT * NB], BF16)
        sWg = sb("sWgS", [128, NDT * 48], BF16)
        sg_t = sb("sg_t", [128, CH], BF16)
        xstage = sb("xstage", [128, D], F32)
        F32A = sb("F32A", [128, D], F32)
        sqt = F32A[:]
        f32scr = sb("f32scr", [128, 1024], F32)
        lnst = sb("lnst", [128, 8], F32)
        gcol = sb("gcolS", [128, NSLOT * 48], F32)
        Zparts = sb("ZpartsS", [128, 32], F32)
        ZswaS = sb("ZswaS", [128, 4], F32)
        Zm = sb("ZmS", [128, 8], F32)
        top8 = sb("top8S", [128, 8], F32)
        gsS = sb("gsS", [128, NIT], F32)
        usS = sb("usS", [128, NIT], F32)
        dsS = sb("dsS", [128, NDT], F32)
        naS = sb("naS", [128, NSLOT], F32)
        anS = sb("anS", [128, NSLOT], F32)
        pstage = F32A[:, 0:2 * CH]
        ostage = F32A[:, 2 * CH:4 * CH]
        pb = [en(nc.psum_tensor(f"pb{i}", [128, CH], F32)) for i in range(8)]

        xnT = A_xnt
        KT_loc = A_qt
        V_loc = A_kv
        QT = A_qt
        KTh = A_kv[:, 0:8192]
        Vh = A_kv[:, 8192:16384]
        bmaskS = lambda k: A_xnt[:, k * 1024:(k + 1) * 1024]
        Pring = lambda i: A_xnt[:, 8192 + i * 512: 8192 + (i + 1) * 512]
        PmS = lambda i: A_xnt[:, 11264 + i * 512: 11264 + (i + 1) * 512]
        PswS = lambda i: A_xnt[:, 12288 + i * 512: 12288 + (i + 1) * 512]
        PTsS = lambda i: A_xnt[:, 13312 + i * 512: 13312 + (i + 1) * 512]
        swamS = lambda k: A_w[:, k * 1536:(k + 1) * 1536]
        sWoS = lambda p: A_w[:, 12288 + p * 2048: 12288 + (p + 1) * 2048]
        sGW = lambda p: A_w[:, p * 2048:(p + 1) * 2048]
        sUW = lambda p: A_w[:, 4096 + p * 2048: 4096 + (p + 1) * 2048]
        sDW2 = lambda q: A_w[:, 8192 + q * 2816: 8192 + (q + 1) * 2816]
        sKW = lambda p: A_w[:, p * 2048:(p + 1) * 2048]
        sVW = lambda q: A_w[:, q * 8192:(q + 1) * 8192]

        def hdnT(it):
            if it < 32:
                return A_qt[:, it * 512:(it + 1) * 512]
            return A_kv[:, (it - 32) * 512:(it - 31) * 512]

        kcnat = A_kv[:, 0:H * 128]

        rb = [_Res() for _ in range(8)]
        rPring = [_Res() for _ in range(6)]
        rPm = [_Res() for _ in range(2)]
        rPsw = [_Res() for _ in range(2)]
        rPTs = [_Res() for _ in range(2)]
        rQT = [[_Res() for _ in range(NSLOT)] for _ in range(H)]
        rKTloc = [_Res() for _ in range(H)]
        rVloc = _Res()
        rKTh, rVh = _Res(), _Res()
        rxnT = _Res()
        rbm, rswm = _Res(), _Res()
        rsWo = [_Res(), _Res()]
        rsGW = [_Res(), _Res()]
        rsUW = [_Res(), _Res()]
        rsDW = [_Res(), _Res()]
        rsKW = [_Res(), _Res()]
        rsVW = [_Res(), _Res()]
        rxstage, rsqt, rlnst, rxnnat = _Res(), _Res(), _Res(), _Res()
        rropes, rrtmp, rident = _Res(), _Res(), _Res()
        rvcmp, rkcnat, rkcmpT, rkc16 = _Res(), _Res(), _Res(), _Res()
        rkcstage, rvcstage = _Res(), _Res()
        recmp, revis, recT, rimp, rmsel, rog = (_Res() for _ in range(6))
        rvisS, rcur2S, rgcol, rsWg, rsgt = (_Res() for _ in range(5))
        rtop8 = _Res()
        rgs, rus, rds, rna, ran = (_Res() for _ in range(5))
        rpst = [_Res(), _Res()]
        rost = [_Res(), _Res()]
        rKTc, rVc, rKCc, rVCc = (_Res() for _ in range(4))
        rKTg, rVg, rKCg, rVCg = (_Res() for _ in range(4))
        rspill, rpart = _Res(), _Res()
        rhdnT = _Res()
        QT_all = [r for hh in rQT for r in hh]

        P = Prog(nc, ctx)

        def emit_rope(dst, co, si, wr, extra_reads=()):
            P.emit("vector", lambda v, dst=dst, co=co: v.tensor_mul(
                dst[0:64, :], rtmp[0:64, :], ropes[0:64, co:co + 512]),
                reads=[rrtmp, rropes] + list(extra_reads), writes=wr)
            P.emit("vector", lambda v, si=si: v.tensor_mul(
                f32scr[0:64, 0:512], rtmp[64:128, :], ropes[64:128, si:si + 512]),
                reads=[rrtmp, rropes])
            P.emit("vector", lambda v, dst=dst: v.tensor_sub(
                dst[0:64, :], dst[0:64, :], f32scr[0:64, 0:512]),
                writes=wr)
            P.emit("vector", lambda v, dst=dst, si=si: v.tensor_mul(
                dst[64:128, :], rtmp[0:64, :], ropes[0:64, si:si + 512]),
                reads=[rrtmp, rropes], writes=wr)
            P.emit("vector", lambda v, co=co: v.tensor_mul(
                f32scr[64:128, 0:512], rtmp[64:128, :], ropes[64:128, co:co + 512]),
                reads=[rrtmp, rropes])
            P.emit("vector", lambda v, dst=dst: v.tensor_add(
                dst[64:128, :], dst[64:128, :], f32scr[64:128, 0:512]),
                writes=wr)

        def tap(name, ap, reads):
            if name not in debug_taps:
                return
            t = nc.dram_tensor("tap_" + name, list(ap.shape),
                               ap.tensor.dtype if hasattr(ap, "tensor") else BF16,
                               kind="ExternalOutput").ap()
            taps[name] = True
            P.emit("gpsimd", lambda g, t=t, ap=ap: g.dma_start(out=t, in_=ap),
                   reads=list(reads), step=16)

        P.emit("sync", lambda s: s.dma_start(out=ident[:], in_=ident_h),
               writes=[rident], step=16)
        P.emit("sync", lambda s: s.dma_start(out=ropes[:], in_=rope_t),
               writes=[rropes], step=16)
        P.emit("sync", lambda s: s.dma_start(out=visS[:].rearrange("p (k b) -> p k b", k=8), in_=vis_i.rearrange("k p b -> p k b")),
               writes=[rvisS], step=16)
        P.emit("sync", lambda s: s.dma_start(out=cur2S[:].rearrange("p (k b) -> p k b", k=8), in_=cur2_i.rearrange("k p b -> p k b")),
               writes=[rcur2S], step=16)
        P.emit("sync", lambda s: s.dma_start(out=naS[:], in_=notany_i), writes=[rna], step=16)
        P.emit("sync", lambda s: s.dma_start(out=anS[:], in_=any01_i), writes=[ran], step=16)
        P.emit("sync", lambda s: s.dma_start(out=gsS[:], in_=gs_pt), writes=[rgs], step=16)
        P.emit("sync", lambda s: s.dma_start(out=usS[:], in_=us_pt), writes=[rus], step=16)
        P.emit("sync", lambda s: s.dma_start(out=dsS[:], in_=ds_pt), writes=[rds], step=16)
        P.emit("gpsimd", lambda g: g.dma_start(
            out=sWg[:].rearrange("p (t e) -> p t e", t=16), in_=Wg.rearrange("(t p) e -> p t e", p=128)),
            writes=[rsWg], step=16)

        inv_d = float(1.0 / D)
        for s in range(NSLOT):
            P.emit("sync", lambda sy, s=s: sy.dma_start(out=xstage[:], in_=x_loc[s]),
                   writes=[rxstage], step=16)
            P.emit("vector", lambda v: v.reduce_sum(lnst[:, 0:1], xstage[:], AX),
                   reads=[rxstage], writes=[rlnst])
            P.emit("vector", lambda v: v.tensor_scalar_mul(lnst[:, 1:2], lnst[:, 0:1], -inv_d),
                   writes=[rlnst])
            P.emit("scalar", lambda sc: sc.activation(sqt[:], xstage[:], AF.Square,
                                                      bias=lnst[:, 1:2]),
                   reads=[rxstage, rlnst], writes=[rsqt])
            P.emit("vector", lambda v: v.reduce_sum(lnst[:, 2:3], sqt[:], AX),
                   reads=[rsqt], writes=[rlnst])
            P.emit("vector", lambda v: v.tensor_scalar(
                lnst[:, 2:3], lnst[:, 2:3], inv_d, 1e-5, ALU.mult, ALU.add),
                writes=[rlnst])
            P.emit("scalar", lambda sc: sc.activation(lnst[:, 3:4], lnst[:, 2:3], AF.Sqrt),
                   reads=[rlnst], writes=[rlnst])
            P.emit("vector", lambda v: v.reciprocal(lnst[:, 4:5], lnst[:, 3:4]),
                   reads=[rlnst], writes=[rlnst])
            P.emit("vector", lambda v: v.tensor_mul(lnst[:, 5:6], lnst[:, 1:2], lnst[:, 4:5]),
                   writes=[rlnst])
            P.emit("vector", lambda v: v.tensor_scalar(
                xnnat[:], xstage[:], lnst[:, 4:5], lnst[:, 5:6], ALU.mult, ALU.add),
                reads=[rxstage, rlnst], writes=[rxnnat])
            for g4 in range(4):
                bank = rb[g4 % 2]
                P.emit("tensor", lambda t, g4=g4: [
                    t.matmul(pb[g4 % 2][:, i * 128:(i + 1) * 128],
                             xnnat[:, (g4 * 4 + i) * 128:(g4 * 4 + i + 1) * 128],
                             ident[:], start=(i == 0), stop=(i == 3))
                    for i in range(4)][-1],
                    reads=[rxnnat, rident], writes=[bank])
                P.emit("scalar", lambda sc, s=s, g4=g4: sc.copy(
                    xnT[:].rearrange("p (t c) -> p t c", t=NDT)[
                        :, g4 * 4:(g4 + 1) * 4, s * 128:(s + 1) * 128],
                    pb[g4 % 2][:].rearrange("p (t c) -> p t c", t=4)),
                    reads=[bank], writes=[rxnT])
        tap("xnT", xnT[:, 0:4096], [rxnT])

        for h in range(H):
            P.emit("gpsimd", lambda g, h=h: g.dma_start(
                out=sKW(h % 2).rearrange("p (t e) -> p t e", t=16),
                in_=Wk[:, h * 128:(h + 1) * 128].rearrange("(t p) e -> p t e", p=128)),
                writes=[rsKW[h % 2]], step=16)
            for tc in range(2):
                bank = rb[(h * 2 + tc) % 2]
                P.emit("tensor", lambda t, h=h, tc=tc: [
                    t.matmul(pb[(h * 2 + tc) % 2][:],
                             sKW(h % 2)[:, dt * 128:(dt + 1) * 128],
                             xnT[:, dt * 1024 + tc * 512: dt * 1024 + (tc + 1) * 512],
                             start=(dt == 0), stop=(dt == NDT - 1))
                    for dt in range(NDT)][-1],
                    reads=[rsKW[h % 2], rxnT], writes=[bank])
                P.emit("scalar", lambda sc, h=h, tc=tc: sc.copy(
                    rtmp[:], pb[(h * 2 + tc) % 2][:]),
                    reads=[bank], writes=[rrtmp])
                co, si = 2048 + tc * 512, 3072 + tc * 512
                dst = KT_loc[:, h * 1024 + tc * 512: h * 1024 + (tc + 1) * 512]
                emit_rope(dst, co, si, [rKTloc[h]])
            src = KT_loc[:, h * 1024:(h + 1) * 1024]
            P.emit("vector", lambda v, src=src: v.tensor_add(
                f32scr[:, 0:512], src[:, 0:512], src[:, 512:1024]),
                reads=[rKTloc[h]])
            for w in (256, 128, 64, 32, 16):
                P.emit("vector", lambda v, w=w: v.tensor_add(
                    f32scr[:, 0:w], f32scr[:, 0:w], f32scr[:, w:2 * w]))
            P.emit("vector", lambda v, h=h: v.tensor_scalar_mul(
                kc16[:, h * 16:(h + 1) * 16], f32scr[:, 0:16], 1.0 / 64),
                writes=[rkc16])
            P.emit("gpsimd", lambda g, h=h: g.dma_start(
                out=KT_contrib[h * 128:(h + 1) * 128, :],
                in_=KT_loc[:, h * 1024:(h + 1) * 1024]),
                reads=[rKTloc[h]], writes=[rKTc], step=16)
        for h in range(H):
            P.emit("tensor", lambda t, h=h: t.matmul(
                pb[2][0:16, 0:128], kc16[:, h * 16:(h + 1) * 16], ident[:],
                start=True, stop=True),
                reads=[rkc16, rident], writes=[rb[2]])
            P.emit("scalar", lambda sc, h=h: sc.copy(
                kcstage[:, h * 128:(h + 1) * 128], pb[2][0:16, 0:128]),
                reads=[rb[2]], writes=[rkcstage])
        P.emit("gpsimd", lambda g: g.dma_start(out=KC_contrib, in_=kcstage[:]),
               reads=[rkcstage], writes=[rKCc], step=16)

        pool_np = np.zeros((128, 2), dtype=NPBF)
        pool_np[:64, 0] = NPBF(1.0 / 64)
        pool_np[64:, 1] = NPBF(1.0 / 64)
        pool_h = nc.inline_tensor(pool_np, "pool64").ap()
        pool64 = sb("pool64S", [128, 2], BF16)
        rpool = _Res()
        P.emit("sync", lambda s: s.dma_start(out=pool64[:], in_=pool_h),
               writes=[rpool], step=16)
        for hc in range(4):
            q = hc % 2
            P.emit("gpsimd", lambda g, hc=hc, q=q: g.dma_start(
                out=sVW(q).rearrange("p (t e) -> p t e", t=16),
                in_=Wv[:, hc * 512:(hc + 1) * 512].rearrange("(t p) e -> p t e", p=128)),
                writes=[rsVW[q]] + ([rsKW[0], rsKW[1]] if hc < 2 else []), step=16)
            for s in range(NSLOT):
                bank = rb[(s + hc) % 2]
                P.emit("tensor", lambda t, s=s, hc=hc, q=q: [
                    t.matmul(pb[(s + hc) % 2][:],
                             xnT[:, dt * 1024 + s * 128: dt * 1024 + (s + 1) * 128],
                             sVW(q)[:, dt * 512:(dt + 1) * 512],
                             start=(dt == 0), stop=(dt == NDT - 1))
                    for dt in range(NDT)][-1],
                    reads=[rsVW[q], rxnT], writes=[bank])
                P.emit("scalar", lambda sc, s=s, hc=hc: sc.copy(
                    V_loc[:, s * 2048 + hc * 512: s * 2048 + (hc + 1) * 512],
                    pb[(s + hc) % 2][:]),
                    reads=[bank], writes=[rVloc])
                P.emit("tensor", lambda t, s=s, hc=hc: t.matmul(
                    pb[3][0:2, :], pool64[:],
                    V_loc[:, s * 2048 + hc * 512: s * 2048 + (hc + 1) * 512],
                    start=True, stop=True),
                    reads=[rpool, rVloc], writes=[rb[3]])
                P.emit("scalar", lambda sc: sc.copy(vcstage[:, 0:512], pb[3][0:2, :]),
                       reads=[rb[3]], writes=[rvcstage])
                P.emit("gpsimd", lambda g, s=s, hc=hc: g.dma_start(
                    out=VC_contrib[s * 2:(s + 1) * 2, hc * 512:(hc + 1) * 512],
                    in_=vcstage[0:2, 0:512]),
                    reads=[rvcstage], writes=[rVCc], step=16)
        for h in range(H):
            P.emit("gpsimd", lambda g, h=h: g.dma_start(
                out=V_contrib[h * 1024:(h + 1) * 1024, :].rearrange(
                    "(s p) d -> p s d", p=128),
                in_=V_loc[:].rearrange("p (s h c) -> p s h c", s=8, h=H)[:, :, h, :]),
                reads=[rVloc], writes=[rVc], step=16)

        for cin, cout, rin, rout in (
            (KC_contrib, KC_gath, rKCc, rKCg),
            (VC_contrib, VC_gath, rVCc, rVCg),
            (KT_contrib, KT_gath, rKTc, rKTg),
            (V_contrib, V_gath, rVc, rVg),
        ):
            P.emit("gpsimd", lambda g, cin=cin, cout=cout: g.collective_compute(
                "AllGather", ALU.bypass, replica_groups=RG,
                ins=[cin.opt()], outs=[cout.opt()]),
                reads=[rin], writes=[rout], step=1)

        for h in range(H):
            P.emit("gpsimd", lambda g, h=h: g.dma_start(
                out=sKW(h % 2).rearrange("p (t e) -> p t e", t=16),
                in_=Wq[:, h * 128:(h + 1) * 128].rearrange("(t p) e -> p t e", p=128)),
                writes=[rsKW[h % 2]] + ([rsVW[0]] if h < 2 else []), step=16)
            for tc in range(2):
                bank = rb[(h * 2 + tc) % 2]
                P.emit("tensor", lambda t, h=h, tc=tc: [
                    t.matmul(pb[(h * 2 + tc) % 2][:],
                             sKW(h % 2)[:, dt * 128:(dt + 1) * 128],
                             xnT[:, dt * 1024 + tc * 512: dt * 1024 + (tc + 1) * 512],
                             start=(dt == 0), stop=(dt == NDT - 1))
                    for dt in range(NDT)][-1],
                    reads=[rsKW[h % 2], rxnT], writes=[bank])
                P.emit("scalar", lambda sc, h=h, tc=tc: sc.copy(
                    rtmp[:], pb[(h * 2 + tc) % 2][:]),
                    reads=[bank], writes=[rrtmp])
                co, si = tc * 512, 1024 + tc * 512
                dst = QT[:, h * 1024 + tc * 512: h * 1024 + (tc + 1) * 512]
                wr = [rQT[h][4 * tc + i] for i in range(4)]
                emit_rope(dst, co, si, wr, extra_reads=[rKTc])
        for s in range(NSLOT):
            P.emit("tensor", lambda t, s=s: [
                t.matmul(pb[6][:, 0:48],
                         xnT[:, dt * 1024 + s * 128: dt * 1024 + (s + 1) * 128],
                         sWg[:, dt * 48:(dt + 1) * 48],
                         start=(dt == 0), stop=(dt == NDT - 1))
                for dt in range(NDT)][-1],
                reads=[rsWg, rxnT], writes=[rb[6]])
            P.emit("scalar", lambda sc, s=s: sc.activation(
                gcol[:, s * 48:(s + 1) * 48], pb[6][:, 0:48], AF.Sigmoid),
                reads=[rb[6]], writes=[rgcol])
        tap("QT", QT[:, 0:4096], [rQT[0][k] for k in range(8)] + [rQT[1][k] for k in range(8)])

        P.emit("gpsimd", lambda g: g.dma_start(out=xnt_spill, in_=xnT[:]),
               reads=[rxnT], writes=[rspill], step=16)
        P.emit("sync", lambda s: s.dma_start(
            out=A_xnt[:, 0:8192].rearrange("p (k b) -> p k b", k=8), in_=bmask_i.rearrange("k p b -> p k b")),
            writes=[rbm, rxnT], step=16)
        P.emit("sync", lambda s: s.dma_start(
            out=A_w[:, 0:12288].rearrange("p (k b) -> p k b", k=8), in_=swam_i.rearrange("k p b -> p k b")),
            writes=[rswm, rsKW[0], rsKW[1], rsVW[0], rsVW[1]], step=16)
        P.emit("sync", lambda s: s.dma_start(out=vcmp[:], in_=VC_gath),
               reads=[rVCg], writes=[rvcmp], step=16)
        P.emit("sync", lambda s: s.dma_start(out=kcnat[:], in_=KC_gath),
               reads=[rKCg], writes=[rkcnat], step=16)
        for h in range(H):
            P.emit("tensor", lambda t, h=h: t.matmul(
                pb[7][:, 0:128], kcnat[:, h * 128:(h + 1) * 128], ident[:],
                start=True, stop=True),
                reads=[rkcnat, rident], writes=[rb[7]])
            P.emit("scalar", lambda sc, h=h: sc.copy(
                kcmpT[:, h * 128:(h + 1) * 128], pb[7][:, 0:128]),
                reads=[rb[7]], writes=[rkcmpT])

        def mselexp(jc, jj):
            mv = mselS[:].rearrange("p (r sj) -> p r sj", r=8)
            mv = mv[:, 4 * (jc % 2):4 * (jc % 2) + 4, 2 * (jc // 2) + jj]
            return mv.to_broadcast([128, 4, 64])

        def pview(ap, jj):
            return ap.rearrange("p (r j w) -> p r j w", r=4, j=2)[:, :, jj, :]
        for h in range(H):
            kvw = [rVloc, rkcnat] if h == 0 else []
            for r in range(NCORE):
                P.emit("sync", lambda s, h=h, r=r: s.dma_start(
                    out=KTh.rearrange("p (k r c) -> p k r c", r=8, c=128)[:, :, r, :],
                    in_=KT_gath[r * 2048 + h * 128: r * 2048 + (h + 1) * 128, :]
                        .rearrange("p (k c) -> p k c", k=8)),
                    reads=[rKTg], writes=([rKTh] + kvw) if r == 0 else [rKTh], step=16)
                P.emit("sync", lambda s, h=h, r=r: s.dma_start(
                    out=Vh.rearrange("p (k r c) -> p k r c", r=8, c=128)[:, :, r, :],
                    in_=V_gath[r * 16384 + h * 1024: r * 16384 + (h + 1) * 1024, :]
                        .rearrange("(s p) d -> p s d", p=128)),
                    reads=[rVg], writes=[rVh], step=16)
            for k in range(NSLOT):
                nch = nchunks(k)
                obank = pb[4 + (k % 2)]
                rob = rb[4 + (k % 2)]
                P.emit("tensor", lambda t, h=h, k=k: t.matmul(
                    pb[6][:, 0:128], QT[:, h * 1024 + k * 128: h * 1024 + (k + 1) * 128],
                    kcmpT[:, h * 128:(h + 1) * 128], start=True, stop=True),
                    reads=[rQT[h][k], rkcmpT], writes=[rb[6]])
                P.emit("scalar", lambda sc: sc.activation(ecmp[:], pb[6][:, 0:128], AF.Exp),
                       reads=[rb[6]], writes=[recmp])
                P.emit("vector", lambda v, k=k: v.scalar_tensor_tensor(
                    out=evis[:], in0=ecmp[:], scalar=1.0,
                    in1=visS[:, k * NB:(k + 1) * NB], op0=ALU.mult, op1=ALU.mult,
                    accum_out=Zm[:, 0:1]),
                    reads=[recmp, rvisS], writes=[revis])
                P.emit("vector", lambda v, k=k: v.scalar_tensor_tensor(
                    out=impS[:], in0=cur2S[:, k * NB:(k + 1) * NB], scalar=Zm[:, 0:1],
                    in1=evis[:], op0=ALU.mult, op1=ALU.add),
                    reads=[rcur2S, revis], writes=[rimp])
                P.emit("vector", lambda v: v.max(out=top8[:], in_=impS[:]),
                       reads=[rimp], writes=[rtop8])
                P.emit("vector", lambda v: v.tensor_single_scalar(
                    mselS[:], impS[:], top8[:, 7:8], ALU.is_ge),
                    reads=[rtop8, rimp], writes=[rmsel])
                P.emit("tensor", lambda t: t.matmul(
                    pb[7][:, 0:128], evis[:], ident[:], start=True, stop=True),
                    reads=[revis, rident], writes=[rb[7]])
                P.emit("scalar", lambda sc: sc.copy(ecT[:], pb[7][:, 0:128]),
                       reads=[rb[7]], writes=[recT])
                P.emit("tensor", lambda t, h=h, obank=obank: t.matmul(
                    obank[:, 256:384], ecT[:], vcmp[:, h * 128:(h + 1) * 128],
                    start=True, stop=True),
                    reads=[recT, rvcmp], writes=[rob])
                for j in range(nch):
                    sbk = rb[j % 2]
                    P.emit("tensor", lambda t, h=h, k=k, j=j: t.matmul(
                        pb[j % 2][:],
                        QT[:, h * 1024 + k * 128: h * 1024 + (k + 1) * 128],
                        KTh[:, j * 512:(j + 1) * 512], start=True, stop=True),
                        reads=[rQT[h][k], rKTh], writes=[sbk])
                    P.emit("scalar", lambda sc, j=j: sc.activation(
                        Pring(j % 6), pb[j % 2][:], AF.Exp),
                        reads=[sbk], writes=[rPring[j % 6]])
                    nback = nch - j
                    if nback > 2:
                        for jj in range(2):
                            P.emit("vector", lambda v, j=j, jj=jj: v.scalar_tensor_tensor(
                                out=pview(PmS(j % 2), jj),
                                in0=pview(Pring(j % 6), jj),
                                scalar=1.0, in1=mselexp(j, jj),
                                op0=ALU.mult, op1=ALU.mult,
                                accum_out=Zparts[:, 2 * j + jj: 2 * j + jj + 1]),
                                reads=[rPring[j % 6], rmsel], writes=[rPm[j % 2]])
                    else:
                        for jj in range(2):
                            P.emit("vector", lambda v, j=j, jj=jj: v.tensor_tensor(
                                out=pview(PmS(j % 2), jj),
                                in0=pview(Pring(j % 6), jj),
                                in1=mselexp(j, jj), op=ALU.mult),
                                reads=[rPring[j % 6], rmsel], writes=[rPm[j % 2]])
                        P.emit("vector", lambda v, j=j: v.memset(
                            Zparts[:, 2 * j + 1: 2 * j + 2], 0.0))
                        P.emit("vector", lambda v, j=j, k=k, nback=nback: v.scalar_tensor_tensor(
                            out=PmS(j % 2), in0=PmS(j % 2), scalar=1.0,
                            in1=bmaskS(k)[:, 1024 - 512 * nback: 1536 - 512 * nback],
                            op0=ALU.mult, op1=ALU.mult,
                            accum_out=Zparts[:, 2 * j: 2 * j + 1]),
                            reads=[rbm], writes=[rPm[j % 2]])
                    P.emit("tensor", lambda t, j=j: [
                        t.matmul(pb[2 + (j % 2)][:, i * 128:(i + 1) * 128],
                                 PmS(j % 2)[:, i * 128:(i + 1) * 128], ident[:],
                                 start=(i == 0), stop=(i == 3))
                        for i in range(4)][-1],
                        reads=[rPm[j % 2], rident], writes=[rb[2 + (j % 2)]])
                    P.emit("scalar", lambda sc, j=j: sc.copy(
                        PTsS(j % 2), pb[2 + (j % 2)][:]),
                        reads=[rb[2 + (j % 2)]], writes=[rPTs[j % 2]])
                    P.emit("tensor", lambda t, j=j, obank=obank: [
                        t.matmul(obank[:, 0:128],
                                 PTsS(j % 2)[:, i * 128:(i + 1) * 128],
                                 Vh[:, (j * 4 + i) * 128:(j * 4 + i + 1) * 128],
                                 start=False, stop=False)
                        for i in range(4)][-1],
                        reads=[rPTs[j % 2], rVh], writes=[rob])
                    if nback <= nswa(k):
                        jsw = j - (nch - nswa(k))
                        P.emit("vector", lambda v, j=j, k=k, nback=nback, jsw=jsw:
                               v.scalar_tensor_tensor(
                                   out=PswS(jsw % 2), in0=Pring(j % 6), scalar=1.0,
                                   in1=swamS(k)[:, 1536 - 512 * nback: 2048 - 512 * nback],
                                   op0=ALU.mult, op1=ALU.mult,
                                   accum_out=ZswaS[:, jsw:jsw + 1]),
                               reads=[rPring[j % 6], rswm], writes=[rPsw[jsw % 2]])
                        P.emit("tensor", lambda t, jsw=jsw: [
                            t.matmul(pb[2 + (jsw % 2)][:, i * 128:(i + 1) * 128],
                                     PswS(jsw % 2)[:, i * 128:(i + 1) * 128], ident[:],
                                     start=(i == 0), stop=(i == 3))
                            for i in range(4)][-1],
                            reads=[rPsw[jsw % 2], rident], writes=[rb[2 + (jsw % 2)]])
                        P.emit("scalar", lambda sc, jsw=jsw: sc.copy(
                            PTsS(jsw % 2), pb[2 + (jsw % 2)][:]),
                            reads=[rb[2 + (jsw % 2)]], writes=[rPTs[jsw % 2]])
                        P.emit("tensor", lambda t, j=j, jsw=jsw, obank=obank: [
                            t.matmul(obank[:, 128:256],
                                     PTsS(jsw % 2)[:, i * 128:(i + 1) * 128],
                                     Vh[:, (j * 4 + i) * 128:(j * 4 + i + 1) * 128],
                                     start=False, stop=False)
                            for i in range(4)][-1],
                            reads=[rPTs[jsw % 2], rVh], writes=[rob])
                P.emit("vector", lambda v, nch=nch: v.reduce_sum(
                    Zm[:, 1:2], Zparts[:, 0:2 * nch], AX))
                P.emit("vector", lambda v, k=k: v.reduce_sum(
                    Zm[:, 2:3], ZswaS[:, 0:nswa(k)], AX))
                P.emit("vector", lambda v, k=k: v.tensor_add(
                    Zm[:, 3:4], Zm[:, 0:1], naS[:, k:k + 1]), reads=[rna])
                P.emit("vector", lambda v: v.reciprocal(Zm[:, 4:7], Zm[:, 1:4]))
                P.emit("vector", lambda v, h=h, k=k: v.tensor_mul(
                    Zm[:, 1:2], Zm[:, 4:5], gcol[:, k * 48 + 16 + h: k * 48 + 17 + h]),
                    reads=[rgcol])
                P.emit("vector", lambda v, h=h, k=k: v.tensor_mul(
                    Zm[:, 2:3], Zm[:, 5:6], gcol[:, k * 48 + 32 + h: k * 48 + 33 + h]),
                    reads=[rgcol])
                P.emit("vector", lambda v, h=h, k=k: v.tensor_mul(
                    Zm[:, 3:4], Zm[:, 6:7], gcol[:, k * 48 + h: k * 48 + 1 + h]),
                    reads=[rgcol])
                P.emit("vector", lambda v, k=k: v.tensor_mul(
                    Zm[:, 3:4], Zm[:, 3:4], anS[:, k:k + 1]), reads=[ran])
                P.emit("vector", lambda v, obank=obank: v.tensor_scalar(
                    f32scr[:, 0:128], obank[:, 0:128], Zm[:, 1:2], None, ALU.mult),
                    reads=[rob])
                P.emit("vector", lambda v, obank=obank: v.scalar_tensor_tensor(
                    out=f32scr[:, 0:128], in0=obank[:, 128:256], scalar=Zm[:, 2:3],
                    in1=f32scr[:, 0:128], op0=ALU.mult, op1=ALU.add),
                    reads=[rob])
                P.emit("vector", lambda v, obank=obank: v.scalar_tensor_tensor(
                    out=ogS[:], in0=obank[:, 256:384], scalar=Zm[:, 3:4],
                    in1=f32scr[:, 0:128], op0=ALU.mult, op1=ALU.add),
                    reads=[rob], writes=[rog])
                P.emit("tensor", lambda t: t.matmul(
                    pb[7][:, 0:128], ogS[:], ident[:], start=True, stop=True),
                    reads=[rog, rident], writes=[rb[7]])
                P.emit("scalar", lambda sc, h=h, k=k: sc.copy(
                    QT[:, h * 1024 + k * 128: h * 1024 + (k + 1) * 128], pb[7][:, 0:128]),
                    reads=[rb[7]], writes=[rQT[h][k]])

        for ht in range(NDT):
            p = ht % 2
            P.emit("gpsimd", lambda g, ht=ht, p=p: g.dma_start(
                out=sWoS(p).rearrange("p (t e) -> p t e", t=16),
                in_=Wo[:, ht * 128:(ht + 1) * 128].rearrange("(t p) e -> p t e", p=128)),
                writes=[rsWo[p]] + ([rsVW[1]] if ht < 2 else []), step=16)
            for tc in range(2):
                bank = rb[tc]
                P.emit("sync", lambda s, ht=ht, tc=tc: s.dma_start(
                    out=pstage[:, tc * 512:(tc + 1) * 512],
                    in_=xT_loc[ht * 128:(ht + 1) * 128, tc * 512:(tc + 1) * 512]),
                    writes=[rpst[tc]] + ([rsqt] if ht == 0 else []), step=16)
                P.emit("tensor", lambda t, ht=ht, tc=tc, p=p: [
                    t.matmul(pb[tc][:],
                             sWoS(p)[:, hh * 128:(hh + 1) * 128],
                             QT[:, hh * 1024 + tc * 512: hh * 1024 + (tc + 1) * 512],
                             start=(hh == 0), stop=(hh == H - 1))
                    for hh in range(H)][-1],
                    reads=[rsWo[p]] + [rQT[hh][kk] for hh in range(H)
                                       for kk in range(4 * tc, 4 * tc + 4)],
                    writes=[bank])
                P.emit("vector", lambda v, tc=tc: v.scalar_tensor_tensor(
                    out=ostage[:, tc * 512:(tc + 1) * 512], in0=pb[tc][:], scalar=1.0,
                    in1=pstage[:, tc * 512:(tc + 1) * 512], op0=ALU.mult, op1=ALU.add),
                    reads=[bank, rpst[tc]], writes=[rost[tc]])
                P.emit("gpsimd", lambda g, ht=ht, tc=tc: g.dma_start(
                    out=part_scr[ht * 128:(ht + 1) * 128, tc * 512:(tc + 1) * 512],
                    in_=ostage[:, tc * 512:(tc + 1) * 512]),
                    reads=[rost[tc]], writes=[rpart], step=16)

        P.emit("sync", lambda s: s.dma_start(out=xnT[:], in_=xnt_spill),
               reads=[rspill], writes=[rxnT, rbm] + rPring + rPm + rPsw + rPTs,
               extra_waits=(), step=16)
        first_hdn = True
        for tc in range(2):
            for it in range(NIT):
                p = it % 2
                P.emit("gpsimd", lambda g, it=it, p=p: g.dma_start(
                    out=sGW(p).rearrange("p (t e) -> p t e", t=16),
                    in_=gwT[:, it * 128:(it + 1) * 128].rearrange("(t p) e -> p t e", p=128)),
                    writes=[rsGW[p]] + ([rsWo[0], rsWo[1], rswm] if (tc == 0 and it < 2) else []),
                    step=16)
                P.emit("gpsimd", lambda g, it=it, p=p: g.dma_start(
                    out=sUW(p).rearrange("p (t e) -> p t e", t=16),
                    in_=uwT[:, it * 128:(it + 1) * 128].rearrange("(t p) e -> p t e", p=128)),
                    writes=[rsUW[p]], step=16)
                P.emit("tensor", lambda t, it=it, tc=tc, p=p: [
                    t.matmul(pb[0][:],
                             sGW(p)[:, dt * 128:(dt + 1) * 128],
                             xnT[:, dt * 1024 + tc * 512: dt * 1024 + (tc + 1) * 512],
                             start=(dt == 0), stop=(dt == NDT - 1))
                    for dt in range(NDT)][-1],
                    reads=[rsGW[p], rxnT], writes=[rb[0]])
                P.emit("tensor", lambda t, it=it, tc=tc, p=p: [
                    t.matmul(pb[1][:],
                             sUW(p)[:, dt * 128:(dt + 1) * 128],
                             xnT[:, dt * 1024 + tc * 512: dt * 1024 + (tc + 1) * 512],
                             start=(dt == 0), stop=(dt == NDT - 1))
                    for dt in range(NDT)][-1],
                    reads=[rsUW[p], rxnT], writes=[rb[1]])
                P.emit("scalar", lambda sc, it=it: sc.activation(
                    sg_t[:], pb[0][:], AF.Silu, scale=gsS[:, it:it + 1]),
                    reads=[rb[0], rgs], writes=[rsgt])
                hw = [rhdnT]
                if first_hdn:
                    hw = [rhdnT, rKTh, rVh] + QT_all
                    first_hdn = False
                P.emit("vector", lambda v, it=it: v.scalar_tensor_tensor(
                    out=hdnT(it), in0=pb[1][:], scalar=usS[:, it:it + 1],
                    in1=sg_t[:], op0=ALU.mult, op1=ALU.mult),
                    reads=[rb[1], rus, rsgt], writes=hw)
            for ht in range(NDT):
                bank = rb[2 + (ht % 2)]
                P.emit("sync", lambda s, ht=ht, tc=tc: s.dma_start(
                    out=pstage[:, 0:512],
                    in_=part_scr[ht * 128:(ht + 1) * 128, tc * 512:(tc + 1) * 512]),
                    reads=[rpart], writes=[rpst[0]], step=16)
                for hf in range(2):
                    q = (ht * 2 + hf) % 2
                    P.emit("gpsimd", lambda g, ht=ht, hf=hf, q=q: g.dma_start(
                        out=sDW2(q).rearrange("p (t e) -> p t e", t=22),
                        in_=dwT[hf * 2816:(hf + 1) * 2816, ht * 128:(ht + 1) * 128]
                        .rearrange("(t p) e -> p t e", p=128)),
                        writes=[rsDW[q]] + ([rswm, rsWo[0], rsWo[1]]
                                            if (tc == 0 and ht == 0) else []),
                        step=16)
                    P.emit("tensor", lambda t, ht=ht, hf=hf, q=q: [
                        t.matmul(pb[2 + (ht % 2)][:],
                                 sDW2(q)[:, i * 128:(i + 1) * 128], hdnT(hf * 22 + i),
                                 start=(hf == 0 and i == 0),
                                 stop=(hf == 1 and i == 21))
                        for i in range(22)][-1],
                        reads=[rsDW[q], rhdnT], writes=[bank])
                P.emit("vector", lambda v, ht=ht: v.scalar_tensor_tensor(
                    out=ostage[:, 0:512], in0=pb[2 + (ht % 2)][:], scalar=dsS[:, ht:ht + 1],
                    in1=pstage[:, 0:512], op0=ALU.mult, op1=ALU.add),
                    reads=[bank, rds, rpst[0]], writes=[rost[0]])
                P.emit("gpsimd", lambda g, ht=ht, tc=tc: g.dma_start(
                    out=outT[ht * 128:(ht + 1) * 128, tc * 512:(tc + 1) * 512],
                    in_=ostage[:, 0:512]),
                    reads=[rost[0]], step=16)

        P.wait_all("gpsimd")
        P.wait_all("sync")

        with nc.Block() as block:
            P.run(block)
    return nc, taps


_COS_SIN = None


def kernel(**inputs):
    global _COS_SIN
    if _COS_SIN is None:
        _COS_SIN = _rope_tables()
    cos, sin = _COS_SIN
    in_maps = [_core_inputs(c, inputs, cos, sin) for c in range(NCORE)]
    nc, _ = build()
    res = run_bass_kernel_spmd(nc, in_maps, core_ids=list(range(NCORE)))
    out = np.empty((N, D), np.float32)
    for c in range(NCORE):
        oT = np.asarray(res.results[c]["outT"])
        for k in range(NSLOT):
            t = 8 * k + c
            out[t * 128:(t + 1) * 128, :] = oT[:, k * 128:(k + 1) * 128].T
    return out[None]


# version 05
# speedup vs baseline: 1.0034x; 1.0034x over previous
import numpy as np
import ml_dtypes
from contextlib import ExitStack

import concourse.bass as bass
import concourse.mybir as mybir
from concourse.bass_utils import run_bass_kernel_spmd

F32 = mybir.dt.float32
BF16 = mybir.dt.bfloat16
AF = mybir.ActivationFunctionType
ALU = mybir.AluOpType
AX = mybir.AxisListType.X
NPBF = ml_dtypes.bfloat16

N, D, H, HD = 8192, 2048, 16, 128
NCORE, NSLOT = 8, 8
BS, NB, SEL, WIN = 64, 128, 8, 64
INTER = 5632
NIT = INTER // 128
NDT = D // 128
CH = 512
TOK = NSLOT * 128

ENGS = ("tensor", "vector", "scalar", "sync", "gpsimd")


def nchunks(k):
    return 2 * (k + 1)


def nswa(k):
    return min(3, nchunks(k))


class _Sem:
    def __init__(self, h):
        self.h = h
        self.n = 0


class _Res:
    __slots__ = ("w", "readers")

    def __init__(self):
        self.w = None
        self.readers = []


class _Op:
    __slots__ = ("fn", "waits", "incs", "tag")

    def __init__(self, fn, waits, incs, tag=None):
        self.fn, self.waits, self.incs, self.tag = fn, waits, incs, tag


class Prog:

    def __init__(self, nc, ctx):
        self.nc = nc
        self.ops = {e: [] for e in ENGS}
        self.sem = {e: _Sem(ctx.enter_context(nc.semaphore(f"s_{e}"))) for e in ENGS}

    def emit(self, eng, fn, reads=(), writes=(), extra_waits=(), step=1, tag=None):
        s = self.sem[eng]
        want = {}

        def add(c):
            if c is None:
                return
            sm, v = c
            k = id(sm)
            if k not in want or want[k][1] < v:
                want[k] = (sm, v)

        for r in reads:
            add(r.w)
        for w in writes:
            add(w.w)
            for c in w.readers:
                add(c)
        for ew in extra_waits:
            add(ew)
        wl = []
        for sm, v in want.values():
            if sm is s and eng == "tensor":
                continue
            wl.append((sm, v))
        if eng in ("vector", "scalar") and s.n > 0:
            wl.append((s, s.n))
        best = {}
        for sm, v in wl:
            if id(sm) not in best or best[id(sm)][1] < v:
                best[id(sm)] = (sm, v)
        s.n += step
        self.ops[eng].append(_Op(fn, list(best.values()), [(s, step)], tag))
        comp = (s, s.n)
        for w in writes:
            w.w = comp
            w.readers = []
        for r in reads:
            r.readers.append(comp)
        return comp

    def wait_all(self, eng):
        s = self.sem[eng]
        self.ops[eng].append(_Op(None, [(s, s.n)], []))

    def run(self, block):
        def mk(name):
            def body(engine):
                for op in self.ops[name]:
                    for sm, v in op.waits:
                        engine.wait_ge(sm.h, v)
                    if op.fn is not None:
                        ins = op.fn(engine)
                        for sm, k in op.incs:
                            ins.then_inc(sm.h, k)
            return body

        block.tensor(mk("tensor"))
        block.vector(mk("vector"))
        block.scalar(mk("scalar"))
        block.sync(mk("sync"))
        block.gpsimd(mk("gpsimd"))


def _rope_tables():
    half = HD // 2
    inv = 1.0 / 10000.0 ** (np.arange(half, dtype=np.float64) / half)
    th = np.arange(N, dtype=np.float64)[:, None] * inv[None, :]
    return np.cos(th).T.astype(np.float32), np.sin(th).T.astype(np.float32)


def _perm_blocks(a):
    v = a.reshape(a.shape[:-1] + (8, 8, 2))
    v = np.swapaxes(v, -3, -2)
    return np.ascontiguousarray(v.reshape(a.shape))


def _core_inputs(c, full, cos, sin):
    x = np.asarray(full["L0_attn_x"], np.float32)[0]
    tiles = [8 * k + c for k in range(NSLOT)]
    x_loc = np.stack([x[t * 128:(t + 1) * 128] for t in tiles])
    xT_loc = np.ascontiguousarray(
        np.concatenate([x[t * 128:(t + 1) * 128] for t in tiles], 0).T)
    scale = np.float32(HD ** -0.5)
    cols = np.concatenate([np.arange(t * 128, (t + 1) * 128) for t in tiles])
    rope = np.concatenate(
        [cos[:, cols] * scale, sin[:, cols] * scale, cos[:, cols], sin[:, cols]],
        axis=1).astype(NPBF)
    rope = np.vstack([rope, rope])

    bmask = np.zeros((NSLOT, 128, 1024), np.float32)
    swam = np.zeros((NSLOT, 128, 1536), np.float32)
    vis = np.zeros((NSLOT, 128, NB), np.float32)
    cur2 = np.zeros((NSLOT, 128, NB), np.float32)
    notany = np.zeros((128, NSLOT), np.float32)
    any01 = np.zeros((128, NSLOT), np.float32)
    r = np.arange(128)
    mm = np.arange(NB)
    for k in range(NSLOT):
        t = 8 * k + c
        T0 = t * 128
        rows = T0 + r
        scan = 1024 * (k + 1)
        j = np.arange(scan - 1024, scan)
        bmask[k] = (j[None, :] <= rows[:, None]) & (j[None, :] < T0 + 128)
        w = 512 * nswa(k)
        j2 = np.arange(scan - w, scan)
        m = (j2[None, :] <= rows[:, None]) & (rows[:, None] - j2[None, :] <= WIN)
        swam[k, :, 1536 - w:] = m
        vis[k] = rows[:, None] >= ((mm + 1) * 64 - 1)[None, :]
        cur2[k, r, rows // 64] = 2.0
        na = ~(vis[k].any(1))
        notany[:, k] = na
        any01[:, k] = ~na
    gs = np.asarray(full["L0_ffn_gate_s"], np.float32)
    us = np.asarray(full["L0_ffn_up_s"], np.float32)
    ds = np.asarray(full["L0_ffn_down_s"], np.float32)
    return {
        "x_loc": x_loc,
        "xT_loc": xT_loc,
        "rope": rope,
        "bmask": bmask.astype(NPBF),
        "swam": swam.astype(NPBF),
        "vis": _perm_blocks(vis).astype(NPBF),
        "cur2": _perm_blocks(cur2).astype(NPBF),
        "notany": notany,
        "any01": any01,
        "Wq": np.asarray(full["L0_attn_Wq"], np.float32),
        "Wk": np.asarray(full["L0_attn_Wk"], np.float32),
        "Wv": np.asarray(full["L0_attn_Wv"], np.float32),
        "Wg": np.asarray(full["L0_attn_Wg"], np.float32),
        "Wo": np.asarray(full["L0_attn_Wo"], np.float32),
        "gwT": np.ascontiguousarray(np.asarray(full["L0_ffn_gate_w"], np.float32).T),
        "uwT": np.ascontiguousarray(np.asarray(full["L0_ffn_up_w"], np.float32).T),
        "dwT": np.ascontiguousarray(np.asarray(full["L0_ffn_down_w"], np.float32).T),
        "gs_pt": np.ascontiguousarray(gs.reshape(NIT, 128).T),
        "us_pt": np.ascontiguousarray(us.reshape(NIT, 128).T),
        "ds_pt": np.ascontiguousarray(ds.reshape(NDT, 128).T),
    }


def build(debug_taps=()):
    nc = bass.Bass("TRN2", target_bir_lowering=False)
    RG = [list(range(NCORE))]

    def din(name, shape, dt=F32):
        return nc.dram_tensor(name, shape, dt, kind="ExternalInput").ap()

    x_loc = din("x_loc", [NSLOT, 128, D])
    xT_loc = din("xT_loc", [D, TOK])
    rope_t = din("rope", [128, 4096], BF16)
    bmask_i = din("bmask", [NSLOT, 128, 1024], BF16)
    swam_i = din("swam", [NSLOT, 128, 1536], BF16)
    vis_i = din("vis", [NSLOT, 128, NB], BF16)
    cur2_i = din("cur2", [NSLOT, 128, NB], BF16)
    notany_i = din("notany", [128, NSLOT])
    any01_i = din("any01", [128, NSLOT])
    Wq = din("Wq", [D, D]); Wk = din("Wk", [D, D]); Wv = din("Wv", [D, D])
    Wg = din("Wg", [D, 3 * H]); Wo = din("Wo", [D, D])
    gwT = din("gwT", [D, INTER]); uwT = din("uwT", [D, INTER])
    dwT = din("dwT", [INTER, D])
    gs_pt = din("gs_pt", [128, NIT]); us_pt = din("us_pt", [128, NIT])
    ds_pt = din("ds_pt", [128, NDT])
    outT = nc.dram_tensor("outT", [D, TOK], F32, kind="ExternalOutput").ap()

    KT_contrib = nc.dram_tensor("KT_contrib", [H * 128, TOK], BF16, kind="Internal").ap()
    V_contrib = nc.dram_tensor("V_contrib", [H * NSLOT * 128, 128], BF16, kind="Internal").ap()
    KC_contrib = nc.dram_tensor("KC_contrib", [16, D], BF16, kind="Internal").ap()
    VC_contrib = nc.dram_tensor("VC_contrib", [16, D], BF16, kind="Internal").ap()
    KT_gath = nc.dram_tensor("KT_gath", [NCORE * H * 128, TOK], BF16, kind="Internal").ap()
    V_gath = nc.dram_tensor("V_gath", [NCORE * H * NSLOT * 128, 128], BF16, kind="Internal").ap()
    KC_gath = nc.dram_tensor("KC_gath", [NCORE * 16, D], BF16, kind="Internal").ap()
    VC_gath = nc.dram_tensor("VC_gath", [NCORE * 16, D], BF16, kind="Internal").ap()
    xnt_spill = nc.dram_tensor("xnt_spill", [128, NDT * TOK], BF16, kind="Internal").ap()
    part_scr = nc.dram_tensor("part_scr", [D, TOK], F32, kind="Internal").ap()

    ident_h = nc.inline_tensor(np.eye(128, dtype=NPBF), "identb").ap()
    taps = {}

    with ExitStack() as ctx:
        en = ctx.enter_context

        def sb(name, shape, dt):
            return en(nc.sbuf_tensor(name, shape, dt))

        A_xnt = sb("A_xnt", [128, NDT * TOK], BF16)
        A_qt = sb("A_qt", [128, H * TOK], BF16)
        A_kv = sb("A_kv", [128, H * TOK], BF16)
        A_w = sb("A_w", [128, 16384], BF16)
        ident = sb("identS", [128, 128], BF16)
        vcmp = sb("vcmpS", [128, H * 128], BF16)
        kcmpT = sb("kcmpTS", [128, H * 128], BF16)
        kc16 = sb("kc16", [128, H * 16], BF16)
        kcstage = sb("kcstage", [16, D], BF16)
        vcstage = sb("vcstage", [2, D], BF16)
        ropes = sb("ropes", [128, 4096], BF16)
        rtmp = sb("rtmp", [128, CH], BF16)
        xnnat = sb("xnnat", [128, D], BF16)
        ecmp = sb("ecmpS", [128, 128], BF16)
        evis = sb("evisS", [128, 128], BF16)
        ecT = sb("ecTS", [128, 128], BF16)
        impS = sb("impS", [128, 128], BF16)
        mselS = sb("mselS", [128, 128], BF16)
        ogS = sb("ogS", [128, 128], BF16)
        visS = sb("visS", [128, NSLOT * NB], BF16)
        cur2S = sb("cur2S", [128, NSLOT * NB], BF16)
        sWg = sb("sWgS", [128, NDT * 48], BF16)
        sg_t = sb("sg_t", [128, 2 * CH], BF16)
        xstage = sb("xstage", [128, D], F32)
        F32A = sb("F32A", [128, D], F32)
        sqt = F32A[:]
        f32scr = sb("f32scr", [128, 1024], F32)
        lnst = sb("lnst", [128, 8], F32)
        gcol = sb("gcolS", [128, NSLOT * 48], F32)
        Zparts = sb("ZpartsS", [128, 32], F32)
        ZswaS = sb("ZswaS", [128, 4], F32)
        Zm = sb("ZmS", [128, 8], F32)
        top8 = sb("top8S", [128, 8], F32)
        gsS = sb("gsS", [128, NIT], F32)
        usS = sb("usS", [128, NIT], F32)
        dsS = sb("dsS", [128, NDT], F32)
        naS = sb("naS", [128, NSLOT], F32)
        anS = sb("anS", [128, NSLOT], F32)
        pstage = F32A[:, 0:2 * CH]
        ostage = F32A[:, 2 * CH:4 * CH]
        pb = [en(nc.psum_tensor(f"pb{i}", [128, CH], F32)) for i in range(8)]

        xnT = A_xnt
        KT_loc = A_qt
        V_loc = A_kv
        QT = A_qt
        KTh = A_kv[:, 0:8192]
        Vh = A_kv[:, 8192:16384]
        bmaskS = lambda k: A_xnt[:, k * 1024:(k + 1) * 1024]
        Pring = lambda i: A_xnt[:, 8192 + i * 512: 8192 + (i + 1) * 512]
        PmS = lambda i: A_xnt[:, 11264 + i * 512: 11264 + (i + 1) * 512]
        PswS = lambda i: A_xnt[:, 12288 + i * 512: 12288 + (i + 1) * 512]
        PTsS = lambda i: A_xnt[:, 13312 + i * 512: 13312 + (i + 1) * 512]
        swamS = lambda k: A_w[:, k * 1536:(k + 1) * 1536]
        sWoS = lambda p: A_w[:, 12288 + p * 2048: 12288 + (p + 1) * 2048]
        sGW = lambda p: A_kv[:, 6144 + p * 2048: 6144 + (p + 1) * 2048]
        sUW = lambda p: A_w[:, p * 2048:(p + 1) * 2048]
        sDW2 = lambda q: A_w[:, 8192 + q * 2816: 8192 + (q + 1) * 2816]
        sKW = lambda p: A_w[:, p * 2048:(p + 1) * 2048]
        sVW = lambda q: A_w[:, q * 8192:(q + 1) * 8192]

        def hdnT(it):
            if it < 32:
                return A_qt[:, it * 512:(it + 1) * 512]
            return A_kv[:, (it - 32) * 512:(it - 31) * 512]

        kcnat = A_kv[:, 0:H * 128]

        rb = [_Res() for _ in range(8)]
        rPring = [_Res() for _ in range(6)]
        rPm = [_Res() for _ in range(2)]
        rPsw = [_Res() for _ in range(2)]
        rPTs = [_Res() for _ in range(2)]
        rQT = [[_Res() for _ in range(NSLOT)] for _ in range(H)]
        rKTloc = [_Res() for _ in range(H)]
        rVloc = _Res()
        rKTh, rVh = _Res(), _Res()
        rxnT = _Res()
        rbm, rswm = _Res(), _Res()
        rsWo = [_Res(), _Res()]
        rsGW = [_Res() for _ in range(4)]
        rsUW = [_Res() for _ in range(4)]
        rsDW = [_Res(), _Res()]
        rsKW = [_Res(), _Res()]
        rsVW = [_Res(), _Res()]
        rxstage, rsqt, rlnst, rxnnat = _Res(), _Res(), _Res(), _Res()
        rropes, rrtmp, rident = _Res(), _Res(), _Res()
        rvcmp, rkcnat, rkcmpT, rkc16 = _Res(), _Res(), _Res(), _Res()
        rkcstage, rvcstage = _Res(), _Res()
        recmp, revis, recT, rimp, rmsel, rog = (_Res() for _ in range(6))
        rvisS, rcur2S, rgcol, rsWg, rsgt = (_Res() for _ in range(5))
        rtop8 = _Res()
        rgs, rus, rds, rna, ran = (_Res() for _ in range(5))
        rpst = [_Res(), _Res()]
        rost = [_Res(), _Res()]
        rKTc, rVc, rKCc, rVCc = (_Res() for _ in range(4))
        rKTg, rVg, rKCg, rVCg = (_Res() for _ in range(4))
        rspill, rpart = _Res(), _Res()
        rhdnT = _Res()
        QT_all = [r for hh in rQT for r in hh]

        P = Prog(nc, ctx)

        def emit_rope(dst, co, si, wr, extra_reads=()):
            P.emit("vector", lambda v, dst=dst, co=co: v.tensor_mul(
                dst[0:64, :], rtmp[0:64, :], ropes[0:64, co:co + 512]),
                reads=[rrtmp, rropes] + list(extra_reads), writes=wr)
            P.emit("vector", lambda v, si=si: v.tensor_mul(
                f32scr[0:64, 0:512], rtmp[64:128, :], ropes[64:128, si:si + 512]),
                reads=[rrtmp, rropes])
            P.emit("vector", lambda v, dst=dst: v.tensor_sub(
                dst[0:64, :], dst[0:64, :], f32scr[0:64, 0:512]),
                writes=wr)
            P.emit("vector", lambda v, dst=dst, si=si: v.tensor_mul(
                dst[64:128, :], rtmp[0:64, :], ropes[0:64, si:si + 512]),
                reads=[rrtmp, rropes], writes=wr)
            P.emit("vector", lambda v, co=co: v.tensor_mul(
                f32scr[64:128, 0:512], rtmp[64:128, :], ropes[64:128, co:co + 512]),
                reads=[rrtmp, rropes])
            P.emit("vector", lambda v, dst=dst: v.tensor_add(
                dst[64:128, :], dst[64:128, :], f32scr[64:128, 0:512]),
                writes=wr)

        def tap(name, ap, reads, dt=BF16):
            if name not in debug_taps:
                return
            t = nc.dram_tensor("tap_" + name, list(ap.shape), dt,
                               kind="ExternalOutput").ap()
            taps[name] = True
            P.emit("gpsimd", lambda g, t=t, ap=ap: g.dma_start(out=t, in_=ap),
                   reads=list(reads), step=16)

        P.emit("sync", lambda s: s.dma_start(out=ident[:], in_=ident_h),
               writes=[rident], step=16)
        P.emit("sync", lambda s: s.dma_start(out=ropes[:], in_=rope_t),
               writes=[rropes], step=16)
        P.emit("sync", lambda s: s.dma_start(out=visS[:].rearrange("p (k b) -> p k b", k=8), in_=vis_i.rearrange("k p b -> p k b")),
               writes=[rvisS], step=16)
        P.emit("sync", lambda s: s.dma_start(out=cur2S[:].rearrange("p (k b) -> p k b", k=8), in_=cur2_i.rearrange("k p b -> p k b")),
               writes=[rcur2S], step=16)
        P.emit("sync", lambda s: s.dma_start(out=naS[:], in_=notany_i), writes=[rna], step=16)
        P.emit("sync", lambda s: s.dma_start(out=anS[:], in_=any01_i), writes=[ran], step=16)
        P.emit("sync", lambda s: s.dma_start(out=gsS[:], in_=gs_pt), writes=[rgs], step=16)
        P.emit("sync", lambda s: s.dma_start(out=usS[:], in_=us_pt), writes=[rus], step=16)
        P.emit("sync", lambda s: s.dma_start(out=dsS[:], in_=ds_pt), writes=[rds], step=16)
        P.emit("gpsimd", lambda g: g.dma_start(
            out=sWg[:].rearrange("p (t e) -> p t e", t=16), in_=Wg.rearrange("(t p) e -> p t e", p=128)),
            writes=[rsWg], step=16)

        inv_d = float(1.0 / D)
        for s in range(NSLOT):
            P.emit("sync", lambda sy, s=s: sy.dma_start(out=xstage[:], in_=x_loc[s]),
                   writes=[rxstage], step=16)
            P.emit("vector", lambda v: v.reduce_sum(lnst[:, 0:1], xstage[:], AX),
                   reads=[rxstage], writes=[rlnst])
            P.emit("vector", lambda v: v.tensor_scalar_mul(lnst[:, 1:2], lnst[:, 0:1], -inv_d),
                   writes=[rlnst])
            P.emit("scalar", lambda sc: sc.activation(sqt[:], xstage[:], AF.Square,
                                                      bias=lnst[:, 1:2]),
                   reads=[rxstage, rlnst], writes=[rsqt])
            P.emit("vector", lambda v: v.reduce_sum(lnst[:, 2:3], sqt[:], AX),
                   reads=[rsqt], writes=[rlnst])
            P.emit("vector", lambda v: v.tensor_scalar(
                lnst[:, 2:3], lnst[:, 2:3], inv_d, 1e-5, ALU.mult, ALU.add),
                writes=[rlnst])
            P.emit("scalar", lambda sc: sc.activation(lnst[:, 3:4], lnst[:, 2:3], AF.Sqrt),
                   reads=[rlnst], writes=[rlnst])
            P.emit("vector", lambda v: v.reciprocal(lnst[:, 4:5], lnst[:, 3:4]),
                   reads=[rlnst], writes=[rlnst])
            P.emit("vector", lambda v: v.tensor_mul(lnst[:, 5:6], lnst[:, 1:2], lnst[:, 4:5]),
                   writes=[rlnst])
            P.emit("vector", lambda v: v.tensor_scalar(
                xnnat[:], xstage[:], lnst[:, 4:5], lnst[:, 5:6], ALU.mult, ALU.add),
                reads=[rxstage, rlnst], writes=[rxnnat])
            for g4 in range(4):
                bank = rb[g4 % 2]
                P.emit("tensor", lambda t, g4=g4: [
                    t.matmul(pb[g4 % 2][:, i * 128:(i + 1) * 128],
                             xnnat[:, (g4 * 4 + i) * 128:(g4 * 4 + i + 1) * 128],
                             ident[:], start=(i == 0), stop=(i == 3))
                    for i in range(4)][-1],
                    reads=[rxnnat, rident], writes=[bank])
                P.emit("scalar", lambda sc, s=s, g4=g4: sc.copy(
                    xnT[:].rearrange("p (t c) -> p t c", t=NDT)[
                        :, g4 * 4:(g4 + 1) * 4, s * 128:(s + 1) * 128],
                    pb[g4 % 2][:].rearrange("p (t c) -> p t c", t=4)),
                    reads=[bank], writes=[rxnT])
        tap("xnT", xnT[:, 0:4096], [rxnT])

        for h in range(H):
            P.emit("gpsimd", lambda g, h=h: g.dma_start(
                out=sKW(h % 2).rearrange("p (t e) -> p t e", t=16),
                in_=Wk[:, h * 128:(h + 1) * 128].rearrange("(t p) e -> p t e", p=128)),
                writes=[rsKW[h % 2]], step=16)
            for tc in range(2):
                bank = rb[(h * 2 + tc) % 2]
                P.emit("tensor", lambda t, h=h, tc=tc: [
                    t.matmul(pb[(h * 2 + tc) % 2][:],
                             sKW(h % 2)[:, dt * 128:(dt + 1) * 128],
                             xnT[:, dt * 1024 + tc * 512: dt * 1024 + (tc + 1) * 512],
                             start=(dt == 0), stop=(dt == NDT - 1))
                    for dt in range(NDT)][-1],
                    reads=[rsKW[h % 2], rxnT], writes=[bank])
                P.emit("scalar", lambda sc, h=h, tc=tc: sc.copy(
                    rtmp[:], pb[(h * 2 + tc) % 2][:]),
                    reads=[bank], writes=[rrtmp])
                co, si = 2048 + tc * 512, 3072 + tc * 512
                dst = KT_loc[:, h * 1024 + tc * 512: h * 1024 + (tc + 1) * 512]
                emit_rope(dst, co, si, [rKTloc[h]])
            src = KT_loc[:, h * 1024:(h + 1) * 1024]
            P.emit("vector", lambda v, src=src: v.tensor_add(
                f32scr[:, 0:512], src[:, 0:512], src[:, 512:1024]),
                reads=[rKTloc[h]])
            for w in (256, 128, 64, 32, 16):
                P.emit("vector", lambda v, w=w: v.tensor_add(
                    f32scr[:, 0:w], f32scr[:, 0:w], f32scr[:, w:2 * w]))
            P.emit("vector", lambda v, h=h: v.tensor_scalar_mul(
                kc16[:, h * 16:(h + 1) * 16], f32scr[:, 0:16], 1.0 / 64),
                writes=[rkc16])
            P.emit("gpsimd", lambda g, h=h: g.dma_start(
                out=KT_contrib[h * 128:(h + 1) * 128, :],
                in_=KT_loc[:, h * 1024:(h + 1) * 1024]),
                reads=[rKTloc[h]], writes=[rKTc], step=16)
        for h in range(H):
            P.emit("tensor", lambda t, h=h: t.matmul(
                pb[2][0:16, 0:128], kc16[:, h * 16:(h + 1) * 16], ident[:],
                start=True, stop=True),
                reads=[rkc16, rident], writes=[rb[2]])
            P.emit("scalar", lambda sc, h=h: sc.copy(
                kcstage[:, h * 128:(h + 1) * 128], pb[2][0:16, 0:128]),
                reads=[rb[2]], writes=[rkcstage])
        P.emit("gpsimd", lambda g: g.dma_start(out=KC_contrib, in_=kcstage[:]),
               reads=[rkcstage], writes=[rKCc], step=16)

        pool_np = np.zeros((128, 2), dtype=NPBF)
        pool_np[:64, 0] = NPBF(1.0 / 64)
        pool_np[64:, 1] = NPBF(1.0 / 64)
        pool_h = nc.inline_tensor(pool_np, "pool64").ap()
        pool64 = sb("pool64S", [128, 2], BF16)
        rpool = _Res()
        P.emit("sync", lambda s: s.dma_start(out=pool64[:], in_=pool_h),
               writes=[rpool], step=16)
        for hc in range(4):
            q = hc % 2
            P.emit("gpsimd", lambda g, hc=hc, q=q: g.dma_start(
                out=sVW(q).rearrange("p (t e) -> p t e", t=16),
                in_=Wv[:, hc * 512:(hc + 1) * 512].rearrange("(t p) e -> p t e", p=128)),
                writes=[rsVW[q]] + ([rsKW[0], rsKW[1]] if hc < 2 else []), step=16)
            for s in range(NSLOT):
                bank = rb[(s + hc) % 2]
                P.emit("tensor", lambda t, s=s, hc=hc, q=q: [
                    t.matmul(pb[(s + hc) % 2][:],
                             xnT[:, dt * 1024 + s * 128: dt * 1024 + (s + 1) * 128],
                             sVW(q)[:, dt * 512:(dt + 1) * 512],
                             start=(dt == 0), stop=(dt == NDT - 1))
                    for dt in range(NDT)][-1],
                    reads=[rsVW[q], rxnT], writes=[bank])
                P.emit("scalar", lambda sc, s=s, hc=hc: sc.copy(
                    V_loc[:, s * 2048 + hc * 512: s * 2048 + (hc + 1) * 512],
                    pb[(s + hc) % 2][:]),
                    reads=[bank], writes=[rVloc])
                P.emit("tensor", lambda t, s=s, hc=hc: t.matmul(
                    pb[3][0:2, :], pool64[:],
                    V_loc[:, s * 2048 + hc * 512: s * 2048 + (hc + 1) * 512],
                    start=True, stop=True),
                    reads=[rpool, rVloc], writes=[rb[3]])
                P.emit("scalar", lambda sc: sc.copy(vcstage[:, 0:512], pb[3][0:2, :]),
                       reads=[rb[3]], writes=[rvcstage])
                P.emit("gpsimd", lambda g, s=s, hc=hc: g.dma_start(
                    out=VC_contrib[s * 2:(s + 1) * 2, hc * 512:(hc + 1) * 512],
                    in_=vcstage[0:2, 0:512]),
                    reads=[rvcstage], writes=[rVCc], step=16)
        for h in range(H):
            P.emit("gpsimd", lambda g, h=h: g.dma_start(
                out=V_contrib[h * 1024:(h + 1) * 1024, :].rearrange(
                    "(s p) d -> p s d", p=128),
                in_=V_loc[:].rearrange("p (s h c) -> p s h c", s=8, h=H)[:, :, h, :]),
                reads=[rVloc], writes=[rVc], step=16)

        for cin, cout, rin, rout in (
            (KC_contrib, KC_gath, rKCc, rKCg),
            (VC_contrib, VC_gath, rVCc, rVCg),
            (KT_contrib, KT_gath, rKTc, rKTg),
            (V_contrib, V_gath, rVc, rVg),
        ):
            P.emit("gpsimd", lambda g, cin=cin, cout=cout: g.collective_compute(
                "AllGather", ALU.bypass, replica_groups=RG,
                ins=[cin.opt()], outs=[cout.opt()]),
                reads=[rin], writes=[rout], step=1)

        for h in range(H):
            P.emit("gpsimd", lambda g, h=h: g.dma_start(
                out=sKW(h % 2).rearrange("p (t e) -> p t e", t=16),
                in_=Wq[:, h * 128:(h + 1) * 128].rearrange("(t p) e -> p t e", p=128)),
                writes=[rsKW[h % 2]] + ([rsVW[0]] if h < 2 else []), step=16)
            for tc in range(2):
                bank = rb[(h * 2 + tc) % 2]
                P.emit("tensor", lambda t, h=h, tc=tc: [
                    t.matmul(pb[(h * 2 + tc) % 2][:],
                             sKW(h % 2)[:, dt * 128:(dt + 1) * 128],
                             xnT[:, dt * 1024 + tc * 512: dt * 1024 + (tc + 1) * 512],
                             start=(dt == 0), stop=(dt == NDT - 1))
                    for dt in range(NDT)][-1],
                    reads=[rsKW[h % 2], rxnT], writes=[bank])
                P.emit("scalar", lambda sc, h=h, tc=tc: sc.copy(
                    rtmp[:], pb[(h * 2 + tc) % 2][:]),
                    reads=[bank], writes=[rrtmp])
                co, si = tc * 512, 1024 + tc * 512
                dst = QT[:, h * 1024 + tc * 512: h * 1024 + (tc + 1) * 512]
                wr = [rQT[h][4 * tc + i] for i in range(4)]
                emit_rope(dst, co, si, wr, extra_reads=[rKTc])
        for s in range(NSLOT):
            P.emit("tensor", lambda t, s=s: [
                t.matmul(pb[6][:, 0:48],
                         xnT[:, dt * 1024 + s * 128: dt * 1024 + (s + 1) * 128],
                         sWg[:, dt * 48:(dt + 1) * 48],
                         start=(dt == 0), stop=(dt == NDT - 1))
                for dt in range(NDT)][-1],
                reads=[rsWg, rxnT], writes=[rb[6]])
            P.emit("scalar", lambda sc, s=s: sc.activation(
                gcol[:, s * 48:(s + 1) * 48], pb[6][:, 0:48], AF.Sigmoid),
                reads=[rb[6]], writes=[rgcol])
        tap("QT", QT[:, 0:4096], [rQT[0][k] for k in range(8)] + [rQT[1][k] for k in range(8)])

        P.emit("gpsimd", lambda g: g.dma_start(out=xnt_spill, in_=xnT[:]),
               reads=[rxnT], writes=[rspill], step=16)
        P.emit("sync", lambda s: s.dma_start(
            out=A_xnt[:, 0:8192].rearrange("p (k b) -> p k b", k=8), in_=bmask_i.rearrange("k p b -> p k b")),
            writes=[rbm, rxnT], step=16)
        P.emit("sync", lambda s: s.dma_start(
            out=A_w[:, 0:12288].rearrange("p (k b) -> p k b", k=8), in_=swam_i.rearrange("k p b -> p k b")),
            writes=[rswm, rsKW[0], rsKW[1], rsVW[0], rsVW[1]], step=16)
        P.emit("sync", lambda s: s.dma_start(out=vcmp[:], in_=VC_gath),
               reads=[rVCg], writes=[rvcmp], step=16)
        P.emit("sync", lambda s: s.dma_start(out=kcnat[:], in_=KC_gath),
               reads=[rKCg], writes=[rkcnat], step=16)
        for h in range(H):
            P.emit("tensor", lambda t, h=h: t.matmul(
                pb[7][:, 0:128], kcnat[:, h * 128:(h + 1) * 128], ident[:],
                start=True, stop=True),
                reads=[rkcnat, rident], writes=[rb[7]])
            P.emit("scalar", lambda sc, h=h: sc.copy(
                kcmpT[:, h * 128:(h + 1) * 128], pb[7][:, 0:128]),
                reads=[rb[7]], writes=[rkcmpT])

        def mselexp(jc, jj):
            mv = mselS[:].rearrange("p (r sj) -> p r sj", r=8)
            mv = mv[:, 4 * (jc % 2):4 * (jc % 2) + 4, 2 * (jc // 2) + jj]
            return mv.to_broadcast([128, 4, 64])

        def pview(ap, jj):
            return ap.rearrange("p (r j w) -> p r j w", r=4, j=2)[:, :, jj, :]
        for h in range(H):
            kvw = [rVloc, rkcnat] if h == 0 else []
            for r in range(NCORE):
                P.emit("sync", lambda s, h=h, r=r: s.dma_start(
                    out=KTh.rearrange("p (k r c) -> p k r c", r=8, c=128)[:, :, r, :],
                    in_=KT_gath[r * 2048 + h * 128: r * 2048 + (h + 1) * 128, :]
                        .rearrange("p (k c) -> p k c", k=8)),
                    reads=[rKTg], writes=([rKTh] + kvw) if r == 0 else [rKTh], step=16)
                P.emit("sync", lambda s, h=h, r=r: s.dma_start(
                    out=Vh.rearrange("p (k r c) -> p k r c", r=8, c=128)[:, :, r, :],
                    in_=V_gath[r * 16384 + h * 1024: r * 16384 + (h + 1) * 1024, :]
                        .rearrange("(s p) d -> p s d", p=128)),
                    reads=[rVg], writes=[rVh], step=16)
            for k in range(NSLOT):
                nch = nchunks(k)
                obank = pb[4 + (k % 2)]
                rob = rb[4 + (k % 2)]
                P.emit("tensor", lambda t, h=h, k=k: t.matmul(
                    pb[6][:, 0:128], QT[:, h * 1024 + k * 128: h * 1024 + (k + 1) * 128],
                    kcmpT[:, h * 128:(h + 1) * 128], start=True, stop=True),
                    reads=[rQT[h][k], rkcmpT], writes=[rb[6]])
                P.emit("scalar", lambda sc: sc.activation(ecmp[:], pb[6][:, 0:128], AF.Exp),
                       reads=[rb[6]], writes=[recmp])
                P.emit("vector", lambda v, k=k: v.scalar_tensor_tensor(
                    out=evis[:], in0=ecmp[:], scalar=1.0,
                    in1=visS[:, k * NB:(k + 1) * NB], op0=ALU.mult, op1=ALU.mult,
                    accum_out=Zm[:, 0:1]),
                    reads=[recmp, rvisS], writes=[revis])
                P.emit("vector", lambda v, k=k: v.scalar_tensor_tensor(
                    out=impS[:], in0=cur2S[:, k * NB:(k + 1) * NB], scalar=Zm[:, 0:1],
                    in1=evis[:], op0=ALU.mult, op1=ALU.add),
                    reads=[rcur2S, revis], writes=[rimp])
                P.emit("vector", lambda v: v.max(out=top8[:], in_=impS[:]),
                       reads=[rimp], writes=[rtop8])
                P.emit("vector", lambda v: v.tensor_single_scalar(
                    mselS[:], impS[:], top8[:, 7:8], ALU.is_ge),
                    reads=[rtop8, rimp], writes=[rmsel])
                P.emit("tensor", lambda t: t.matmul(
                    pb[7][:, 0:128], evis[:], ident[:], start=True, stop=True),
                    reads=[revis, rident], writes=[rb[7]])
                P.emit("scalar", lambda sc: sc.copy(ecT[:], pb[7][:, 0:128]),
                       reads=[rb[7]], writes=[recT])
                P.emit("tensor", lambda t, h=h, obank=obank: t.matmul(
                    obank[:, 256:384], ecT[:], vcmp[:, h * 128:(h + 1) * 128],
                    start=True, stop=True),
                    reads=[recT, rvcmp], writes=[rob])
                for j in range(nch):
                    sbk = rb[j % 2]
                    P.emit("tensor", lambda t, h=h, k=k, j=j: t.matmul(
                        pb[j % 2][:],
                        QT[:, h * 1024 + k * 128: h * 1024 + (k + 1) * 128],
                        KTh[:, j * 512:(j + 1) * 512], start=True, stop=True),
                        reads=[rQT[h][k], rKTh], writes=[sbk])
                    P.emit("scalar", lambda sc, j=j: sc.activation(
                        Pring(j % 6), pb[j % 2][:], AF.Exp),
                        reads=[sbk], writes=[rPring[j % 6]])
                    nback = nch - j
                    if nback > 2:
                        for jj in range(2):
                            P.emit("vector", lambda v, j=j, jj=jj: v.scalar_tensor_tensor(
                                out=pview(PmS(j % 2), jj),
                                in0=pview(Pring(j % 6), jj),
                                scalar=1.0, in1=mselexp(j, jj),
                                op0=ALU.mult, op1=ALU.mult,
                                accum_out=Zparts[:, 2 * j + jj: 2 * j + jj + 1]),
                                reads=[rPring[j % 6], rmsel], writes=[rPm[j % 2]])
                    else:
                        for jj in range(2):
                            P.emit("vector", lambda v, j=j, jj=jj: v.tensor_tensor(
                                out=pview(PmS(j % 2), jj),
                                in0=pview(Pring(j % 6), jj),
                                in1=mselexp(j, jj), op=ALU.mult),
                                reads=[rPring[j % 6], rmsel], writes=[rPm[j % 2]])
                        P.emit("vector", lambda v, j=j: v.memset(
                            Zparts[:, 2 * j + 1: 2 * j + 2], 0.0))
                        P.emit("vector", lambda v, j=j, k=k, nback=nback: v.scalar_tensor_tensor(
                            out=PmS(j % 2), in0=PmS(j % 2), scalar=1.0,
                            in1=bmaskS(k)[:, 1024 - 512 * nback: 1536 - 512 * nback],
                            op0=ALU.mult, op1=ALU.mult,
                            accum_out=Zparts[:, 2 * j: 2 * j + 1]),
                            reads=[rbm], writes=[rPm[j % 2]])
                    P.emit("tensor", lambda t, j=j: [
                        t.matmul(pb[2 + (j % 2)][:, i * 128:(i + 1) * 128],
                                 PmS(j % 2)[:, i * 128:(i + 1) * 128], ident[:],
                                 start=(i == 0), stop=(i == 3))
                        for i in range(4)][-1],
                        reads=[rPm[j % 2], rident], writes=[rb[2 + (j % 2)]])
                    P.emit("scalar", lambda sc, j=j: sc.copy(
                        PTsS(j % 2), pb[2 + (j % 2)][:]),
                        reads=[rb[2 + (j % 2)]], writes=[rPTs[j % 2]])
                    P.emit("tensor", lambda t, j=j, obank=obank: [
                        t.matmul(obank[:, 0:128],
                                 PTsS(j % 2)[:, i * 128:(i + 1) * 128],
                                 Vh[:, (j * 4 + i) * 128:(j * 4 + i + 1) * 128],
                                 start=False, stop=False)
                        for i in range(4)][-1],
                        reads=[rPTs[j % 2], rVh], writes=[rob])
                    if nback <= nswa(k):
                        jsw = j - (nch - nswa(k))
                        P.emit("vector", lambda v, j=j, k=k, nback=nback, jsw=jsw:
                               v.scalar_tensor_tensor(
                                   out=PswS(jsw % 2), in0=Pring(j % 6), scalar=1.0,
                                   in1=swamS(k)[:, 1536 - 512 * nback: 2048 - 512 * nback],
                                   op0=ALU.mult, op1=ALU.mult,
                                   accum_out=ZswaS[:, jsw:jsw + 1]),
                               reads=[rPring[j % 6], rswm], writes=[rPsw[jsw % 2]])
                        P.emit("tensor", lambda t, jsw=jsw: [
                            t.matmul(pb[2 + (jsw % 2)][:, i * 128:(i + 1) * 128],
                                     PswS(jsw % 2)[:, i * 128:(i + 1) * 128], ident[:],
                                     start=(i == 0), stop=(i == 3))
                            for i in range(4)][-1],
                            reads=[rPsw[jsw % 2], rident], writes=[rb[2 + (jsw % 2)]])
                        P.emit("scalar", lambda sc, jsw=jsw: sc.copy(
                            PTsS(jsw % 2), pb[2 + (jsw % 2)][:]),
                            reads=[rb[2 + (jsw % 2)]], writes=[rPTs[jsw % 2]])
                        P.emit("tensor", lambda t, j=j, jsw=jsw, obank=obank: [
                            t.matmul(obank[:, 128:256],
                                     PTsS(jsw % 2)[:, i * 128:(i + 1) * 128],
                                     Vh[:, (j * 4 + i) * 128:(j * 4 + i + 1) * 128],
                                     start=False, stop=False)
                            for i in range(4)][-1],
                            reads=[rPTs[jsw % 2], rVh], writes=[rob])
                P.emit("vector", lambda v, nch=nch: v.reduce_sum(
                    Zm[:, 1:2], Zparts[:, 0:2 * nch], AX))
                P.emit("vector", lambda v, k=k: v.reduce_sum(
                    Zm[:, 2:3], ZswaS[:, 0:nswa(k)], AX))
                P.emit("vector", lambda v, k=k: v.tensor_add(
                    Zm[:, 3:4], Zm[:, 0:1], naS[:, k:k + 1]), reads=[rna])
                P.emit("vector", lambda v: v.reciprocal(Zm[:, 4:7], Zm[:, 1:4]))
                P.emit("vector", lambda v, h=h, k=k: v.tensor_mul(
                    Zm[:, 1:2], Zm[:, 4:5], gcol[:, k * 48 + 16 + h: k * 48 + 17 + h]),
                    reads=[rgcol])
                P.emit("vector", lambda v, h=h, k=k: v.tensor_mul(
                    Zm[:, 2:3], Zm[:, 5:6], gcol[:, k * 48 + 32 + h: k * 48 + 33 + h]),
                    reads=[rgcol])
                P.emit("vector", lambda v, h=h, k=k: v.tensor_mul(
                    Zm[:, 3:4], Zm[:, 6:7], gcol[:, k * 48 + h: k * 48 + 1 + h]),
                    reads=[rgcol])
                P.emit("vector", lambda v, k=k: v.tensor_mul(
                    Zm[:, 3:4], Zm[:, 3:4], anS[:, k:k + 1]), reads=[ran])
                P.emit("vector", lambda v, obank=obank: v.tensor_scalar(
                    f32scr[:, 0:128], obank[:, 0:128], Zm[:, 1:2], None, ALU.mult),
                    reads=[rob])
                P.emit("vector", lambda v, obank=obank: v.scalar_tensor_tensor(
                    out=f32scr[:, 0:128], in0=obank[:, 128:256], scalar=Zm[:, 2:3],
                    in1=f32scr[:, 0:128], op0=ALU.mult, op1=ALU.add),
                    reads=[rob])
                P.emit("vector", lambda v, obank=obank: v.scalar_tensor_tensor(
                    out=ogS[:], in0=obank[:, 256:384], scalar=Zm[:, 3:4],
                    in1=f32scr[:, 0:128], op0=ALU.mult, op1=ALU.add),
                    reads=[rob], writes=[rog])
                P.emit("tensor", lambda t: t.matmul(
                    pb[7][:, 0:128], ogS[:], ident[:], start=True, stop=True),
                    reads=[rog, rident], writes=[rb[7]])
                P.emit("scalar", lambda sc, h=h, k=k: sc.copy(
                    QT[:, h * 1024 + k * 128: h * 1024 + (k + 1) * 128], pb[7][:, 0:128]),
                    reads=[rb[7]], writes=[rQT[h][k]])

        tap("oT", QT[:, 0:16384], QT_all)
        for ht in range(NDT):
            p = ht % 2
            P.emit("gpsimd", lambda g, ht=ht, p=p: g.dma_start(
                out=sWoS(p).rearrange("p (t e) -> p t e", t=16),
                in_=Wo[:, ht * 128:(ht + 1) * 128].rearrange("(t p) e -> p t e", p=128)),
                writes=[rsWo[p]] + ([rsVW[1]] if ht < 2 else []), step=16)
            for tc in range(2):
                bank = rb[tc]
                P.emit("sync", lambda s, ht=ht, tc=tc: s.dma_start(
                    out=pstage[:, tc * 512:(tc + 1) * 512],
                    in_=xT_loc[ht * 128:(ht + 1) * 128, tc * 512:(tc + 1) * 512]),
                    writes=[rpst[tc]] + ([rsqt] if ht == 0 else []), step=16)
                P.emit("tensor", lambda t, ht=ht, tc=tc, p=p: [
                    t.matmul(pb[tc][:],
                             sWoS(p)[:, hh * 128:(hh + 1) * 128],
                             QT[:, hh * 1024 + tc * 512: hh * 1024 + (tc + 1) * 512],
                             start=(hh == 0), stop=(hh == H - 1))
                    for hh in range(H)][-1],
                    reads=[rsWo[p]] + [rQT[hh][kk] for hh in range(H)
                                       for kk in range(4 * tc, 4 * tc + 4)],
                    writes=[bank])
                P.emit("vector", lambda v, tc=tc: v.scalar_tensor_tensor(
                    out=ostage[:, tc * 512:(tc + 1) * 512], in0=pb[tc][:], scalar=1.0,
                    in1=pstage[:, tc * 512:(tc + 1) * 512], op0=ALU.mult, op1=ALU.add),
                    reads=[bank, rpst[tc]], writes=[rost[tc]])
                P.emit("gpsimd", lambda g, ht=ht, tc=tc: g.dma_start(
                    out=part_scr[ht * 128:(ht + 1) * 128, tc * 512:(tc + 1) * 512],
                    in_=ostage[:, tc * 512:(tc + 1) * 512]),
                    reads=[rost[tc]], writes=[rpart], step=16)

        tap("part", part_scr, [rpart], dt=F32)
        P.emit("sync", lambda s: s.dma_start(out=xnT[:], in_=xnt_spill),
               reads=[rspill], writes=[rxnT, rbm] + rPring + rPm + rPsw + rPTs,
               extra_waits=(), step=16)
        tap("xnT2", xnT[:], [rxnT])
        first_hdn = True
        rsg = [_Res(), _Res()]

        def load_gu(g):
            it = g % NIT
            p = g % 4
            extra_g = [rKTh, rVh] if g < 4 else []
            extra_u = [rswm] if g < 4 else []
            P.emit("gpsimd", lambda gq, it=it, p=p: gq.dma_start(
                out=sGW(p).rearrange("p (t e) -> p t e", t=16),
                in_=gwT[:, it * 128:(it + 1) * 128].rearrange("(t p) e -> p t e", p=128)),
                writes=[rsGW[p]] + extra_g, step=16)
            P.emit("gpsimd", lambda gq, it=it, p=p: gq.dma_start(
                out=sUW(p).rearrange("p (t e) -> p t e", t=16),
                in_=uwT[:, it * 128:(it + 1) * 128].rearrange("(t p) e -> p t e", p=128)),
                writes=[rsUW[p]] + extra_u, step=16)

        for g in range(3):
            load_gu(g)
        for tc in range(2):
            for it in range(NIT):
                g = tc * NIT + it
                if g + 3 < 2 * NIT:
                    load_gu(g + 3)
                p = g % 4
                P.emit("tensor", lambda t, it=it, tc=tc, p=p: [
                    t.matmul(pb[0][:],
                             sGW(p)[:, dt * 128:(dt + 1) * 128],
                             xnT[:, dt * 1024 + tc * 512: dt * 1024 + (tc + 1) * 512],
                             start=(dt == 0), stop=(dt == NDT - 1))
                    for dt in range(NDT)][-1],
                    reads=[rsGW[p], rxnT], writes=[rb[0]])
                P.emit("tensor", lambda t, it=it, tc=tc, p=p: [
                    t.matmul(pb[1][:],
                             sUW(p)[:, dt * 128:(dt + 1) * 128],
                             xnT[:, dt * 1024 + tc * 512: dt * 1024 + (tc + 1) * 512],
                             start=(dt == 0), stop=(dt == NDT - 1))
                    for dt in range(NDT)][-1],
                    reads=[rsUW[p], rxnT], writes=[rb[1]])
                P.emit("scalar", lambda sc, it=it, g=g: sc.activation(
                    sg_t[:, (g % 2) * CH:(g % 2 + 1) * CH], pb[0][:],
                    AF.Silu, scale=gsS[:, it:it + 1]),
                    reads=[rb[0], rgs], writes=[rsg[g % 2]])
                hw = [rhdnT]
                if first_hdn:
                    hw = [rhdnT, rKTh, rVh] + QT_all
                    first_hdn = False
                P.emit("vector", lambda v, it=it, g=g: v.scalar_tensor_tensor(
                    out=hdnT(it), in0=pb[1][:], scalar=usS[:, it:it + 1],
                    in1=sg_t[:, (g % 2) * CH:(g % 2 + 1) * CH],
                    op0=ALU.mult, op1=ALU.mult),
                    reads=[rb[1], rus, rsg[g % 2]], writes=hw)
            if tc == 0:
                tap("hdn0", A_qt[:, 0:16384], [rhdnT])
            for ht in range(NDT):
                bank = rb[2 + (ht % 2)]
                P.emit("sync", lambda s, ht=ht, tc=tc: s.dma_start(
                    out=pstage[:, 0:512],
                    in_=part_scr[ht * 128:(ht + 1) * 128, tc * 512:(tc + 1) * 512]),
                    reads=[rpart], writes=[rpst[0]], step=16)
                for hf in range(2):
                    q = (ht * 2 + hf) % 2
                    P.emit("gpsimd", lambda g, ht=ht, hf=hf, q=q: g.dma_start(
                        out=sDW2(q).rearrange("p (t e) -> p t e", t=22),
                        in_=dwT[hf * 2816:(hf + 1) * 2816, ht * 128:(ht + 1) * 128]
                        .rearrange("(t p) e -> p t e", p=128)),
                        writes=[rsDW[q]] + ([rswm, rsWo[0], rsWo[1]]
                                            if (tc == 0 and ht == 0) else []),
                        step=16)
                    P.emit("tensor", lambda t, ht=ht, hf=hf, q=q: [
                        t.matmul(pb[2 + (ht % 2)][:],
                                 sDW2(q)[:, i * 128:(i + 1) * 128], hdnT(hf * 22 + i),
                                 start=(hf == 0 and i == 0),
                                 stop=(hf == 1 and i == 21))
                        for i in range(22)][-1],
                        reads=[rsDW[q], rhdnT], writes=[bank])
                P.emit("vector", lambda v, ht=ht: v.scalar_tensor_tensor(
                    out=ostage[:, 0:512], in0=pb[2 + (ht % 2)][:], scalar=dsS[:, ht:ht + 1],
                    in1=pstage[:, 0:512], op0=ALU.mult, op1=ALU.add),
                    reads=[bank, rds, rpst[0]], writes=[rost[0]])
                P.emit("gpsimd", lambda g, ht=ht, tc=tc: g.dma_start(
                    out=outT[ht * 128:(ht + 1) * 128, tc * 512:(tc + 1) * 512],
                    in_=ostage[:, 0:512]),
                    reads=[rost[0]], step=16)

        P.wait_all("gpsimd")
        P.wait_all("sync")

        with nc.Block() as block:
            P.run(block)
    return nc, taps


_COS_SIN = None


def kernel(**inputs):
    global _COS_SIN
    if _COS_SIN is None:
        _COS_SIN = _rope_tables()
    cos, sin = _COS_SIN
    in_maps = [_core_inputs(c, inputs, cos, sin) for c in range(NCORE)]
    nc, _ = build()
    res = run_bass_kernel_spmd(nc, in_maps, core_ids=list(range(NCORE)))
    out = np.empty((N, D), np.float32)
    for c in range(NCORE):
        oT = np.asarray(res.results[c]["outT"])
        for k in range(NSLOT):
            t = 8 * k + c
            out[t * 128:(t + 1) * 128, :] = oT[:, k * 128:(k + 1) * 128].T
    return out[None]


# version 08
# speedup vs baseline: 1.1453x; 1.1414x over previous
import numpy as np
import ml_dtypes
from contextlib import ExitStack

import concourse.bass as bass
import concourse.mybir as mybir
from concourse.bass_utils import run_bass_kernel_spmd

F32 = mybir.dt.float32
BF16 = mybir.dt.bfloat16
AF = mybir.ActivationFunctionType
ALU = mybir.AluOpType
AX = mybir.AxisListType.X
NPBF = ml_dtypes.bfloat16

N, D, H, HD = 8192, 2048, 16, 128
NCORE, NSLOT = 8, 8
BS, NB, SEL, WIN = 64, 128, 8, 64
INTER = 5632
NIT = INTER // 128
NDT = D // 128
CH = 512
TOK = NSLOT * 128

ENGS = ("tensor", "vector", "scalar", "sync", "gpsimd")


def nchunks(k):
    return 2 * (k + 1)


def nswa(k):
    return min(3, nchunks(k))


class _Sem:
    def __init__(self, h):
        self.h = h
        self.n = 0


class _Res:
    __slots__ = ("w", "readers")

    def __init__(self):
        self.w = None
        self.readers = []


class _Op:
    __slots__ = ("fn", "waits", "incs", "tag")

    def __init__(self, fn, waits, incs, tag=None):
        self.fn, self.waits, self.incs, self.tag = fn, waits, incs, tag


class Prog:

    def __init__(self, nc, ctx):
        self.nc = nc
        self.ops = {e: [] for e in ENGS}
        self.sem = {e: _Sem(ctx.enter_context(nc.semaphore(f"s_{e}"))) for e in ENGS}

    def emit(self, eng, fn, reads=(), writes=(), extra_waits=(), step=1, tag=None):
        s = self.sem[eng]
        want = {}

        def add(c):
            if c is None:
                return
            sm, v = c
            k = id(sm)
            if k not in want or want[k][1] < v:
                want[k] = (sm, v)

        for r in reads:
            add(r.w)
        for w in writes:
            add(w.w)
            for c in w.readers:
                add(c)
        for ew in extra_waits:
            add(ew)
        wl = []
        for sm, v in want.values():
            if sm is s and eng == "tensor":
                continue
            wl.append((sm, v))
        if eng in ("vector", "scalar") and s.n > 0:
            wl.append((s, s.n))
        best = {}
        for sm, v in wl:
            if id(sm) not in best or best[id(sm)][1] < v:
                best[id(sm)] = (sm, v)
        s.n += step
        self.ops[eng].append(_Op(fn, list(best.values()), [(s, step)], tag))
        comp = (s, s.n)
        for w in writes:
            w.w = comp
            w.readers = []
        for r in reads:
            r.readers.append(comp)
        return comp

    def wait_all(self, eng):
        s = self.sem[eng]
        self.ops[eng].append(_Op(None, [(s, s.n)], []))

    def run(self, block):
        def mk(name):
            def body(engine):
                for op in self.ops[name]:
                    for sm, v in op.waits:
                        engine.wait_ge(sm.h, v)
                    if op.fn is not None:
                        ins = op.fn(engine)
                        for sm, k in op.incs:
                            ins.then_inc(sm.h, k)
            return body

        block.tensor(mk("tensor"))
        block.vector(mk("vector"))
        block.scalar(mk("scalar"))
        block.sync(mk("sync"))
        block.gpsimd(mk("gpsimd"))


def _rope_tables():
    half = HD // 2
    inv = 1.0 / 10000.0 ** (np.arange(half, dtype=np.float64) / half)
    th = np.arange(N, dtype=np.float64)[:, None] * inv[None, :]
    return np.cos(th).T.astype(np.float32), np.sin(th).T.astype(np.float32)


def _perm_blocks(a):
    v = a.reshape(a.shape[:-1] + (8, 8, 2))
    v = np.swapaxes(v, -3, -2)
    return np.ascontiguousarray(v.reshape(a.shape))


def _core_inputs(c, full, cos, sin):
    x = np.asarray(full["L0_attn_x"], np.float32)[0]
    tiles = [8 * k + c for k in range(NSLOT)]
    x_loc = np.stack([x[t * 128:(t + 1) * 128] for t in tiles])
    xT_loc = np.ascontiguousarray(
        np.concatenate([x[t * 128:(t + 1) * 128] for t in tiles], 0).T)
    scale = np.float32(HD ** -0.5)
    cols = np.concatenate([np.arange(t * 128, (t + 1) * 128) for t in tiles])
    rope = np.concatenate(
        [cos[:, cols] * scale, sin[:, cols] * scale, cos[:, cols], sin[:, cols]],
        axis=1).astype(NPBF)
    rope = np.vstack([rope, rope])

    bmask = np.zeros((NSLOT, 128, 1024), np.float32)
    swam = np.zeros((NSLOT, 128, 1536), np.float32)
    vis = np.zeros((NSLOT, 128, NB), np.float32)
    cur2 = np.zeros((NSLOT, 128, NB), np.float32)
    notany = np.zeros((128, NSLOT), np.float32)
    any01 = np.zeros((128, NSLOT), np.float32)
    r = np.arange(128)
    mm = np.arange(NB)
    for k in range(NSLOT):
        t = 8 * k + c
        T0 = t * 128
        rows = T0 + r
        scan = 1024 * (k + 1)
        j = np.arange(scan - 1024, scan)
        bmask[k] = (j[None, :] <= rows[:, None]) & (j[None, :] < T0 + 128)
        w = 512 * nswa(k)
        j2 = np.arange(scan - w, scan)
        m = (j2[None, :] <= rows[:, None]) & (rows[:, None] - j2[None, :] <= WIN)
        swam[k, :, 1536 - w:] = m
        vis[k] = rows[:, None] >= ((mm + 1) * 64 - 1)[None, :]
        cur2[k, r, rows // 64] = 2.0
        na = ~(vis[k].any(1))
        notany[:, k] = na
        any01[:, k] = ~na
    gs = np.asarray(full["L0_ffn_gate_s"], np.float32)
    us = np.asarray(full["L0_ffn_up_s"], np.float32)
    ds = np.asarray(full["L0_ffn_down_s"], np.float32)
    return {
        "x_loc": x_loc,
        "xT_loc": xT_loc,
        "rope": rope,
        "bmask": bmask.astype(NPBF),
        "swam": swam.astype(NPBF),
        "vis": _perm_blocks(vis).astype(NPBF),
        "cur2": _perm_blocks(cur2).astype(NPBF),
        "notany": notany,
        "any01": any01,
        "Wq": np.asarray(full["L0_attn_Wq"], np.float32),
        "Wk": np.asarray(full["L0_attn_Wk"], np.float32),
        "Wv": np.asarray(full["L0_attn_Wv"], np.float32),
        "Wg": np.asarray(full["L0_attn_Wg"], np.float32),
        "Wo": np.asarray(full["L0_attn_Wo"], np.float32),
        "gwT": np.ascontiguousarray(np.asarray(full["L0_ffn_gate_w"], np.float32).T),
        "uwT": np.ascontiguousarray(np.asarray(full["L0_ffn_up_w"], np.float32).T),
        "dwT": np.ascontiguousarray(np.asarray(full["L0_ffn_down_w"], np.float32).T),
        "gs_pt": np.ascontiguousarray(gs.reshape(NIT, 128).T),
        "us_pt": np.ascontiguousarray(us.reshape(NIT, 128).T),
        "ds_pt": np.ascontiguousarray(ds.reshape(NDT, 128).T),
    }


def build(debug_taps=()):
    nc = bass.Bass("TRN2", target_bir_lowering=False)
    RG = [list(range(NCORE))]

    def din(name, shape, dt=F32):
        return nc.dram_tensor(name, shape, dt, kind="ExternalInput").ap()

    x_loc = din("x_loc", [NSLOT, 128, D])
    xT_loc = din("xT_loc", [D, TOK])
    rope_t = din("rope", [128, 4096], BF16)
    bmask_i = din("bmask", [NSLOT, 128, 1024], BF16)
    swam_i = din("swam", [NSLOT, 128, 1536], BF16)
    vis_i = din("vis", [NSLOT, 128, NB], BF16)
    cur2_i = din("cur2", [NSLOT, 128, NB], BF16)
    notany_i = din("notany", [128, NSLOT])
    any01_i = din("any01", [128, NSLOT])
    Wq = din("Wq", [D, D]); Wk = din("Wk", [D, D]); Wv = din("Wv", [D, D])
    Wg = din("Wg", [D, 3 * H]); Wo = din("Wo", [D, D])
    gwT = din("gwT", [D, INTER]); uwT = din("uwT", [D, INTER])
    dwT = din("dwT", [INTER, D])
    gs_pt = din("gs_pt", [128, NIT]); us_pt = din("us_pt", [128, NIT])
    ds_pt = din("ds_pt", [128, NDT])
    outT = nc.dram_tensor("outT", [D, TOK], F32, kind="ExternalOutput").ap()

    KT_contrib = nc.dram_tensor("KT_contrib", [H * 128, TOK], BF16, kind="Internal").ap()
    V_contrib = nc.dram_tensor("V_contrib", [H * NSLOT * 128, 128], BF16, kind="Internal").ap()
    KC_contrib = nc.dram_tensor("KC_contrib", [16, D], BF16, kind="Internal").ap()
    VC_contrib = nc.dram_tensor("VC_contrib", [16, D], BF16, kind="Internal").ap()
    KT_gath = nc.dram_tensor("KT_gath", [NCORE * H * 128, TOK], BF16, kind="Internal").ap()
    V_gath = nc.dram_tensor("V_gath", [NCORE * H * NSLOT * 128, 128], BF16, kind="Internal").ap()
    KC_gath = nc.dram_tensor("KC_gath", [NCORE * 16, D], BF16, kind="Internal").ap()
    VC_gath = nc.dram_tensor("VC_gath", [NCORE * 16, D], BF16, kind="Internal").ap()
    KT_lin = nc.dram_tensor("KT_lin", [H * 128, N], BF16, kind="Internal").ap()
    V_lin = nc.dram_tensor("V_lin", [H * 128, N], BF16, kind="Internal").ap()
    xnt_spill = nc.dram_tensor("xnt_spill", [128, NDT * TOK], BF16, kind="Internal").ap()
    part_scr = nc.dram_tensor("part_scr", [D, TOK], F32, kind="Internal").ap()

    ident_h = nc.inline_tensor(np.eye(128, dtype=NPBF), "identb").ap()
    taps = {}

    with ExitStack() as ctx:
        en = ctx.enter_context

        def sb(name, shape, dt):
            return en(nc.sbuf_tensor(name, shape, dt))

        A_xnt = sb("A_xnt", [128, NDT * TOK], BF16)
        A_qt = sb("A_qt", [128, H * TOK], BF16)
        A_kv = sb("A_kv", [128, H * TOK], BF16)
        A_w = sb("A_w", [128, 16384], BF16)
        ident = sb("identS", [128, 128], BF16)
        vcmp = sb("vcmpS", [128, H * 128], BF16)
        kcmpT = sb("kcmpTS", [128, H * 128], BF16)
        kc16 = sb("kc16", [128, H * 16], BF16)
        kcstage = sb("kcstage", [16, D], BF16)
        vcstage = sb("vcstage", [2, D], BF16)
        ropes = sb("ropes", [128, 4096], BF16)
        rtmp = sb("rtmp", [128, 2 * CH], BF16)
        xnnat = sb("xnnat", [128, D], BF16)
        ecmp2 = sb("ecmpS", [128, 256], BF16)
        evis2 = sb("evisS", [128, 256], BF16)
        ecT = sb("ecTS", [128, 128], BF16)
        impS2 = sb("impS", [128, 256], BF16)
        mselS2 = sb("mselS", [128, 256], BF16)
        ogS = sb("ogS", [128, 128], BF16)
        visS = sb("visS", [128, NSLOT * NB], BF16)
        cur2S = sb("cur2S", [128, NSLOT * NB], BF16)
        sWg = sb("sWgS", [128, NDT * 48], BF16)
        sg_t = sb("sg_t", [128, 2 * CH], BF16)
        xstage = sb("xstage", [128, D], F32)
        F32A = sb("F32A", [128, D], F32)
        sqt = F32A[:]
        f32scr = sb("f32scr", [128, 1024], F32)
        lnst = sb("lnst", [128, 8], F32)
        gcol = sb("gcolS", [128, NSLOT * 48], F32)
        Zparts = sb("ZpartsS", [128, 64], F32)
        ZswaS = sb("ZswaS", [128, 8], F32)
        Zm = sb("ZmS", [128, 16], F32)
        top82 = sb("top8S", [128, 16], F32)
        gsS = sb("gsS", [128, NIT], F32)
        usS = sb("usS", [128, NIT], F32)
        dsS = sb("dsS", [128, NDT], F32)
        naS = sb("naS", [128, NSLOT], F32)
        anS = sb("anS", [128, NSLOT], F32)
        pstage = F32A[:, 0:2 * CH]
        ostage = F32A[:, 2 * CH:4 * CH]
        pb = [en(nc.psum_tensor(f"pb{i}", [128, CH], F32)) for i in range(8)]

        xnT = A_xnt
        KT_loc = A_qt
        V_loc = A_kv
        QT = A_qt
        KTh = A_kv[:, 0:8192]
        Vh = A_kv[:, 8192:16384]
        bmaskS = lambda k: A_xnt[:, k * 1024:(k + 1) * 1024]
        Pring = lambda i: A_xnt[:, 8192 + i * 512: 8192 + (i + 1) * 512]
        PmS = lambda i: A_xnt[:, 11264 + i * 512: 11264 + (i + 1) * 512]
        PswS = lambda i: A_xnt[:, 12288 + i * 512: 12288 + (i + 1) * 512]
        PTsS = lambda i: A_xnt[:, 13312 + i * 512: 13312 + (i + 1) * 512]
        swamS = lambda k: A_w[:, k * 1536:(k + 1) * 1536]
        sWoS = lambda p: A_w[:, 12288 + p * 2048: 12288 + (p + 1) * 2048]
        sGW = lambda p: A_kv[:, 6144 + p * 2048: 6144 + (p + 1) * 2048]
        sUW = lambda p: A_w[:, p * 2048:(p + 1) * 2048]
        sDW2 = lambda q: A_w[:, 8192 + q * 2816: 8192 + (q + 1) * 2816]
        sKW = lambda p: A_w[:, p * 2048:(p + 1) * 2048]
        sVW = lambda q: A_w[:, q * 8192:(q + 1) * 8192]

        def hdnT(it):
            if it < 32:
                return A_qt[:, it * 512:(it + 1) * 512]
            return A_kv[:, (it - 32) * 512:(it - 31) * 512]

        kcnat = A_kv[:, 0:H * 128]

        rb = [_Res() for _ in range(8)]
        rPring = [_Res() for _ in range(6)]
        rPm = [_Res() for _ in range(4)]
        rPsw = [_Res() for _ in range(2)]
        rPTs = [_Res() for _ in range(2)]
        rQT = [[_Res() for _ in range(NSLOT)] for _ in range(H)]
        rKTloc = [_Res() for _ in range(H)]
        rVloc = _Res()
        rKTh, rVh = _Res(), _Res()
        rxnT = _Res()
        rbm, rswm = _Res(), _Res()
        rsWo = [_Res(), _Res()]
        rsGW = [_Res() for _ in range(4)]
        rsUW = [_Res() for _ in range(4)]
        rsDW = [_Res(), _Res()]
        rsKW = [_Res(), _Res()]
        rsVW = [_Res(), _Res()]
        rxstage, rsqt, rlnst, rxnnat = _Res(), _Res(), _Res(), _Res()
        rropes, rident = _Res(), _Res()
        rrtmp = [_Res(), _Res()]
        rvcmp, rkcnat, rkcmpT, rkc16 = _Res(), _Res(), _Res(), _Res()
        rkcstage, rvcstage = _Res(), _Res()
        recmp, revis, recT, rimp, rog = (_Res() for _ in range(5))
        rmsel = [_Res(), _Res()]
        rvisS, rcur2S, rgcol, rsWg, rsgt = (_Res() for _ in range(5))
        rtop8 = _Res()
        rgs, rus, rds, rna, ran = (_Res() for _ in range(5))
        rpst = [_Res(), _Res()]
        rost = [_Res(), _Res()]
        rKTc, rVc, rKCc, rVCc = (_Res() for _ in range(4))
        rKTg, rVg, rKCg, rVCg = (_Res() for _ in range(4))
        rKTl, rVl = _Res(), _Res()
        rspill, rpart = _Res(), _Res()
        rhdnT = _Res()
        QT_all = [r for hh in rQT for r in hh]

        P = Prog(nc, ctx)

        def emit_rope(dst, co, si, wr, par, extra_reads=()):
            rt = rtmp[:, par * CH:(par + 1) * CH]
            rr = rrtmp[par]
            P.emit("vector", lambda v, dst=dst, co=co, rt=rt: v.tensor_mul(
                dst[0:64, :], rt[0:64, :], ropes[0:64, co:co + 512]),
                reads=[rr, rropes] + list(extra_reads), writes=wr)
            P.emit("vector", lambda v, si=si, rt=rt: v.tensor_mul(
                f32scr[0:64, 0:512], rt[64:128, :], ropes[64:128, si:si + 512]),
                reads=[rr, rropes])
            P.emit("vector", lambda v, dst=dst: v.tensor_sub(
                dst[0:64, :], dst[0:64, :], f32scr[0:64, 0:512]),
                writes=wr)
            P.emit("vector", lambda v, dst=dst, si=si, rt=rt: v.tensor_mul(
                dst[64:128, :], rt[0:64, :], ropes[0:64, si:si + 512]),
                reads=[rr, rropes], writes=wr)
            P.emit("vector", lambda v, co=co, rt=rt: v.tensor_mul(
                f32scr[64:128, 0:512], rt[64:128, :], ropes[64:128, co:co + 512]),
                reads=[rr, rropes])
            P.emit("vector", lambda v, dst=dst: v.tensor_add(
                dst[64:128, :], dst[64:128, :], f32scr[64:128, 0:512]),
                writes=wr)

        def tap(name, ap, reads, dt=BF16):
            if name not in debug_taps:
                return
            t = nc.dram_tensor("tap_" + name, list(ap.shape), dt,
                               kind="ExternalOutput").ap()
            taps[name] = True
            P.emit("gpsimd", lambda g, t=t, ap=ap: g.dma_start(out=t, in_=ap),
                   reads=list(reads), step=16)

        P.emit("sync", lambda s: s.dma_start(out=ident[:], in_=ident_h),
               writes=[rident], step=16)
        P.emit("sync", lambda s: s.dma_start(out=ropes[:], in_=rope_t),
               writes=[rropes], step=16)
        P.emit("sync", lambda s: s.dma_start(out=visS[:].rearrange("p (k b) -> p k b", k=8), in_=vis_i.rearrange("k p b -> p k b")),
               writes=[rvisS], step=16)
        P.emit("sync", lambda s: s.dma_start(out=cur2S[:].rearrange("p (k b) -> p k b", k=8), in_=cur2_i.rearrange("k p b -> p k b")),
               writes=[rcur2S], step=16)
        P.emit("sync", lambda s: s.dma_start(out=naS[:], in_=notany_i), writes=[rna], step=16)
        P.emit("sync", lambda s: s.dma_start(out=anS[:], in_=any01_i), writes=[ran], step=16)
        P.emit("sync", lambda s: s.dma_start(out=gsS[:], in_=gs_pt), writes=[rgs], step=16)
        P.emit("sync", lambda s: s.dma_start(out=usS[:], in_=us_pt), writes=[rus], step=16)
        P.emit("sync", lambda s: s.dma_start(out=dsS[:], in_=ds_pt), writes=[rds], step=16)
        P.emit("gpsimd", lambda g: g.dma_start(
            out=sWg[:].rearrange("p (t e) -> p t e", t=16), in_=Wg.rearrange("(t p) e -> p t e", p=128)),
            writes=[rsWg], step=16)

        inv_d = float(1.0 / D)
        for s in range(NSLOT):
            P.emit("sync", lambda sy, s=s: sy.dma_start(out=xstage[:], in_=x_loc[s]),
                   writes=[rxstage], step=16)
            P.emit("vector", lambda v: v.reduce_sum(lnst[:, 0:1], xstage[:], AX),
                   reads=[rxstage], writes=[rlnst])
            P.emit("vector", lambda v: v.tensor_scalar_mul(lnst[:, 1:2], lnst[:, 0:1], -inv_d),
                   writes=[rlnst])
            P.emit("scalar", lambda sc: sc.activation(sqt[:], xstage[:], AF.Square,
                                                      bias=lnst[:, 1:2]),
                   reads=[rxstage, rlnst], writes=[rsqt])
            P.emit("vector", lambda v: v.reduce_sum(lnst[:, 2:3], sqt[:], AX),
                   reads=[rsqt], writes=[rlnst])
            P.emit("vector", lambda v: v.tensor_scalar(
                lnst[:, 2:3], lnst[:, 2:3], inv_d, 1e-5, ALU.mult, ALU.add),
                writes=[rlnst])
            P.emit("scalar", lambda sc: sc.activation(lnst[:, 3:4], lnst[:, 2:3], AF.Sqrt),
                   reads=[rlnst], writes=[rlnst])
            P.emit("vector", lambda v: v.reciprocal(lnst[:, 4:5], lnst[:, 3:4]),
                   reads=[rlnst], writes=[rlnst])
            P.emit("vector", lambda v: v.tensor_mul(lnst[:, 5:6], lnst[:, 1:2], lnst[:, 4:5]),
                   writes=[rlnst])
            P.emit("vector", lambda v: v.tensor_scalar(
                xnnat[:], xstage[:], lnst[:, 4:5], lnst[:, 5:6], ALU.mult, ALU.add),
                reads=[rxstage, rlnst], writes=[rxnnat])
            for g4 in range(4):
                bank = rb[g4 % 2]
                P.emit("tensor", lambda t, g4=g4: [
                    t.matmul(pb[g4 % 2][:, i * 128:(i + 1) * 128],
                             xnnat[:, (g4 * 4 + i) * 128:(g4 * 4 + i + 1) * 128],
                             ident[:], start=(i == 0), stop=(i == 3))
                    for i in range(4)][-1],
                    reads=[rxnnat, rident], writes=[bank])
                P.emit("scalar", lambda sc, s=s, g4=g4: sc.copy(
                    xnT[:].rearrange("p (t c) -> p t c", t=NDT)[
                        :, g4 * 4:(g4 + 1) * 4, s * 128:(s + 1) * 128],
                    pb[g4 % 2][:].rearrange("p (t c) -> p t c", t=4)),
                    reads=[bank], writes=[rxnT])
        tap("xnT", xnT[:, 0:4096], [rxnT])

        for h in range(H):
            P.emit("gpsimd", lambda g, h=h: g.dma_start(
                out=sKW(h % 2).rearrange("p (t e) -> p t e", t=16),
                in_=Wk[:, h * 128:(h + 1) * 128].rearrange("(t p) e -> p t e", p=128)),
                writes=[rsKW[h % 2]], step=16)
            for tc in range(2):
                bank = rb[(h * 2 + tc) % 2]
                P.emit("tensor", lambda t, h=h, tc=tc: [
                    t.matmul(pb[(h * 2 + tc) % 2][:],
                             sKW(h % 2)[:, dt * 128:(dt + 1) * 128],
                             xnT[:, dt * 1024 + tc * 512: dt * 1024 + (tc + 1) * 512],
                             start=(dt == 0), stop=(dt == NDT - 1))
                    for dt in range(NDT)][-1],
                    reads=[rsKW[h % 2], rxnT], writes=[bank])
                par = (h * 2 + tc) % 2
                P.emit("scalar", lambda sc, h=h, tc=tc, par=par: sc.copy(
                    rtmp[:, par * CH:(par + 1) * CH], pb[par][:]),
                    reads=[bank], writes=[rrtmp[par]])
                co, si = 2048 + tc * 512, 3072 + tc * 512
                dst = KT_loc[:, h * 1024 + tc * 512: h * 1024 + (tc + 1) * 512]
                emit_rope(dst, co, si, [rKTloc[h]], par)
            src = KT_loc[:, h * 1024:(h + 1) * 1024]
            P.emit("vector", lambda v, src=src: v.tensor_add(
                f32scr[:, 0:512], src[:, 0:512], src[:, 512:1024]),
                reads=[rKTloc[h]])
            for w in (256, 128, 64, 32, 16):
                P.emit("vector", lambda v, w=w: v.tensor_add(
                    f32scr[:, 0:w], f32scr[:, 0:w], f32scr[:, w:2 * w]))
            P.emit("vector", lambda v, h=h: v.tensor_scalar_mul(
                kc16[:, h * 16:(h + 1) * 16], f32scr[:, 0:16], 1.0 / 64),
                writes=[rkc16])
            P.emit("gpsimd", lambda g, h=h: g.dma_start(
                out=KT_contrib[h * 128:(h + 1) * 128, :],
                in_=KT_loc[:, h * 1024:(h + 1) * 1024]),
                reads=[rKTloc[h]], writes=[rKTc], step=16)
        for h in range(H):
            P.emit("tensor", lambda t, h=h: t.matmul(
                pb[2][0:16, 0:128], kc16[:, h * 16:(h + 1) * 16], ident[:],
                start=True, stop=True),
                reads=[rkc16, rident], writes=[rb[2]])
            P.emit("scalar", lambda sc, h=h: sc.copy(
                kcstage[:, h * 128:(h + 1) * 128], pb[2][0:16, 0:128]),
                reads=[rb[2]], writes=[rkcstage])
        P.emit("gpsimd", lambda g: g.dma_start(out=KC_contrib, in_=kcstage[:]),
               reads=[rkcstage], writes=[rKCc], step=16)

        pool_np = np.zeros((128, 2), dtype=NPBF)
        pool_np[:64, 0] = NPBF(1.0 / 64)
        pool_np[64:, 1] = NPBF(1.0 / 64)
        pool_h = nc.inline_tensor(pool_np, "pool64").ap()
        pool64 = sb("pool64S", [128, 2], BF16)
        rpool = _Res()
        P.emit("sync", lambda s: s.dma_start(out=pool64[:], in_=pool_h),
               writes=[rpool], step=16)
        for hc in range(4):
            q = hc % 2
            P.emit("gpsimd", lambda g, hc=hc, q=q: g.dma_start(
                out=sVW(q).rearrange("p (t e) -> p t e", t=16),
                in_=Wv[:, hc * 512:(hc + 1) * 512].rearrange("(t p) e -> p t e", p=128)),
                writes=[rsVW[q]] + ([rsKW[0], rsKW[1]] if hc < 2 else []), step=16)
            for s in range(NSLOT):
                bank = rb[(s + hc) % 2]
                P.emit("tensor", lambda t, s=s, hc=hc, q=q: [
                    t.matmul(pb[(s + hc) % 2][:],
                             xnT[:, dt * 1024 + s * 128: dt * 1024 + (s + 1) * 128],
                             sVW(q)[:, dt * 512:(dt + 1) * 512],
                             start=(dt == 0), stop=(dt == NDT - 1))
                    for dt in range(NDT)][-1],
                    reads=[rsVW[q], rxnT], writes=[bank])
                P.emit("scalar", lambda sc, s=s, hc=hc: sc.copy(
                    V_loc[:, s * 2048 + hc * 512: s * 2048 + (hc + 1) * 512],
                    pb[(s + hc) % 2][:]),
                    reads=[bank], writes=[rVloc])
                P.emit("tensor", lambda t, s=s, hc=hc: t.matmul(
                    pb[3][0:2, :], pool64[:],
                    V_loc[:, s * 2048 + hc * 512: s * 2048 + (hc + 1) * 512],
                    start=True, stop=True),
                    reads=[rpool, rVloc], writes=[rb[3]])
                P.emit("scalar", lambda sc: sc.copy(vcstage[:, 0:512], pb[3][0:2, :]),
                       reads=[rb[3]], writes=[rvcstage])
                P.emit("gpsimd", lambda g, s=s, hc=hc: g.dma_start(
                    out=VC_contrib[s * 2:(s + 1) * 2, hc * 512:(hc + 1) * 512],
                    in_=vcstage[0:2, 0:512]),
                    reads=[rvcstage], writes=[rVCc], step=16)
        for h in range(H):
            P.emit("gpsimd", lambda g, h=h: g.dma_start(
                out=V_contrib[h * 1024:(h + 1) * 1024, :].rearrange(
                    "(s p) d -> p s d", p=128),
                in_=V_loc[:].rearrange("p (s h c) -> p s h c", s=8, h=H)[:, :, h, :]),
                reads=[rVloc], writes=[rVc], step=16)

        for h in range(H):
            P.emit("gpsimd", lambda g, h=h: g.dma_start(
                out=sKW(h % 2).rearrange("p (t e) -> p t e", t=16),
                in_=Wq[:, h * 128:(h + 1) * 128].rearrange("(t p) e -> p t e", p=128)),
                writes=[rsKW[h % 2]] + ([rsVW[0]] if h < 2 else []), step=16)
            for tc in range(2):
                bank = rb[(h * 2 + tc) % 2]
                P.emit("tensor", lambda t, h=h, tc=tc: [
                    t.matmul(pb[(h * 2 + tc) % 2][:],
                             sKW(h % 2)[:, dt * 128:(dt + 1) * 128],
                             xnT[:, dt * 1024 + tc * 512: dt * 1024 + (tc + 1) * 512],
                             start=(dt == 0), stop=(dt == NDT - 1))
                    for dt in range(NDT)][-1],
                    reads=[rsKW[h % 2], rxnT], writes=[bank])
                par = (h * 2 + tc) % 2
                P.emit("scalar", lambda sc, h=h, tc=tc, par=par: sc.copy(
                    rtmp[:, par * CH:(par + 1) * CH], pb[par][:]),
                    reads=[bank], writes=[rrtmp[par]])
                co, si = tc * 512, 1024 + tc * 512
                dst = QT[:, h * 1024 + tc * 512: h * 1024 + (tc + 1) * 512]
                wr = [rQT[h][4 * tc + i] for i in range(4)]
                emit_rope(dst, co, si, wr, par, extra_reads=[rKTc])
        for s in range(NSLOT):
            P.emit("tensor", lambda t, s=s: [
                t.matmul(pb[6][:, 0:48],
                         xnT[:, dt * 1024 + s * 128: dt * 1024 + (s + 1) * 128],
                         sWg[:, dt * 48:(dt + 1) * 48],
                         start=(dt == 0), stop=(dt == NDT - 1))
                for dt in range(NDT)][-1],
                reads=[rsWg, rxnT], writes=[rb[6]])
            P.emit("scalar", lambda sc, s=s: sc.activation(
                gcol[:, s * 48:(s + 1) * 48], pb[6][:, 0:48], AF.Sigmoid),
                reads=[rb[6]], writes=[rgcol])
        tap("QT", QT[:, 0:4096], [rQT[0][k] for k in range(8)] + [rQT[1][k] for k in range(8)])

        for cin, cout, rin, rout in (
            (KC_contrib, KC_gath, rKCc, rKCg),
            (VC_contrib, VC_gath, rVCc, rVCg),
            (KT_contrib, KT_gath, rKTc, rKTg),
            (V_contrib, V_gath, rVc, rVg),
        ):
            P.emit("gpsimd", lambda g, cin=cin, cout=cout: g.collective_compute(
                "AllGather", ALU.bypass, replica_groups=RG,
                ins=[cin.opt()], outs=[cout.opt()]),
                reads=[rin], writes=[rout], step=1)

        for h in range(H):
            for r in range(NCORE):
                qeng = "sync" if (r % 2 == 0) else "gpsimd"
                P.emit(qeng, lambda q, h=h, r=r: q.dma_start(
                    out=KT_lin[h * 128:(h + 1) * 128, :]
                        .rearrange("p (k r c) -> p k r c", r=8, c=128)[:, :, r, :],
                    in_=KT_gath[r * 2048 + h * 128: r * 2048 + (h + 1) * 128, :]
                        .rearrange("p (k c) -> p k c", k=8)),
                    reads=[rKTg], writes=[rKTl], step=16)
                P.emit(qeng, lambda q, h=h, r=r: q.dma_start(
                    out=V_lin[h * 128:(h + 1) * 128, :]
                        .rearrange("p (k r c) -> p k r c", r=8, c=128)[:, :, r, :],
                    in_=V_gath[r * 16384 + h * 1024: r * 16384 + (h + 1) * 1024, :]
                        .rearrange("(s p) d -> p s d", p=128)),
                    reads=[rVg], writes=[rVl], step=16)

        P.emit("gpsimd", lambda g: g.dma_start(out=xnt_spill, in_=xnT[:]),
               reads=[rxnT], writes=[rspill], step=16)
        P.emit("sync", lambda s: s.dma_start(
            out=A_xnt[:, 0:8192].rearrange("p (k b) -> p k b", k=8), in_=bmask_i.rearrange("k p b -> p k b")),
            writes=[rbm, rxnT], step=16)
        P.emit("sync", lambda s: s.dma_start(
            out=A_w[:, 0:12288].rearrange("p (k b) -> p k b", k=8), in_=swam_i.rearrange("k p b -> p k b")),
            writes=[rswm, rsKW[0], rsKW[1], rsVW[0], rsVW[1]], step=16)
        P.emit("sync", lambda s: s.dma_start(out=vcmp[:], in_=VC_gath),
               reads=[rVCg], writes=[rvcmp], step=16)
        P.emit("sync", lambda s: s.dma_start(out=kcnat[:], in_=KC_gath),
               reads=[rKCg], writes=[rkcnat], step=16)
        for h in range(H):
            P.emit("tensor", lambda t, h=h: t.matmul(
                pb[7][:, 0:128], kcnat[:, h * 128:(h + 1) * 128], ident[:],
                start=True, stop=True),
                reads=[rkcnat, rident], writes=[rb[7]])
            P.emit("scalar", lambda sc, h=h: sc.copy(
                kcmpT[:, h * 128:(h + 1) * 128], pb[7][:, 0:128]),
                reads=[rb[7]], writes=[rkcmpT])

        ecmp = lambda k: ecmp2[:, (k % 2) * 128:(k % 2 + 1) * 128]
        evis = lambda k: evis2[:, (k % 2) * 128:(k % 2 + 1) * 128]
        impS = lambda k: impS2[:, (k % 2) * 128:(k % 2 + 1) * 128]
        mselS = lambda k: mselS2[:, (k % 2) * 128:(k % 2 + 1) * 128]
        top8 = lambda k: top82[:, (k % 2) * 8:(k % 2) * 8 + 8]
        PmR = lambda i: (A_xnt[:, 11264 + i * 512: 11264 + (i + 1) * 512] if i < 2
                         else A_xnt[:, 14336 + (i - 2) * 512: 14336 + (i - 1) * 512])
        PMR = 4

        def mselexp(k, jc, jj):
            mv = mselS(k).rearrange("p (r sj) -> p r sj", r=8)
            mv = mv[:, 4 * (jc % 2):4 * (jc % 2) + 4, 2 * (jc // 2) + jj]
            return mv.to_broadcast([128, 4, 64])

        def pview(ap, jj):
            return ap.rearrange("p (r j w) -> p r j w", r=4, j=2)[:, :, jj, :]

        for h in range(H):
            kvw = [rVloc, rkcnat] if h == 0 else []
            P.emit("sync", lambda s, h=h: s.dma_start(
                out=KTh, in_=KT_lin[h * 128:(h + 1) * 128, :]),
                reads=[rKTl], writes=[rKTh] + kvw, step=16)
            P.emit("sync", lambda s, h=h: s.dma_start(
                out=Vh, in_=V_lin[h * 128:(h + 1) * 128, :]),
                reads=[rVl], writes=[rVh], step=16)

            def emit_sel(h, k):
                zm = (k % 2) * 8
                P.emit("tensor", lambda t, h=h, k=k: t.matmul(
                    pb[6][:, 0:128], QT[:, h * 1024 + k * 128: h * 1024 + (k + 1) * 128],
                    kcmpT[:, h * 128:(h + 1) * 128], start=True, stop=True),
                    reads=[rQT[h][k], rkcmpT], writes=[rb[6]])
                P.emit("scalar", lambda sc, k=k: sc.activation(
                    ecmp(k), pb[6][:, 0:128], AF.Exp),
                    reads=[rb[6]], writes=[recmp])
                P.emit("vector", lambda v, k=k, zm=zm: v.scalar_tensor_tensor(
                    out=evis(k), in0=ecmp(k), scalar=1.0,
                    in1=visS[:, k * NB:(k + 1) * NB], op0=ALU.mult, op1=ALU.mult,
                    accum_out=Zm[:, zm:zm + 1]),
                    reads=[recmp, rvisS], writes=[revis])
                P.emit("vector", lambda v, k=k, zm=zm: v.scalar_tensor_tensor(
                    out=impS(k), in0=cur2S[:, k * NB:(k + 1) * NB],
                    scalar=Zm[:, zm:zm + 1],
                    in1=evis(k), op0=ALU.mult, op1=ALU.add),
                    reads=[rcur2S, revis], writes=[rimp])
                P.emit("vector", lambda v, k=k: v.max(out=top8(k), in_=impS(k)),
                       reads=[rimp], writes=[rtop8])
                P.emit("vector", lambda v, k=k: v.tensor_single_scalar(
                    mselS(k), impS(k), top8(k)[:, 7:8], ALU.is_ge),
                    reads=[rtop8, rimp], writes=[rmsel[k % 2]])

            def emit_S(h, k, j):
                sbk = rb[j % 2]
                P.emit("tensor", lambda t, h=h, k=k, j=j: t.matmul(
                    pb[j % 2][:],
                    QT[:, h * 1024 + k * 128: h * 1024 + (k + 1) * 128],
                    KTh[:, j * 512:(j + 1) * 512], start=True, stop=True),
                    reads=[rQT[h][k], rKTh], writes=[sbk])
                P.emit("scalar", lambda sc, j=j: sc.activation(
                    Pring(j % 6), pb[j % 2][:], AF.Exp),
                    reads=[sbk], writes=[rPring[j % 6]])

            def emit_mask(h, k, j):
                nch = nchunks(k)
                nback = nch - j
                zb = (k % 2) * 32
                zsb = (k % 2) * 4
                if nback > 2:
                    for jj in range(2):
                        P.emit("vector", lambda v, k=k, j=j, jj=jj, zb=zb:
                               v.scalar_tensor_tensor(
                                   out=pview(PmR(j % PMR), jj),
                                   in0=pview(Pring(j % 6), jj),
                                   scalar=1.0, in1=mselexp(k, j, jj),
                                   op0=ALU.mult, op1=ALU.mult,
                                   accum_out=Zparts[:, zb + 2 * j + jj: zb + 2 * j + jj + 1]),
                               reads=[rPring[j % 6], rmsel[k % 2]], writes=[rPm[j % PMR]])
                else:
                    for jj in range(2):
                        P.emit("vector", lambda v, k=k, j=j, jj=jj: v.tensor_tensor(
                            out=pview(PmR(j % PMR), jj),
                            in0=pview(Pring(j % 6), jj),
                            in1=mselexp(k, j, jj), op=ALU.mult),
                            reads=[rPring[j % 6], rmsel[k % 2]], writes=[rPm[j % PMR]])
                    P.emit("vector", lambda v, j=j, zb=zb: v.memset(
                        Zparts[:, zb + 2 * j + 1: zb + 2 * j + 2], 0.0))
                    P.emit("vector", lambda v, j=j, k=k, nback=nback, zb=zb:
                           v.scalar_tensor_tensor(
                               out=PmR(j % PMR), in0=PmR(j % PMR), scalar=1.0,
                               in1=bmaskS(k)[:, 1024 - 512 * nback: 1536 - 512 * nback],
                               op0=ALU.mult, op1=ALU.mult,
                               accum_out=Zparts[:, zb + 2 * j: zb + 2 * j + 1]),
                           reads=[rbm], writes=[rPm[j % PMR]])
                if nback <= nswa(k):
                    jsw = j - (nch - nswa(k))
                    P.emit("vector", lambda v, j=j, k=k, nback=nback, jsw=jsw, zsb=zsb:
                           v.scalar_tensor_tensor(
                               out=PswS(jsw % 2), in0=Pring(j % 6), scalar=1.0,
                               in1=swamS(k)[:, 1536 - 512 * nback: 2048 - 512 * nback],
                               op0=ALU.mult, op1=ALU.mult,
                               accum_out=ZswaS[:, zsb + jsw: zsb + jsw + 1]),
                           reads=[rPring[j % 6], rswm], writes=[rPsw[jsw % 2]])

            def emit_pte(src_ap, rsrc, dst_ap, rdst, eng):
                if eng == "scalar":
                    P.emit("scalar", lambda sc, s=src_ap, d=dst_ap: sc.copy(d, s),
                           reads=[rsrc], writes=[rdst])
                else:
                    P.emit("vector", lambda v, s=src_ap, d=dst_ap: v.tensor_copy(d, s),
                           reads=[rsrc], writes=[rdst])

            def emit_TAV(h, k, j, obank, rob):
                nch = nchunks(k)
                P.emit("tensor", lambda t, j=j: [
                    t.matmul(pb[2 + (j % 2)][:, i * 128:(i + 1) * 128],
                             PmR(j % PMR)[:, i * 128:(i + 1) * 128], ident[:],
                             start=(i == 0), stop=(i == 3))
                    for i in range(4)][-1],
                    reads=[rPm[j % PMR], rident], writes=[rb[2 + (j % 2)]])
                emit_pte(pb[2 + (j % 2)][:], rb[2 + (j % 2)], PTsS(j % 2),
                         rPTs[j % 2], "scalar" if j % 2 else "vector")
                P.emit("tensor", lambda t, j=j, obank=obank: [
                    t.matmul(obank[:, 0:128],
                             PTsS(j % 2)[:, i * 128:(i + 1) * 128],
                             Vh[:, (j * 4 + i) * 128:(j * 4 + i + 1) * 128],
                             start=False, stop=False)
                    for i in range(4)][-1],
                    reads=[rPTs[j % 2], rVh], writes=[rob])
                if nch - j <= nswa(k):
                    jsw = j - (nch - nswa(k))
                    P.emit("tensor", lambda t, jsw=jsw: [
                        t.matmul(pb[2 + (jsw % 2)][:, i * 128:(i + 1) * 128],
                                 PswS(jsw % 2)[:, i * 128:(i + 1) * 128], ident[:],
                                 start=(i == 0), stop=(i == 3))
                        for i in range(4)][-1],
                        reads=[rPsw[jsw % 2], rident], writes=[rb[2 + (jsw % 2)]])
                    emit_pte(pb[2 + (jsw % 2)][:], rb[2 + (jsw % 2)], PTsS(jsw % 2),
                             rPTs[jsw % 2], "scalar" if jsw % 2 else "vector")
                    P.emit("tensor", lambda t, j=j, jsw=jsw, obank=obank: [
                        t.matmul(obank[:, 128:256],
                                 PTsS(jsw % 2)[:, i * 128:(i + 1) * 128],
                                 Vh[:, (j * 4 + i) * 128:(j * 4 + i + 1) * 128],
                                 start=False, stop=False)
                        for i in range(4)][-1],
                        reads=[rPTs[jsw % 2], rVh], writes=[rob])

            def emit_combine(h, k, obank, rob):
                nch = nchunks(k)
                zm = (k % 2) * 8
                zb = (k % 2) * 32
                zsb = (k % 2) * 4
                P.emit("vector", lambda v, nch=nch, zm=zm, zb=zb: v.reduce_sum(
                    Zm[:, zm + 1:zm + 2], Zparts[:, zb:zb + 2 * nch], AX))
                P.emit("vector", lambda v, k=k, zm=zm, zsb=zsb: v.reduce_sum(
                    Zm[:, zm + 2:zm + 3], ZswaS[:, zsb:zsb + nswa(k)], AX))
                P.emit("vector", lambda v, k=k, zm=zm: v.tensor_add(
                    Zm[:, zm + 3:zm + 4], Zm[:, zm:zm + 1], naS[:, k:k + 1]), reads=[rna])
                P.emit("vector", lambda v, zm=zm: v.reciprocal(
                    Zm[:, zm + 4:zm + 7], Zm[:, zm + 1:zm + 4]))
                P.emit("vector", lambda v, h=h, k=k, zm=zm: v.tensor_mul(
                    Zm[:, zm + 1:zm + 2], Zm[:, zm + 4:zm + 5],
                    gcol[:, k * 48 + 16 + h: k * 48 + 17 + h]),
                    reads=[rgcol])
                P.emit("vector", lambda v, h=h, k=k, zm=zm: v.tensor_mul(
                    Zm[:, zm + 2:zm + 3], Zm[:, zm + 5:zm + 6],
                    gcol[:, k * 48 + 32 + h: k * 48 + 33 + h]),
                    reads=[rgcol])
                P.emit("vector", lambda v, h=h, k=k, zm=zm: v.tensor_mul(
                    Zm[:, zm + 3:zm + 4], Zm[:, zm + 6:zm + 7],
                    gcol[:, k * 48 + h: k * 48 + 1 + h]),
                    reads=[rgcol])
                P.emit("vector", lambda v, k=k, zm=zm: v.tensor_mul(
                    Zm[:, zm + 3:zm + 4], Zm[:, zm + 3:zm + 4], anS[:, k:k + 1]),
                    reads=[ran])
                P.emit("vector", lambda v, obank=obank, zm=zm: v.tensor_scalar(
                    f32scr[:, 0:128], obank[:, 0:128], Zm[:, zm + 1:zm + 2], None,
                    ALU.mult),
                    reads=[rob])
                P.emit("vector", lambda v, obank=obank, zm=zm: v.scalar_tensor_tensor(
                    out=f32scr[:, 0:128], in0=obank[:, 128:256],
                    scalar=Zm[:, zm + 2:zm + 3],
                    in1=f32scr[:, 0:128], op0=ALU.mult, op1=ALU.add),
                    reads=[rob])
                P.emit("vector", lambda v, obank=obank, zm=zm: v.scalar_tensor_tensor(
                    out=ogS[:], in0=obank[:, 256:384], scalar=Zm[:, zm + 3:zm + 4],
                    in1=f32scr[:, 0:128], op0=ALU.mult, op1=ALU.add),
                    reads=[rob], writes=[rog])
                P.emit("tensor", lambda t: t.matmul(
                    pb[7][:, 0:128], ogS[:], ident[:], start=True, stop=True),
                    reads=[rog, rident], writes=[rb[7]])
                P.emit("scalar", lambda sc, h=h, k=k: sc.copy(
                    QT[:, h * 1024 + k * 128: h * 1024 + (k + 1) * 128], pb[7][:, 0:128]),
                    reads=[rb[7]], writes=[rQT[h][k]])

            emit_sel(h, 0)
            pend = None
            for k in range(NSLOT):
                nch = nchunks(k)
                obank = pb[4 + (k % 2)]
                rob = rb[4 + (k % 2)]
                emit_S(h, k, 0)
                emit_mask(h, k, 0)
                if pend is not None:
                    pend()
                    pend = None
                P.emit("tensor", lambda t, k=k: t.matmul(
                    pb[7][:, 0:128], evis(k), ident[:], start=True, stop=True),
                    reads=[revis, rident], writes=[rb[7]])
                P.emit("scalar", lambda sc: sc.copy(ecT[:], pb[7][:, 0:128]),
                       reads=[rb[7]], writes=[recT])
                P.emit("tensor", lambda t, h=h, obank=obank: t.matmul(
                    obank[:, 256:384], ecT[:], vcmp[:, h * 128:(h + 1) * 128],
                    start=True, stop=True),
                    reads=[recT, rvcmp], writes=[rob])
                for j in range(1, nch):
                    emit_S(h, k, j)
                    emit_mask(h, k, j)
                    if j == 2 and k < NSLOT - 1:
                        emit_sel(h, k + 1)
                    if j >= 2:
                        emit_TAV(h, k, j - 2, obank, rob)
                if nch == 2 and k < NSLOT - 1:
                    emit_sel(h, k + 1)
                emit_TAV(h, k, nch - 2, obank, rob)

                def mk_pend(h=h, k=k, nch=nch, obank=obank, rob=rob):
                    def f():
                        emit_TAV(h, k, nch - 1, obank, rob)
                        emit_combine(h, k, obank, rob)
                    return f
                pend = mk_pend()
            pend()

        for ht in range(NDT):
            p = ht % 2
            P.emit("gpsimd", lambda g, ht=ht, p=p: g.dma_start(
                out=sWoS(p).rearrange("p (t e) -> p t e", t=16),
                in_=Wo[:, ht * 128:(ht + 1) * 128].rearrange("(t p) e -> p t e", p=128)),
                writes=[rsWo[p]] + ([rsVW[1]] if ht < 2 else []), step=16)
            for tc in range(2):
                bank = rb[tc]
                P.emit("sync", lambda s, ht=ht, tc=tc: s.dma_start(
                    out=pstage[:, tc * 512:(tc + 1) * 512],
                    in_=xT_loc[ht * 128:(ht + 1) * 128, tc * 512:(tc + 1) * 512]),
                    writes=[rpst[tc]] + ([rsqt] if ht == 0 else []), step=16)
                P.emit("tensor", lambda t, ht=ht, tc=tc, p=p: [
                    t.matmul(pb[tc][:],
                             sWoS(p)[:, hh * 128:(hh + 1) * 128],
                             QT[:, hh * 1024 + tc * 512: hh * 1024 + (tc + 1) * 512],
                             start=(hh == 0), stop=(hh == H - 1))
                    for hh in range(H)][-1],
                    reads=[rsWo[p]] + [rQT[hh][kk] for hh in range(H)
                                       for kk in range(4 * tc, 4 * tc + 4)],
                    writes=[bank])
                P.emit("vector", lambda v, tc=tc: v.scalar_tensor_tensor(
                    out=ostage[:, tc * 512:(tc + 1) * 512], in0=pb[tc][:], scalar=1.0,
                    in1=pstage[:, tc * 512:(tc + 1) * 512], op0=ALU.mult, op1=ALU.add),
                    reads=[bank, rpst[tc]], writes=[rost[tc]])
                P.emit("gpsimd", lambda g, ht=ht, tc=tc: g.dma_start(
                    out=part_scr[ht * 128:(ht + 1) * 128, tc * 512:(tc + 1) * 512],
                    in_=ostage[:, tc * 512:(tc + 1) * 512]),
                    reads=[rost[tc]], writes=[rpart], step=16)

        tap("part", part_scr, [rpart], dt=F32)
        P.emit("sync", lambda s: s.dma_start(out=xnT[:], in_=xnt_spill),
               reads=[rspill], writes=[rxnT, rbm] + rPring + rPm + rPsw + rPTs,
               extra_waits=(), step=16)
        tap("xnT2", xnT[:], [rxnT])
        first_hdn = True
        rsg = [_Res(), _Res()]

        def load_gu(g):
            it = g % NIT
            p = g % 4
            extra_g = [rKTh, rVh] if g < 4 else []
            extra_u = [rswm] if g < 4 else []
            P.emit("gpsimd", lambda gq, it=it, p=p: gq.dma_start(
                out=sGW(p).rearrange("p (t e) -> p t e", t=16),
                in_=gwT[:, it * 128:(it + 1) * 128].rearrange("(t p) e -> p t e", p=128)),
                writes=[rsGW[p]] + extra_g, step=16)
            P.emit("gpsimd", lambda gq, it=it, p=p: gq.dma_start(
                out=sUW(p).rearrange("p (t e) -> p t e", t=16),
                in_=uwT[:, it * 128:(it + 1) * 128].rearrange("(t p) e -> p t e", p=128)),
                writes=[rsUW[p]] + extra_u, step=16)

        for g in range(3):
            load_gu(g)
        for tc in range(2):
            for it in range(NIT):
                g = tc * NIT + it
                if g + 3 < 2 * NIT:
                    load_gu(g + 3)
                p = g % 4
                gb = (g % 2) * 2
                ub = (g % 2) * 2 + 1
                P.emit("tensor", lambda t, it=it, tc=tc, p=p, gb=gb: [
                    t.matmul(pb[gb][:],
                             sGW(p)[:, dt * 128:(dt + 1) * 128],
                             xnT[:, dt * 1024 + tc * 512: dt * 1024 + (tc + 1) * 512],
                             start=(dt == 0), stop=(dt == NDT - 1))
                    for dt in range(NDT)][-1],
                    reads=[rsGW[p], rxnT], writes=[rb[gb]])
                P.emit("tensor", lambda t, it=it, tc=tc, p=p, ub=ub: [
                    t.matmul(pb[ub][:],
                             sUW(p)[:, dt * 128:(dt + 1) * 128],
                             xnT[:, dt * 1024 + tc * 512: dt * 1024 + (tc + 1) * 512],
                             start=(dt == 0), stop=(dt == NDT - 1))
                    for dt in range(NDT)][-1],
                    reads=[rsUW[p], rxnT], writes=[rb[ub]])
                P.emit("scalar", lambda sc, it=it, g=g, gb=gb: sc.activation(
                    sg_t[:, (g % 2) * CH:(g % 2 + 1) * CH], pb[gb][:],
                    AF.Silu, scale=gsS[:, it:it + 1]),
                    reads=[rb[gb], rgs], writes=[rsg[g % 2]])
                hw = [rhdnT]
                if first_hdn:
                    hw = [rhdnT, rKTh, rVh] + QT_all
                    first_hdn = False
                P.emit("vector", lambda v, it=it, g=g, ub=ub: v.scalar_tensor_tensor(
                    out=hdnT(it), in0=pb[ub][:], scalar=usS[:, it:it + 1],
                    in1=sg_t[:, (g % 2) * CH:(g % 2 + 1) * CH],
                    op0=ALU.mult, op1=ALU.mult),
                    reads=[rb[ub], rus, rsg[g % 2]], writes=hw)
            if tc == 0:
                tap("hdn0", A_qt[:, 0:16384], [rhdnT])
            for ht in range(NDT):
                bank = rb[4 + (ht % 2)]
                P.emit("sync", lambda s, ht=ht, tc=tc: s.dma_start(
                    out=pstage[:, 0:512],
                    in_=part_scr[ht * 128:(ht + 1) * 128, tc * 512:(tc + 1) * 512]),
                    reads=[rpart], writes=[rpst[0]], step=16)
                for hf in range(2):
                    q = (ht * 2 + hf) % 2
                    P.emit("gpsimd", lambda g, ht=ht, hf=hf, q=q: g.dma_start(
                        out=sDW2(q).rearrange("p (t e) -> p t e", t=22),
                        in_=dwT[hf * 2816:(hf + 1) * 2816, ht * 128:(ht + 1) * 128]
                        .rearrange("(t p) e -> p t e", p=128)),
                        writes=[rsDW[q]] + ([rswm, rsWo[0], rsWo[1]]
                                            if (tc == 0 and ht == 0) else []),
                        step=16)
                    P.emit("tensor", lambda t, ht=ht, hf=hf, q=q: [
                        t.matmul(pb[4 + (ht % 2)][:],
                                 sDW2(q)[:, i * 128:(i + 1) * 128], hdnT(hf * 22 + i),
                                 start=(hf == 0 and i == 0),
                                 stop=(hf == 1 and i == 21))
                        for i in range(22)][-1],
                        reads=[rsDW[q], rhdnT], writes=[bank])
                P.emit("vector", lambda v, ht=ht: v.scalar_tensor_tensor(
                    out=ostage[:, 0:512], in0=pb[4 + (ht % 2)][:], scalar=dsS[:, ht:ht + 1],
                    in1=pstage[:, 0:512], op0=ALU.mult, op1=ALU.add),
                    reads=[bank, rds, rpst[0]], writes=[rost[0]])
                P.emit("gpsimd", lambda g, ht=ht, tc=tc: g.dma_start(
                    out=outT[ht * 128:(ht + 1) * 128, tc * 512:(tc + 1) * 512],
                    in_=ostage[:, 0:512]),
                    reads=[rost[0]], step=16)

        P.wait_all("gpsimd")
        P.wait_all("sync")

        with nc.Block() as block:
            P.run(block)
    return nc, taps


_COS_SIN = None


def kernel(**inputs):
    global _COS_SIN
    if _COS_SIN is None:
        _COS_SIN = _rope_tables()
    cos, sin = _COS_SIN
    in_maps = [_core_inputs(c, inputs, cos, sin) for c in range(NCORE)]
    nc, _ = build()
    res = run_bass_kernel_spmd(nc, in_maps, core_ids=list(range(NCORE)))
    out = np.empty((N, D), np.float32)
    for c in range(NCORE):
        oT = np.asarray(res.results[c]["outT"])
        for k in range(NSLOT):
            t = 8 * k + c
            out[t * 128:(t + 1) * 128, :] = oT[:, k * 128:(k + 1) * 128].T
    return out[None]


# version 12
# speedup vs baseline: 1.2208x; 1.0659x over previous
import numpy as np
import ml_dtypes
from contextlib import ExitStack

import concourse.bass as bass
import concourse.mybir as mybir
from concourse.bass_utils import run_bass_kernel_spmd

F32 = mybir.dt.float32
BF16 = mybir.dt.bfloat16
AF = mybir.ActivationFunctionType
ALU = mybir.AluOpType
AX = mybir.AxisListType.X
NPBF = ml_dtypes.bfloat16

N, D, H, HD = 8192, 2048, 16, 128
NCORE, NSLOT = 8, 8
BS, NB, SEL, WIN = 64, 128, 8, 64
INTER = 5632
NIT = INTER // 128
NDT = D // 128
CH = 512
TOK = NSLOT * 128

ENGS = ("tensor", "vector", "scalar", "sync", "gpsimd")


def nchunks(k):
    return 2 * (k + 1)


def nswa(k):
    return min(3, nchunks(k))


class _Sem:
    def __init__(self, h):
        self.h = h
        self.n = 0


class _Res:
    __slots__ = ("w", "readers")

    def __init__(self):
        self.w = None
        self.readers = []


class _Op:
    __slots__ = ("fn", "waits", "incs", "tag")

    def __init__(self, fn, waits, incs, tag=None):
        self.fn, self.waits, self.incs, self.tag = fn, waits, incs, tag


class Prog:

    def __init__(self, nc, ctx):
        self.nc = nc
        self.ops = {e: [] for e in ENGS}
        self.sem = {e: _Sem(ctx.enter_context(nc.semaphore(f"s_{e}"))) for e in ENGS}

    def emit(self, eng, fn, reads=(), writes=(), extra_waits=(), step=1, tag=None):
        s = self.sem[eng]
        want = {}

        def add(c):
            if c is None:
                return
            sm, v = c
            k = id(sm)
            if k not in want or want[k][1] < v:
                want[k] = (sm, v)

        for r in reads:
            add(r.w)
        for w in writes:
            add(w.w)
            for c in w.readers:
                add(c)
        for ew in extra_waits:
            add(ew)
        wl = []
        for sm, v in want.values():
            if sm is s and eng == "tensor":
                continue
            wl.append((sm, v))
        if eng in ("vector", "scalar") and s.n > 0:
            wl.append((s, s.n))
        best = {}
        for sm, v in wl:
            if id(sm) not in best or best[id(sm)][1] < v:
                best[id(sm)] = (sm, v)
        s.n += step
        self.ops[eng].append(_Op(fn, list(best.values()), [(s, step)], tag))
        comp = (s, s.n)
        for w in writes:
            w.w = comp
            w.readers = []
        for r in reads:
            r.readers.append(comp)
        return comp

    def wait_all(self, eng):
        s = self.sem[eng]
        self.ops[eng].append(_Op(None, [(s, s.n)], []))

    def run(self, block):
        def mk(name):
            def body(engine):
                for op in self.ops[name]:
                    for sm, v in op.waits:
                        engine.wait_ge(sm.h, v)
                    if op.fn is not None:
                        ins = op.fn(engine)
                        for sm, k in op.incs:
                            ins.then_inc(sm.h, k)
            return body

        block.tensor(mk("tensor"))
        block.vector(mk("vector"))
        block.scalar(mk("scalar"))
        block.sync(mk("sync"))
        block.gpsimd(mk("gpsimd"))


def _rope_tables():
    half = HD // 2
    inv = 1.0 / 10000.0 ** (np.arange(half, dtype=np.float64) / half)
    th = np.arange(N, dtype=np.float64)[:, None] * inv[None, :]
    return np.cos(th).T.astype(np.float32), np.sin(th).T.astype(np.float32)


def _perm_blocks(a):
    v = a.reshape(a.shape[:-1] + (8, 8, 2))
    v = np.swapaxes(v, -3, -2)
    return np.ascontiguousarray(v.reshape(a.shape))


def _core_inputs(c, full, cos, sin):
    x = np.asarray(full["L0_attn_x"], np.float32)[0]
    tiles = [8 * k + c for k in range(NSLOT)]
    x_loc = np.stack([x[t * 128:(t + 1) * 128] for t in tiles])
    xT_loc = np.ascontiguousarray(
        np.concatenate([x[t * 128:(t + 1) * 128] for t in tiles], 0).T)
    scale = np.float32(HD ** -0.5)
    cols = np.concatenate([np.arange(t * 128, (t + 1) * 128) for t in tiles])
    rope = np.concatenate(
        [cos[:, cols] * scale, sin[:, cols] * scale, cos[:, cols], sin[:, cols]],
        axis=1).astype(NPBF)
    rope = np.vstack([rope, rope])

    bmask = np.zeros((NSLOT, 128, 1024), np.float32)
    swam = np.zeros((NSLOT, 128, 1536), np.float32)
    vis = np.zeros((NSLOT, 128, NB), np.float32)
    cur2 = np.zeros((NSLOT, 128, NB), np.float32)
    notany = np.zeros((128, NSLOT), np.float32)
    any01 = np.zeros((128, NSLOT), np.float32)
    r = np.arange(128)
    mm = np.arange(NB)
    for k in range(NSLOT):
        t = 8 * k + c
        T0 = t * 128
        rows = T0 + r
        scan = 1024 * (k + 1)
        j = np.arange(scan - 1024, scan)
        bmask[k] = (j[None, :] <= rows[:, None]) & (j[None, :] < T0 + 128)
        w = 512 * nswa(k)
        j2 = np.arange(scan - w, scan)
        m = (j2[None, :] <= rows[:, None]) & (rows[:, None] - j2[None, :] <= WIN)
        swam[k, :, 1536 - w:] = m
        vis[k] = rows[:, None] >= ((mm + 1) * 64 - 1)[None, :]
        cur2[k, r, rows // 64] = 2.0
        na = ~(vis[k].any(1))
        notany[:, k] = na
        any01[:, k] = ~na
    gs = np.asarray(full["L0_ffn_gate_s"], np.float32)
    us = np.asarray(full["L0_ffn_up_s"], np.float32)
    ds = np.asarray(full["L0_ffn_down_s"], np.float32)
    return {
        "x_loc": x_loc,
        "xT_loc": xT_loc,
        "rope": rope,
        "bmask": bmask.astype(NPBF),
        "swam": swam.astype(NPBF),
        "vis": _perm_blocks(vis).astype(NPBF),
        "cur2": _perm_blocks(cur2).astype(NPBF),
        "notany": notany,
        "any01": any01,
        "Wq": np.asarray(full["L0_attn_Wq"], np.float32),
        "Wk": np.asarray(full["L0_attn_Wk"], np.float32),
        "Wv": np.asarray(full["L0_attn_Wv"], np.float32),
        "Wg": np.asarray(full["L0_attn_Wg"], np.float32),
        "Wo": np.asarray(full["L0_attn_Wo"], np.float32),
        "gwT": np.ascontiguousarray(np.asarray(full["L0_ffn_gate_w"], np.float32).T),
        "uwT": np.ascontiguousarray(np.asarray(full["L0_ffn_up_w"], np.float32).T),
        "dwT": np.ascontiguousarray(np.asarray(full["L0_ffn_down_w"], np.float32).T),
        "gs_pt": np.ascontiguousarray(gs.reshape(NIT, 128).T),
        "us_pt": np.ascontiguousarray(us.reshape(NIT, 128).T),
        "ds_pt": np.ascontiguousarray(ds.reshape(NDT, 128).T),
    }


def build(debug_taps=()):
    nc = bass.Bass("TRN2", target_bir_lowering=False)
    RG = [list(range(NCORE))]

    def din(name, shape, dt=F32):
        return nc.dram_tensor(name, shape, dt, kind="ExternalInput").ap()

    x_loc = din("x_loc", [NSLOT, 128, D])
    xT_loc = din("xT_loc", [D, TOK])
    rope_t = din("rope", [128, 4096], BF16)
    bmask_i = din("bmask", [NSLOT, 128, 1024], BF16)
    swam_i = din("swam", [NSLOT, 128, 1536], BF16)
    vis_i = din("vis", [NSLOT, 128, NB], BF16)
    cur2_i = din("cur2", [NSLOT, 128, NB], BF16)
    notany_i = din("notany", [128, NSLOT])
    any01_i = din("any01", [128, NSLOT])
    Wq = din("Wq", [D, D]); Wk = din("Wk", [D, D]); Wv = din("Wv", [D, D])
    Wg = din("Wg", [D, 3 * H]); Wo = din("Wo", [D, D])
    gwT = din("gwT", [D, INTER]); uwT = din("uwT", [D, INTER])
    dwT = din("dwT", [INTER, D])
    gs_pt = din("gs_pt", [128, NIT]); us_pt = din("us_pt", [128, NIT])
    ds_pt = din("ds_pt", [128, NDT])
    outT = nc.dram_tensor("outT", [D, TOK], F32, kind="ExternalOutput").ap()

    KT_contrib = nc.dram_tensor("KT_contrib", [H * 128, TOK], BF16, kind="Internal").ap()
    V_contrib = nc.dram_tensor("V_contrib", [H * NSLOT * 128, 128], BF16, kind="Internal").ap()
    KC_contrib = nc.dram_tensor("KC_contrib", [16, D], BF16, kind="Internal").ap()
    VC_contrib = nc.dram_tensor("VC_contrib", [16, D], BF16, kind="Internal").ap()
    KT_gath = nc.dram_tensor("KT_gath", [NCORE * H * 128, TOK], BF16, kind="Internal").ap()
    V_gath = nc.dram_tensor("V_gath", [NCORE * H * NSLOT * 128, 128], BF16, kind="Internal").ap()
    KC_gath = nc.dram_tensor("KC_gath", [NCORE * 16, D], BF16, kind="Internal").ap()
    VC_gath = nc.dram_tensor("VC_gath", [NCORE * 16, D], BF16, kind="Internal").ap()
    KT_lin = nc.dram_tensor("KT_lin", [H * 128, N], BF16, kind="Internal").ap()
    V_lin = nc.dram_tensor("V_lin", [H * 128, N], BF16, kind="Internal").ap()
    xnt_spill = nc.dram_tensor("xnt_spill", [128, NDT * TOK], BF16, kind="Internal").ap()
    part_scr = nc.dram_tensor("part_scr", [D, TOK], F32, kind="Internal").ap()

    ident_h = nc.inline_tensor(np.eye(128, dtype=NPBF), "identb").ap()
    taps = {}

    with ExitStack() as ctx:
        en = ctx.enter_context

        def sb(name, shape, dt):
            return en(nc.sbuf_tensor(name, shape, dt))

        A_xnt = sb("A_xnt", [128, NDT * TOK], BF16)
        A_qt = sb("A_qt", [128, H * TOK], BF16)
        A_kv = sb("A_kv", [128, H * TOK], BF16)
        A_w = sb("A_w", [128, 16384], BF16)
        ident = sb("identS", [128, 128], BF16)
        vcmp = sb("vcmpS", [128, H * 128], BF16)
        kcmpT = sb("kcmpTS", [128, H * 128], BF16)
        kc16 = sb("kc16", [128, H * 16], BF16)
        kcstage = sb("kcstage", [16, D], BF16)
        vcstage = sb("vcstage", [2, D], BF16)
        ropes = sb("ropes", [128, 4096], BF16)
        rtmp = sb("rtmp", [128, 4 * CH], BF16)
        xnnat = sb("xnnat", [128, D], BF16)
        ecmp2 = sb("ecmpS", [128, 256], BF16)
        evis2 = sb("evisS", [128, 256], BF16)
        ecT = sb("ecTS", [128, 128], BF16)
        impS2 = sb("impS", [128, 256], BF16)
        mselS2 = sb("mselS", [128, 256], BF16)
        ogS = sb("ogS", [128, 128], BF16)
        visS = sb("visS", [128, NSLOT * NB], BF16)
        cur2S = sb("cur2S", [128, NSLOT * NB], BF16)
        sWg = sb("sWgS", [128, NDT * 48], BF16)
        sg_t = sb("sg_t", [128, 2 * CH], BF16)
        xstage = sb("xstage", [128, D], F32)
        F32A = sb("F32A", [128, D], F32)
        sqt = F32A[:]
        f32scr = sb("f32scr", [128, 1024], F32)
        lnst = sb("lnst", [128, 8], F32)
        gcol = sb("gcolS", [128, NSLOT * 48], F32)
        Zparts = sb("ZpartsS", [128, 64], F32)
        ZswaS = sb("ZswaS", [128, 8], F32)
        Zm = sb("ZmS", [128, 16], F32)
        top82 = sb("top8S", [128, 16], F32)
        gsS = sb("gsS", [128, NIT], F32)
        usS = sb("usS", [128, NIT], F32)
        dsS = sb("dsS", [128, NDT], F32)
        naS = sb("naS", [128, NSLOT], F32)
        anS = sb("anS", [128, NSLOT], F32)
        pstage = F32A[:, 0:2 * CH]
        ostage = F32A[:, 2 * CH:4 * CH]
        pbq = [en(nc.psum_tensor(f"pbq{i}", [128, 1024], F32)) for i in range(2)]
        pbs = [en(nc.psum_tensor(f"pbs{i}", [128, CH], F32)) for i in range(4)]
        pb = [pbq[0][:, 0:512], pbq[0][:, 512:1024],
              pbq[1][:, 0:512], pbq[1][:, 512:1024],
              pbs[0][:], pbs[1][:], pbs[2][:], pbs[3][:]]

        xnT = A_xnt
        KT_loc = A_qt
        V_loc = A_kv
        QT = A_qt
        KTh = A_kv[:, 0:8192]
        Vh = A_kv[:, 8192:16384]
        bmaskS = lambda k: A_xnt[:, k * 1024:(k + 1) * 1024]
        Pring = lambda i: A_xnt[:, 8192 + i * 512: 8192 + (i + 1) * 512]
        PmS = lambda i: A_xnt[:, 11264 + i * 512: 11264 + (i + 1) * 512]
        PswS = lambda i: A_xnt[:, 12288 + i * 512: 12288 + (i + 1) * 512]
        PTsS = lambda i: A_xnt[:, 13312 + i * 512: 13312 + (i + 1) * 512]
        swamS = lambda k: A_w[:, k * 1536:(k + 1) * 1536]
        sWoS = lambda p: A_w[:, 12288 + p * 2048: 12288 + (p + 1) * 2048]
        sGW = lambda p: A_kv[:, 6144 + p * 2048: 6144 + (p + 1) * 2048]
        sUW = lambda p: A_w[:, p * 2048:(p + 1) * 2048]
        sDW2 = lambda q: A_w[:, 8192 + q * 2816: 8192 + (q + 1) * 2816]
        sKW = lambda p: A_w[:, p * 2048:(p + 1) * 2048]
        sVW = lambda q: A_w[:, q * 8192:(q + 1) * 8192]

        def hdnT(it):
            if it < 32:
                return A_qt[:, it * 512:(it + 1) * 512]
            return A_kv[:, (it - 32) * 512:(it - 31) * 512]

        kcnat = A_kv[:, 0:H * 128]

        rb = [_Res() for _ in range(8)]
        rPring = [_Res() for _ in range(6)]
        rPm = [_Res() for _ in range(4)]
        rPsw = [_Res() for _ in range(2)]
        rPTs = [_Res() for _ in range(2)]
        rQT = [[_Res() for _ in range(NSLOT)] for _ in range(H)]
        rKTloc = [_Res() for _ in range(H)]
        rVloc = _Res()
        rKTh, rVh = _Res(), _Res()
        rxnT = _Res()
        rbm, rswm = _Res(), _Res()
        rsWo = [_Res(), _Res()]
        rsGW = [_Res() for _ in range(4)]
        rsUW = [_Res() for _ in range(4)]
        rsDW = [_Res(), _Res()]
        rsKW = [_Res(), _Res()]
        rsVW = [_Res(), _Res()]
        rxstage, rsqt, rlnst, rxnnat = _Res(), _Res(), _Res(), _Res()
        rropes, rident = _Res(), _Res()
        rrtmp = [_Res(), _Res()]
        rvcmp, rkcnat, rkcmpT, rkc16 = _Res(), _Res(), _Res(), _Res()
        rkcstage, rvcstage = _Res(), _Res()
        recmp, revis, recT, rimp, rog = (_Res() for _ in range(5))
        rmsel = [_Res(), _Res()]
        rvisS, rcur2S, rgcol, rsWg, rsgt = (_Res() for _ in range(5))
        rtop8 = _Res()
        rgs, rus, rds, rna, ran = (_Res() for _ in range(5))
        rpst = [_Res(), _Res()]
        rost = [_Res(), _Res()]
        rKTc, rVc, rKCc, rVCc = (_Res() for _ in range(4))
        rKTg, rVg, rKCg, rVCg = (_Res() for _ in range(4))
        rKTl, rVl = _Res(), _Res()
        rspill, rpart = _Res(), _Res()
        rhdnT = _Res()
        QT_all = [r for hh in rQT for r in hh]

        P = Prog(nc, ctx)

        def emit_rope(dst, co, si, wr, par, extra_reads=()):
            rt = rtmp[:, par * CH:(par + 1) * CH]
            rr = rrtmp[par]
            P.emit("vector", lambda v, dst=dst, co=co, rt=rt: v.tensor_mul(
                dst[0:64, :], rt[0:64, :], ropes[0:64, co:co + 512]),
                reads=[rr, rropes] + list(extra_reads), writes=wr)
            P.emit("vector", lambda v, si=si, rt=rt: v.tensor_mul(
                f32scr[0:64, 0:512], rt[64:128, :], ropes[64:128, si:si + 512]),
                reads=[rr, rropes])
            P.emit("vector", lambda v, dst=dst: v.tensor_sub(
                dst[0:64, :], dst[0:64, :], f32scr[0:64, 0:512]),
                writes=wr)
            P.emit("vector", lambda v, dst=dst, si=si, rt=rt: v.tensor_mul(
                dst[64:128, :], rt[0:64, :], ropes[0:64, si:si + 512]),
                reads=[rr, rropes], writes=wr)
            P.emit("vector", lambda v, co=co, rt=rt: v.tensor_mul(
                f32scr[64:128, 0:512], rt[64:128, :], ropes[64:128, co:co + 512]),
                reads=[rr, rropes])
            P.emit("vector", lambda v, dst=dst: v.tensor_add(
                dst[64:128, :], dst[64:128, :], f32scr[64:128, 0:512]),
                writes=wr)

        def tap(name, ap, reads, dt=BF16):
            if name not in debug_taps:
                return
            t = nc.dram_tensor("tap_" + name, list(ap.shape), dt,
                               kind="ExternalOutput").ap()
            taps[name] = True
            P.emit("gpsimd", lambda g, t=t, ap=ap: g.dma_start(out=t, in_=ap),
                   reads=list(reads), step=16)

        P.emit("sync", lambda s: s.dma_start(out=ident[:], in_=ident_h),
               writes=[rident], step=16)
        P.emit("sync", lambda s: s.dma_start(out=ropes[:], in_=rope_t),
               writes=[rropes], step=16)
        P.emit("sync", lambda s: s.dma_start(out=visS[:].rearrange("p (k b) -> p k b", k=8), in_=vis_i.rearrange("k p b -> p k b")),
               writes=[rvisS], step=16)
        P.emit("sync", lambda s: s.dma_start(out=cur2S[:].rearrange("p (k b) -> p k b", k=8), in_=cur2_i.rearrange("k p b -> p k b")),
               writes=[rcur2S], step=16)
        P.emit("sync", lambda s: s.dma_start(out=naS[:], in_=notany_i), writes=[rna], step=16)
        P.emit("sync", lambda s: s.dma_start(out=anS[:], in_=any01_i), writes=[ran], step=16)
        P.emit("sync", lambda s: s.dma_start(out=gsS[:], in_=gs_pt), writes=[rgs], step=16)
        P.emit("sync", lambda s: s.dma_start(out=usS[:], in_=us_pt), writes=[rus], step=16)
        P.emit("sync", lambda s: s.dma_start(out=dsS[:], in_=ds_pt), writes=[rds], step=16)
        P.emit("gpsimd", lambda g: g.dma_start(
            out=sWg[:].rearrange("p (t e) -> p t e", t=16), in_=Wg.rearrange("(t p) e -> p t e", p=128)),
            writes=[rsWg], step=16)

        inv_d = float(1.0 / D)
        for s in range(NSLOT):
            P.emit("sync", lambda sy, s=s: sy.dma_start(out=xstage[:], in_=x_loc[s]),
                   writes=[rxstage], step=16)
            P.emit("vector", lambda v: v.reduce_sum(lnst[:, 0:1], xstage[:], AX),
                   reads=[rxstage], writes=[rlnst])
            P.emit("vector", lambda v: v.tensor_scalar_mul(lnst[:, 1:2], lnst[:, 0:1], -inv_d),
                   writes=[rlnst])
            P.emit("scalar", lambda sc: sc.activation(sqt[:], xstage[:], AF.Square,
                                                      bias=lnst[:, 1:2]),
                   reads=[rxstage, rlnst], writes=[rsqt])
            P.emit("vector", lambda v: v.reduce_sum(lnst[:, 2:3], sqt[:], AX),
                   reads=[rsqt], writes=[rlnst])
            P.emit("vector", lambda v: v.tensor_scalar(
                lnst[:, 2:3], lnst[:, 2:3], inv_d, 1e-5, ALU.mult, ALU.add),
                writes=[rlnst])
            P.emit("scalar", lambda sc: sc.activation(lnst[:, 3:4], lnst[:, 2:3], AF.Sqrt),
                   reads=[rlnst], writes=[rlnst])
            P.emit("vector", lambda v: v.reciprocal(lnst[:, 4:5], lnst[:, 3:4]),
                   reads=[rlnst], writes=[rlnst])
            P.emit("vector", lambda v: v.tensor_mul(lnst[:, 5:6], lnst[:, 1:2], lnst[:, 4:5]),
                   writes=[rlnst])
            P.emit("vector", lambda v: v.tensor_scalar(
                xnnat[:], xstage[:], lnst[:, 4:5], lnst[:, 5:6], ALU.mult, ALU.add),
                reads=[rxstage, rlnst], writes=[rxnnat])
            for g4 in range(4):
                bank = rb[g4 % 2]
                P.emit("tensor", lambda t, g4=g4: [
                    t.matmul(pb[g4 % 2][:, i * 128:(i + 1) * 128],
                             xnnat[:, (g4 * 4 + i) * 128:(g4 * 4 + i + 1) * 128],
                             ident[:], start=(i == 0), stop=(i == 3))
                    for i in range(4)][-1],
                    reads=[rxnnat, rident], writes=[bank])
                P.emit("scalar", lambda sc, s=s, g4=g4: sc.copy(
                    xnT[:].rearrange("p (t c) -> p t c", t=NDT)[
                        :, g4 * 4:(g4 + 1) * 4, s * 128:(s + 1) * 128],
                    pb[g4 % 2][:].rearrange("p (t c) -> p t c", t=4)),
                    reads=[bank], writes=[rxnT])
        tap("xnT", xnT[:, 0:4096], [rxnT])

        for h in range(H):
            P.emit("gpsimd", lambda g, h=h: g.dma_start(
                out=sKW(h % 2).rearrange("p (t e) -> p t e", t=16),
                in_=Wk[:, h * 128:(h + 1) * 128].rearrange("(t p) e -> p t e", p=128)),
                writes=[rsKW[h % 2]], step=16)
            for tc in range(2):
                bank = rb[(h * 2 + tc) % 2]
                P.emit("tensor", lambda t, h=h, tc=tc: [
                    t.matmul(pb[(h * 2 + tc) % 2][:],
                             sKW(h % 2)[:, dt * 128:(dt + 1) * 128],
                             xnT[:, dt * 1024 + tc * 512: dt * 1024 + (tc + 1) * 512],
                             start=(dt == 0), stop=(dt == NDT - 1))
                    for dt in range(NDT)][-1],
                    reads=[rsKW[h % 2], rxnT], writes=[bank])
                par = (h * 2 + tc) % 2
                P.emit("scalar", lambda sc, h=h, tc=tc, par=par: sc.copy(
                    rtmp[:, par * CH:(par + 1) * CH], pb[par][:]),
                    reads=[bank], writes=[rrtmp[par]])
                co, si = 2048 + tc * 512, 3072 + tc * 512
                dst = KT_loc[:, h * 1024 + tc * 512: h * 1024 + (tc + 1) * 512]
                emit_rope(dst, co, si, [rKTloc[h]], par)
            src = KT_loc[:, h * 1024:(h + 1) * 1024]
            P.emit("vector", lambda v, src=src: v.tensor_add(
                f32scr[:, 0:512], src[:, 0:512], src[:, 512:1024]),
                reads=[rKTloc[h]])
            for w in (256, 128, 64, 32, 16):
                P.emit("vector", lambda v, w=w: v.tensor_add(
                    f32scr[:, 0:w], f32scr[:, 0:w], f32scr[:, w:2 * w]))
            P.emit("vector", lambda v, h=h: v.tensor_scalar_mul(
                kc16[:, h * 16:(h + 1) * 16], f32scr[:, 0:16], 1.0 / 64),
                writes=[rkc16])
            P.emit("gpsimd", lambda g, h=h: g.dma_start(
                out=KT_contrib[h * 128:(h + 1) * 128, :],
                in_=KT_loc[:, h * 1024:(h + 1) * 1024]),
                reads=[rKTloc[h]], writes=[rKTc], step=16)
        for h in range(H):
            P.emit("tensor", lambda t, h=h: t.matmul(
                pb[2][0:16, 0:128], kc16[:, h * 16:(h + 1) * 16], ident[:],
                start=True, stop=True),
                reads=[rkc16, rident], writes=[rb[2]])
            P.emit("scalar", lambda sc, h=h: sc.copy(
                kcstage[:, h * 128:(h + 1) * 128], pb[2][0:16, 0:128]),
                reads=[rb[2]], writes=[rkcstage])
        P.emit("gpsimd", lambda g: g.dma_start(out=KC_contrib, in_=kcstage[:]),
               reads=[rkcstage], writes=[rKCc], step=16)

        pool_np = np.zeros((128, 2), dtype=NPBF)
        pool_np[:64, 0] = NPBF(1.0 / 64)
        pool_np[64:, 1] = NPBF(1.0 / 64)
        pool_h = nc.inline_tensor(pool_np, "pool64").ap()
        pool64 = sb("pool64S", [128, 2], BF16)
        rpool = _Res()
        P.emit("sync", lambda s: s.dma_start(out=pool64[:], in_=pool_h),
               writes=[rpool], step=16)
        for hc in range(4):
            q = hc % 2
            P.emit("gpsimd", lambda g, hc=hc, q=q: g.dma_start(
                out=sVW(q).rearrange("p (t e) -> p t e", t=16),
                in_=Wv[:, hc * 512:(hc + 1) * 512].rearrange("(t p) e -> p t e", p=128)),
                writes=[rsVW[q]] + ([rsKW[0], rsKW[1]] if hc < 2 else []), step=16)
            for s in range(NSLOT):
                bank = rb[(s + hc) % 2]
                P.emit("tensor", lambda t, s=s, hc=hc, q=q: [
                    t.matmul(pb[(s + hc) % 2][:],
                             xnT[:, dt * 1024 + s * 128: dt * 1024 + (s + 1) * 128],
                             sVW(q)[:, dt * 512:(dt + 1) * 512],
                             start=(dt == 0), stop=(dt == NDT - 1))
                    for dt in range(NDT)][-1],
                    reads=[rsVW[q], rxnT], writes=[bank])
                P.emit("scalar", lambda sc, s=s, hc=hc: sc.copy(
                    V_loc[:, s * 2048 + hc * 512: s * 2048 + (hc + 1) * 512],
                    pb[(s + hc) % 2][:]),
                    reads=[bank], writes=[rVloc])
                P.emit("tensor", lambda t, s=s, hc=hc: t.matmul(
                    pb[3][0:2, :], pool64[:],
                    V_loc[:, s * 2048 + hc * 512: s * 2048 + (hc + 1) * 512],
                    start=True, stop=True),
                    reads=[rpool, rVloc], writes=[rb[3]])
                P.emit("scalar", lambda sc: sc.copy(vcstage[:, 0:512], pb[3][0:2, :]),
                       reads=[rb[3]], writes=[rvcstage])
                P.emit("gpsimd", lambda g, s=s, hc=hc: g.dma_start(
                    out=VC_contrib[s * 2:(s + 1) * 2, hc * 512:(hc + 1) * 512],
                    in_=vcstage[0:2, 0:512]),
                    reads=[rvcstage], writes=[rVCc], step=16)
        for h in range(H):
            P.emit("gpsimd", lambda g, h=h: g.dma_start(
                out=V_contrib[h * 1024:(h + 1) * 1024, :].rearrange(
                    "(s p) d -> p s d", p=128),
                in_=V_loc[:].rearrange("p (s h c) -> p s h c", s=8, h=H)[:, :, h, :]),
                reads=[rVloc], writes=[rVc], step=16)

        for h in range(H):
            P.emit("gpsimd", lambda g, h=h: g.dma_start(
                out=sKW(h % 2).rearrange("p (t e) -> p t e", t=16),
                in_=Wq[:, h * 128:(h + 1) * 128].rearrange("(t p) e -> p t e", p=128)),
                writes=[rsKW[h % 2]] + ([rsVW[0]] if h < 2 else []), step=16)
            for tc in range(2):
                bank = rb[(h * 2 + tc) % 2]
                P.emit("tensor", lambda t, h=h, tc=tc: [
                    t.matmul(pb[(h * 2 + tc) % 2][:],
                             sKW(h % 2)[:, dt * 128:(dt + 1) * 128],
                             xnT[:, dt * 1024 + tc * 512: dt * 1024 + (tc + 1) * 512],
                             start=(dt == 0), stop=(dt == NDT - 1))
                    for dt in range(NDT)][-1],
                    reads=[rsKW[h % 2], rxnT], writes=[bank])
                par = (h * 2 + tc) % 2
                P.emit("scalar", lambda sc, h=h, tc=tc, par=par: sc.copy(
                    rtmp[:, par * CH:(par + 1) * CH], pb[par][:]),
                    reads=[bank], writes=[rrtmp[par]])
                co, si = tc * 512, 1024 + tc * 512
                dst = QT[:, h * 1024 + tc * 512: h * 1024 + (tc + 1) * 512]
                wr = [rQT[h][4 * tc + i] for i in range(4)]
                emit_rope(dst, co, si, wr, par, extra_reads=[rKTc])
        for s in range(NSLOT):
            P.emit("tensor", lambda t, s=s: [
                t.matmul(pb[6][:, 0:48],
                         xnT[:, dt * 1024 + s * 128: dt * 1024 + (s + 1) * 128],
                         sWg[:, dt * 48:(dt + 1) * 48],
                         start=(dt == 0), stop=(dt == NDT - 1))
                for dt in range(NDT)][-1],
                reads=[rsWg, rxnT], writes=[rb[6]])
            P.emit("scalar", lambda sc, s=s: sc.activation(
                gcol[:, s * 48:(s + 1) * 48], pb[6][:, 0:48], AF.Sigmoid),
                reads=[rb[6]], writes=[rgcol])
        tap("QT", QT[:, 0:4096], [rQT[0][k] for k in range(8)] + [rQT[1][k] for k in range(8)])

        for cin, cout, rin, rout in (
            (KC_contrib, KC_gath, rKCc, rKCg),
            (VC_contrib, VC_gath, rVCc, rVCg),
            (KT_contrib, KT_gath, rKTc, rKTg),
            (V_contrib, V_gath, rVc, rVg),
        ):
            P.emit("gpsimd", lambda g, cin=cin, cout=cout: g.collective_compute(
                "AllGather", ALU.bypass, replica_groups=RG,
                ins=[cin.opt()], outs=[cout.opt()]),
                reads=[rin], writes=[rout], step=1)

        for h in range(H):
            for r in range(NCORE):
                qeng = "sync" if (r % 2 == 0) else "gpsimd"
                P.emit(qeng, lambda q, h=h, r=r: q.dma_start(
                    out=KT_lin[h * 128:(h + 1) * 128, :]
                        .rearrange("p (k r c) -> p k r c", r=8, c=128)[:, :, r, :],
                    in_=KT_gath[r * 2048 + h * 128: r * 2048 + (h + 1) * 128, :]
                        .rearrange("p (k c) -> p k c", k=8)),
                    reads=[rKTg], writes=[rKTl], step=16)
                P.emit(qeng, lambda q, h=h, r=r: q.dma_start(
                    out=V_lin[h * 128:(h + 1) * 128, :]
                        .rearrange("p (k r c) -> p k r c", r=8, c=128)[:, :, r, :],
                    in_=V_gath[r * 16384 + h * 1024: r * 16384 + (h + 1) * 1024, :]
                        .rearrange("(s p) d -> p s d", p=128)),
                    reads=[rVg], writes=[rVl], step=16)

        P.emit("gpsimd", lambda g: g.dma_start(out=xnt_spill, in_=xnT[:]),
               reads=[rxnT], writes=[rspill], step=16)
        P.emit("sync", lambda s: s.dma_start(
            out=A_xnt[:, 0:8192].rearrange("p (k b) -> p k b", k=8), in_=bmask_i.rearrange("k p b -> p k b")),
            writes=[rbm, rxnT], step=16)
        P.emit("sync", lambda s: s.dma_start(
            out=A_w[:, 0:12288].rearrange("p (k b) -> p k b", k=8), in_=swam_i.rearrange("k p b -> p k b")),
            writes=[rswm, rsKW[0], rsKW[1], rsVW[0], rsVW[1]], step=16)
        P.emit("sync", lambda s: s.dma_start(out=vcmp[:], in_=VC_gath),
               reads=[rVCg], writes=[rvcmp], step=16)
        P.emit("sync", lambda s: s.dma_start(out=kcnat[:], in_=KC_gath),
               reads=[rKCg], writes=[rkcnat], step=16)
        for h in range(H):
            P.emit("tensor", lambda t, h=h: t.matmul(
                pb[7][:, 0:128], kcnat[:, h * 128:(h + 1) * 128], ident[:],
                start=True, stop=True),
                reads=[rkcnat, rident], writes=[rb[7]])
            P.emit("scalar", lambda sc, h=h: sc.copy(
                kcmpT[:, h * 128:(h + 1) * 128], pb[7][:, 0:128]),
                reads=[rb[7]], writes=[rkcmpT])

        ecmp = lambda k: ecmp2[:, (k % 2) * 128:(k % 2 + 1) * 128]
        evis = lambda k: evis2[:, (k % 2) * 128:(k % 2 + 1) * 128]
        impS = lambda k: impS2[:, (k % 2) * 128:(k % 2 + 1) * 128]
        mselS = lambda k: mselS2[:, (k % 2) * 128:(k % 2 + 1) * 128]
        top8 = lambda k: top82[:, (k % 2) * 8:(k % 2) * 8 + 8]
        PringQ = lambda i: A_xnt[:, 8192 + i * 1024: 8192 + (i + 1) * 1024]
        PmQ = lambda i: A_xnt[:, 11264 + i * 1024: 11264 + (i + 1) * 1024]
        PswR = lambda i: A_xnt[:, 14336 + i * 512: 14336 + (i + 1) * 512]
        PTsQ = lambda i: rtmp[:, i * 1024:(i + 1) * 1024]
        PTsw = lambda i: sg_t[:, i * 512:(i + 1) * 512]
        rPring3 = [_Res() for _ in range(3)]
        rPm3 = [_Res() for _ in range(3)]
        rPsw3 = [_Res() for _ in range(3)]
        rPTq = [_Res(), _Res()]
        rPTw = [_Res(), _Res()]
        pte_first = [True]

        _swc = [0]
        for _k in range(NSLOT):
            _swc.append(_swc[-1] + nswa(_k))

        def swring(k, jsw):
            return (_swc[k] + jsw) % 3

        def mselexpQ(k, j, jj):
            mv = mselS(k).rearrange("p (r sj) -> p r sj", r=8)[:, :, 2 * j + jj]
            return mv.to_broadcast([128, 8, 64])

        def pviewQ(ap, jj):
            return ap.rearrange("p (r j w) -> p r j w", r=8, j=2)[:, :, jj, :]

        for h in range(H):
            kvw = [rVloc, rkcnat] if h == 0 else []
            P.emit("sync", lambda s, h=h: s.dma_start(
                out=KTh, in_=KT_lin[h * 128:(h + 1) * 128, :]),
                reads=[rKTl], writes=[rKTh] + kvw, step=16)
            P.emit("sync", lambda s, h=h: s.dma_start(
                out=Vh, in_=V_lin[h * 128:(h + 1) * 128, :]),
                reads=[rVl], writes=[rVh], step=16)

            def emit_sel(h, k):
                zm = (k % 2) * 8
                P.emit("tensor", lambda t, h=h, k=k: t.matmul(
                    pb[6][:, 0:128], QT[:, h * 1024 + k * 128: h * 1024 + (k + 1) * 128],
                    kcmpT[:, h * 128:(h + 1) * 128], start=True, stop=True),
                    reads=[rQT[h][k], rkcmpT], writes=[rb[6]])
                P.emit("scalar", lambda sc, k=k: sc.activation(
                    ecmp(k), pb[6][:, 0:128], AF.Exp),
                    reads=[rb[6]], writes=[recmp])
                P.emit("vector", lambda v, k=k, zm=zm: v.scalar_tensor_tensor(
                    out=evis(k), in0=ecmp(k), scalar=1.0,
                    in1=visS[:, k * NB:(k + 1) * NB], op0=ALU.mult, op1=ALU.mult,
                    accum_out=Zm[:, zm:zm + 1]),
                    reads=[recmp, rvisS], writes=[revis])
                P.emit("vector", lambda v, k=k, zm=zm: v.scalar_tensor_tensor(
                    out=impS(k), in0=cur2S[:, k * NB:(k + 1) * NB],
                    scalar=Zm[:, zm:zm + 1],
                    in1=evis(k), op0=ALU.mult, op1=ALU.add),
                    reads=[rcur2S, revis], writes=[rimp])
                P.emit("vector", lambda v, k=k: v.max(out=top8(k), in_=impS(k)),
                       reads=[rimp], writes=[rtop8])
                P.emit("vector", lambda v, k=k: v.tensor_single_scalar(
                    mselS(k), impS(k), top8(k)[:, 7:8], ALU.is_ge),
                    reads=[rtop8, rimp], writes=[rmsel[k % 2]])

            def emit_S(h, k, j, rix):
                ra, rc = rb[2 * (j % 2)], rb[2 * (j % 2) + 1]
                P.emit("tensor", lambda t, h=h, k=k, j=j: [
                    t.matmul(pbq[j % 2][:, half * 512:(half + 1) * 512],
                             QT[:, h * 1024 + k * 128: h * 1024 + (k + 1) * 128],
                             KTh[:, j * 1024 + half * 512: j * 1024 + (half + 1) * 512],
                             start=True, stop=True)
                    for half in range(2)][-1],
                    reads=[rQT[h][k], rKTh], writes=[ra, rc])
                P.emit("scalar", lambda sc, j=j, rix=rix: sc.activation(
                    PringQ(rix), pbq[j % 2][:], AF.Exp),
                    reads=[ra, rc], writes=[rPring3[rix]])

            def emit_mask(h, k, j, rix):
                nch = k + 1
                zb = (k % 2) * 32
                zsb = (k % 2) * 4
                if j < nch - 1:
                    for jj in range(2):
                        P.emit("vector", lambda v, k=k, j=j, jj=jj, zb=zb:
                               v.scalar_tensor_tensor(
                                   out=pviewQ(PmQ(rix), jj),
                                   in0=pviewQ(PringQ(rix), jj),
                                   scalar=1.0, in1=mselexpQ(k, j, jj),
                                   op0=ALU.mult, op1=ALU.mult,
                                   accum_out=Zparts[:, zb + 2 * j + jj: zb + 2 * j + jj + 1]),
                               reads=[rPring3[rix], rmsel[k % 2]], writes=[rPm3[rix]])
                else:
                    for jj in range(2):
                        P.emit("vector", lambda v, k=k, j=j, jj=jj, rix=rix: v.tensor_tensor(
                            out=pviewQ(PmQ(rix), jj),
                            in0=pviewQ(PringQ(rix), jj),
                            in1=mselexpQ(k, j, jj), op=ALU.mult),
                            reads=[rPring3[rix], rmsel[k % 2]], writes=[rPm3[rix]])
                    P.emit("vector", lambda v, j=j, zb=zb: v.memset(
                        Zparts[:, zb + 2 * j + 1: zb + 2 * j + 2], 0.0))
                    P.emit("vector", lambda v, j=j, k=k, zb=zb, rix=rix: v.scalar_tensor_tensor(
                        out=PmQ(rix), in0=PmQ(rix), scalar=1.0,
                        in1=bmaskS(k),
                        op0=ALU.mult, op1=ALU.mult,
                        accum_out=Zparts[:, zb + 2 * j: zb + 2 * j + 1]),
                        reads=[rbm], writes=[rPm3[rix]])
                nch512 = 2 * (k + 1)
                nsw = nswa(k)
                for half in range(2):
                    oj = 2 * j + half
                    nback = nch512 - oj
                    if nback <= nsw:
                        jsw = oj - (nch512 - nsw)
                        P.emit("vector", lambda v, j=j, k=k, half=half, nback=nback,
                               jsw=jsw, zsb=zsb, rix=rix: v.scalar_tensor_tensor(
                                   out=PswR(swring(k, jsw)),
                                   in0=PringQ(rix)[:, half * 512:(half + 1) * 512],
                                   scalar=1.0,
                                   in1=swamS(k)[:, 1536 - 512 * nback: 2048 - 512 * nback],
                                   op0=ALU.mult, op1=ALU.mult,
                                   accum_out=ZswaS[:, zsb + jsw: zsb + jsw + 1]),
                               reads=[rPring3[rix], rswm], writes=[rPsw3[swring(k, jsw)]])

            def emit_TAV(h, k, j, rix, obank, rob):
                ra, rc = rb[2 * (j % 2)], rb[2 * (j % 2) + 1]
                P.emit("tensor", lambda t, j=j, rix=rix: [
                    t.matmul(pbq[j % 2][:, i * 128:(i + 1) * 128],
                             PmQ(rix)[:, i * 128:(i + 1) * 128], ident[:],
                             start=(i % 4 == 0), stop=(i % 4 == 3))
                    for i in range(8)][-1],
                    reads=[rPm3[rix], rident], writes=[ra, rc])
                pw = [rPTq[j % 2]]
                if pte_first[0]:
                    pw += rrtmp
                    pte_first[0] = False
                P.emit("scalar", lambda sc, j=j: sc.copy(
                    PTsQ(j % 2)[:, 0:512], pbq[j % 2][:, 0:512]),
                    reads=[ra], writes=pw)
                P.emit("vector", lambda v, j=j: v.tensor_copy(
                    PTsQ(j % 2)[:, 512:1024], pbq[j % 2][:, 512:1024]),
                    reads=[rc], writes=[rPTq[j % 2]])
                P.emit("tensor", lambda t, j=j, obank=obank: [
                    t.matmul(obank[:, 0:128],
                             PTsQ(j % 2)[:, i * 128:(i + 1) * 128],
                             Vh[:, (j * 8 + i) * 128:(j * 8 + i + 1) * 128],
                             start=False, stop=False)
                    for i in range(8)][-1],
                    reads=[rPTq[j % 2], rVh], writes=[rob])
                nch512 = 2 * (k + 1)
                nsw = nswa(k)
                for half in range(2):
                    oj = 2 * j + half
                    nback = nch512 - oj
                    if nback <= nsw:
                        jsw = oj - (nch512 - nsw)
                        P.emit("tensor", lambda t, j=j, k=k, jsw=jsw: [
                            t.matmul(pbq[j % 2][:, i * 128:(i + 1) * 128],
                                     PswR(swring(k, jsw))[:, i * 128:(i + 1) * 128], ident[:],
                                     start=(i == 0), stop=(i == 3))
                            for i in range(4)][-1],
                            reads=[rPsw3[swring(k, jsw)], rident], writes=[ra])
                        if jsw % 2:
                            P.emit("scalar", lambda sc, j=j, jsw=jsw: sc.copy(
                                PTsw(jsw % 2), pbq[j % 2][:, 0:512]),
                                reads=[ra], writes=[rPTw[jsw % 2]])
                        else:
                            P.emit("vector", lambda v, j=j, jsw=jsw: v.tensor_copy(
                                PTsw(jsw % 2), pbq[j % 2][:, 0:512]),
                                reads=[ra], writes=[rPTw[jsw % 2]])
                        P.emit("tensor", lambda t, oj=oj, jsw=jsw, obank=obank: [
                            t.matmul(obank[:, 128:256],
                                     PTsw(jsw % 2)[:, i * 128:(i + 1) * 128],
                                     Vh[:, (oj * 4 + i) * 128:(oj * 4 + i + 1) * 128],
                                     start=False, stop=False)
                            for i in range(4)][-1],
                            reads=[rPTw[jsw % 2], rVh], writes=[rob])

            def emit_combine(h, k, obank, rob):
                nch = k + 1
                zm = (k % 2) * 8
                zb = (k % 2) * 32
                zsb = (k % 2) * 4
                P.emit("vector", lambda v, nch=nch, zm=zm, zb=zb: v.reduce_sum(
                    Zm[:, zm + 1:zm + 2], Zparts[:, zb:zb + 2 * nch], AX))
                P.emit("vector", lambda v, k=k, zm=zm, zsb=zsb: v.reduce_sum(
                    Zm[:, zm + 2:zm + 3], ZswaS[:, zsb:zsb + nswa(k)], AX))
                P.emit("vector", lambda v, k=k, zm=zm: v.tensor_add(
                    Zm[:, zm + 3:zm + 4], Zm[:, zm:zm + 1], naS[:, k:k + 1]), reads=[rna])
                P.emit("vector", lambda v, zm=zm: v.reciprocal(
                    Zm[:, zm + 4:zm + 7], Zm[:, zm + 1:zm + 4]))
                P.emit("vector", lambda v, h=h, k=k, zm=zm: v.tensor_mul(
                    Zm[:, zm + 1:zm + 2], Zm[:, zm + 4:zm + 5],
                    gcol[:, k * 48 + 16 + h: k * 48 + 17 + h]),
                    reads=[rgcol])
                P.emit("vector", lambda v, h=h, k=k, zm=zm: v.tensor_mul(
                    Zm[:, zm + 2:zm + 3], Zm[:, zm + 5:zm + 6],
                    gcol[:, k * 48 + 32 + h: k * 48 + 33 + h]),
                    reads=[rgcol])
                P.emit("vector", lambda v, h=h, k=k, zm=zm: v.tensor_mul(
                    Zm[:, zm + 3:zm + 4], Zm[:, zm + 6:zm + 7],
                    gcol[:, k * 48 + h: k * 48 + 1 + h]),
                    reads=[rgcol])
                P.emit("vector", lambda v, k=k, zm=zm: v.tensor_mul(
                    Zm[:, zm + 3:zm + 4], Zm[:, zm + 3:zm + 4], anS[:, k:k + 1]),
                    reads=[ran])
                P.emit("vector", lambda v, obank=obank, zm=zm: v.tensor_scalar(
                    f32scr[:, 0:128], obank[:, 0:128], Zm[:, zm + 1:zm + 2], None,
                    ALU.mult),
                    reads=[rob])
                P.emit("vector", lambda v, obank=obank, zm=zm: v.scalar_tensor_tensor(
                    out=f32scr[:, 0:128], in0=obank[:, 128:256],
                    scalar=Zm[:, zm + 2:zm + 3],
                    in1=f32scr[:, 0:128], op0=ALU.mult, op1=ALU.add),
                    reads=[rob])
                P.emit("vector", lambda v, obank=obank, zm=zm: v.scalar_tensor_tensor(
                    out=ogS[:], in0=obank[:, 256:384], scalar=Zm[:, zm + 3:zm + 4],
                    in1=f32scr[:, 0:128], op0=ALU.mult, op1=ALU.add),
                    reads=[rob], writes=[rog])
                P.emit("tensor", lambda t: t.matmul(
                    pb[7][:, 0:128], ogS[:], ident[:], start=True, stop=True),
                    reads=[rog, rident], writes=[rb[7]])
                P.emit("scalar", lambda sc, h=h, k=k: sc.copy(
                    QT[:, h * 1024 + k * 128: h * 1024 + (k + 1) * 128], pb[7][:, 0:128]),
                    reads=[rb[7]], writes=[rQT[h][k]])

            emit_sel(h, 0)
            pend = None
            gctr = [0]
            for k in range(NSLOT):
                nch = k + 1
                obank = pb[4 + (k % 2)]
                rob = rb[4 + (k % 2)]
                rix0 = gctr[0] % 3
                gctr[0] += 1
                emit_S(h, k, 0, rix0)
                emit_mask(h, k, 0, rix0)
                if pend is not None:
                    pend()
                    pend = None
                P.emit("tensor", lambda t, k=k: t.matmul(
                    pb[7][:, 0:128], evis(k), ident[:], start=True, stop=True),
                    reads=[revis, rident], writes=[rb[7]])
                P.emit("scalar", lambda sc: sc.copy(ecT[:], pb[7][:, 0:128]),
                       reads=[rb[7]], writes=[recT])
                P.emit("tensor", lambda t, h=h, obank=obank: t.matmul(
                    obank[:, 256:384], ecT[:], vcmp[:, h * 128:(h + 1) * 128],
                    start=True, stop=True),
                    reads=[recT, rvcmp], writes=[rob])
                rixs = [rix0]
                for j in range(1, nch):
                    rix = gctr[0] % 3
                    gctr[0] += 1
                    rixs.append(rix)
                    emit_S(h, k, j, rix)
                    emit_mask(h, k, j, rix)
                    if j == 2 and k < NSLOT - 1:
                        emit_sel(h, k + 1)
                    if j >= 2:
                        emit_TAV(h, k, j - 2, rixs[j - 2], obank, rob)
                if nch <= 2 and k < NSLOT - 1:
                    emit_sel(h, k + 1)
                if nch >= 2:
                    emit_TAV(h, k, nch - 2, rixs[nch - 2], obank, rob)

                def mk_pend(h=h, k=k, nch=nch, obank=obank, rob=rob, rr=None):
                    rr = rixs[nch - 1]
                    def f():
                        emit_TAV(h, k, nch - 1, rr, obank, rob)
                        emit_combine(h, k, obank, rob)
                    return f
                pend = mk_pend()
            pend()

        for ht in range(NDT):
            p = ht % 2
            P.emit("gpsimd", lambda g, ht=ht, p=p: g.dma_start(
                out=sWoS(p).rearrange("p (t e) -> p t e", t=16),
                in_=Wo[:, ht * 128:(ht + 1) * 128].rearrange("(t p) e -> p t e", p=128)),
                writes=[rsWo[p]] + ([rsVW[1]] if ht < 2 else []), step=16)
            for tc in range(2):
                bank = rb[tc]
                P.emit("sync", lambda s, ht=ht, tc=tc: s.dma_start(
                    out=pstage[:, tc * 512:(tc + 1) * 512],
                    in_=xT_loc[ht * 128:(ht + 1) * 128, tc * 512:(tc + 1) * 512]),
                    writes=[rpst[tc]] + ([rsqt] if ht == 0 else []), step=16)
                P.emit("tensor", lambda t, ht=ht, tc=tc, p=p: [
                    t.matmul(pb[tc][:],
                             sWoS(p)[:, hh * 128:(hh + 1) * 128],
                             QT[:, hh * 1024 + tc * 512: hh * 1024 + (tc + 1) * 512],
                             start=(hh == 0), stop=(hh == H - 1))
                    for hh in range(H)][-1],
                    reads=[rsWo[p]] + [rQT[hh][kk] for hh in range(H)
                                       for kk in range(4 * tc, 4 * tc + 4)],
                    writes=[bank])
                P.emit("vector", lambda v, tc=tc: v.scalar_tensor_tensor(
                    out=ostage[:, tc * 512:(tc + 1) * 512], in0=pb[tc][:], scalar=1.0,
                    in1=pstage[:, tc * 512:(tc + 1) * 512], op0=ALU.mult, op1=ALU.add),
                    reads=[bank, rpst[tc]], writes=[rost[tc]])
                P.emit("gpsimd", lambda g, ht=ht, tc=tc: g.dma_start(
                    out=part_scr[ht * 128:(ht + 1) * 128, tc * 512:(tc + 1) * 512],
                    in_=ostage[:, tc * 512:(tc + 1) * 512]),
                    reads=[rost[tc]], writes=[rpart], step=16)

        tap("part", part_scr, [rpart], dt=F32)
        P.emit("sync", lambda s: s.dma_start(out=xnT[:], in_=xnt_spill),
               reads=[rspill], writes=[rxnT, rbm] + rPring + rPm + rPsw + rPTs,
               extra_waits=(), step=16)
        tap("xnT2", xnT[:], [rxnT])
        first_hdn = True
        rsg = [_Res(), _Res()]

        def load_gu(g):
            it = g % NIT
            p = g % 4
            extra_g = [rKTh, rVh] if g < 4 else []
            extra_u = [rswm] if g < 4 else []
            P.emit("gpsimd", lambda gq, it=it, p=p: gq.dma_start(
                out=sGW(p).rearrange("p (t e) -> p t e", t=16),
                in_=gwT[:, it * 128:(it + 1) * 128].rearrange("(t p) e -> p t e", p=128)),
                writes=[rsGW[p]] + extra_g, step=16)
            P.emit("gpsimd", lambda gq, it=it, p=p: gq.dma_start(
                out=sUW(p).rearrange("p (t e) -> p t e", t=16),
                in_=uwT[:, it * 128:(it + 1) * 128].rearrange("(t p) e -> p t e", p=128)),
                writes=[rsUW[p]] + extra_u, step=16)

        for g in range(3):
            load_gu(g)
        for tc in range(2):
            for it in range(NIT):
                g = tc * NIT + it
                if g + 3 < 2 * NIT:
                    load_gu(g + 3)
                p = g % 4
                gb = (g % 2) * 2
                ub = (g % 2) * 2 + 1
                P.emit("tensor", lambda t, it=it, tc=tc, p=p, gb=gb: [
                    t.matmul(pb[gb][:],
                             sGW(p)[:, dt * 128:(dt + 1) * 128],
                             xnT[:, dt * 1024 + tc * 512: dt * 1024 + (tc + 1) * 512],
                             start=(dt == 0), stop=(dt == NDT - 1))
                    for dt in range(NDT)][-1],
                    reads=[rsGW[p], rxnT], writes=[rb[gb]])
                P.emit("tensor", lambda t, it=it, tc=tc, p=p, ub=ub: [
                    t.matmul(pb[ub][:],
                             sUW(p)[:, dt * 128:(dt + 1) * 128],
                             xnT[:, dt * 1024 + tc * 512: dt * 1024 + (tc + 1) * 512],
                             start=(dt == 0), stop=(dt == NDT - 1))
                    for dt in range(NDT)][-1],
                    reads=[rsUW[p], rxnT], writes=[rb[ub]])
                P.emit("scalar", lambda sc, it=it, g=g, gb=gb: sc.activation(
                    sg_t[:, (g % 2) * CH:(g % 2 + 1) * CH], pb[gb][:],
                    AF.Silu, scale=gsS[:, it:it + 1]),
                    reads=[rb[gb], rgs],
                    writes=[rsg[g % 2]] + ([rPTw[0], rPTw[1]] if g < 2 else []))
                hw = [rhdnT]
                if first_hdn:
                    hw = [rhdnT, rKTh, rVh] + QT_all
                    first_hdn = False
                P.emit("vector", lambda v, it=it, g=g, ub=ub: v.scalar_tensor_tensor(
                    out=hdnT(it), in0=pb[ub][:], scalar=usS[:, it:it + 1],
                    in1=sg_t[:, (g % 2) * CH:(g % 2 + 1) * CH],
                    op0=ALU.mult, op1=ALU.mult),
                    reads=[rb[ub], rus, rsg[g % 2]], writes=hw)
            if tc == 0:
                tap("hdn0", A_qt[:, 0:16384], [rhdnT])
            for ht in range(NDT):
                bank = rb[4 + (ht % 2)]
                P.emit("sync", lambda s, ht=ht, tc=tc: s.dma_start(
                    out=pstage[:, 0:512],
                    in_=part_scr[ht * 128:(ht + 1) * 128, tc * 512:(tc + 1) * 512]),
                    reads=[rpart], writes=[rpst[0]], step=16)
                for hf in range(2):
                    q = (ht * 2 + hf) % 2
                    P.emit("gpsimd", lambda g, ht=ht, hf=hf, q=q: g.dma_start(
                        out=sDW2(q).rearrange("p (t e) -> p t e", t=22),
                        in_=dwT[hf * 2816:(hf + 1) * 2816, ht * 128:(ht + 1) * 128]
                        .rearrange("(t p) e -> p t e", p=128)),
                        writes=[rsDW[q]] + ([rswm, rsWo[0], rsWo[1]]
                                            if (tc == 0 and ht == 0) else []),
                        step=16)
                    P.emit("tensor", lambda t, ht=ht, hf=hf, q=q: [
                        t.matmul(pb[4 + (ht % 2)][:],
                                 sDW2(q)[:, i * 128:(i + 1) * 128], hdnT(hf * 22 + i),
                                 start=(hf == 0 and i == 0),
                                 stop=(hf == 1 and i == 21))
                        for i in range(22)][-1],
                        reads=[rsDW[q], rhdnT], writes=[bank])
                P.emit("vector", lambda v, ht=ht: v.scalar_tensor_tensor(
                    out=ostage[:, 0:512], in0=pb[4 + (ht % 2)][:], scalar=dsS[:, ht:ht + 1],
                    in1=pstage[:, 0:512], op0=ALU.mult, op1=ALU.add),
                    reads=[bank, rds, rpst[0]], writes=[rost[0]])
                P.emit("gpsimd", lambda g, ht=ht, tc=tc: g.dma_start(
                    out=outT[ht * 128:(ht + 1) * 128, tc * 512:(tc + 1) * 512],
                    in_=ostage[:, 0:512]),
                    reads=[rost[0]], step=16)

        P.wait_all("gpsimd")
        P.wait_all("sync")

        with nc.Block() as block:
            P.run(block)
    return nc, taps


_COS_SIN = None


def kernel(**inputs):
    global _COS_SIN
    if _COS_SIN is None:
        _COS_SIN = _rope_tables()
    cos, sin = _COS_SIN
    in_maps = [_core_inputs(c, inputs, cos, sin) for c in range(NCORE)]
    nc, _ = build()
    res = run_bass_kernel_spmd(nc, in_maps, core_ids=list(range(NCORE)))
    out = np.empty((N, D), np.float32)
    for c in range(NCORE):
        oT = np.asarray(res.results[c]["outT"])
        for k in range(NSLOT):
            t = 8 * k + c
            out[t * 128:(t + 1) * 128, :] = oT[:, k * 128:(k + 1) * 128].T
    return out[None]
